# version 05
# baseline (speedup 1.0000x reference)
import contextlib
import numpy as np
import ml_dtypes
import concourse.bass as bass
import concourse.mybir as mybir
from concourse.bass_primitives import MemorySpace
from concourse.bass_utils import run_bass_kernel_spmd

F32 = mybir.dt.float32
BF16 = mybir.dt.bfloat16
I32 = mybir.dt.int32
U8 = mybir.dt.uint8
AF = mybir.ActivationFunctionType
ALU = mybir.AluOpType
AX = mybir.AxisListType
BFNP = ml_dtypes.bfloat16

T, D, HEADS, DH, M, DFF, NE = 16384, 1024, 16, 64, 256, 4096, 8
BT = 2048
HB = 1024
NSLOT = 4
CAP = 2560
EPS_K = 1e-4
LN_EPS = 1e-5
NORM = float(DH) ** -0.25
RATIO = float(M) ** -0.5
LNR = float(np.log(RATIO))
BIG = 1.0e4
KB = 1024
NCORES = 8
ENGINES = ("sync", "gpsimd", "tensor", "vector", "scalar")


class _Dummy:
    def then_inc(self, *a, **k):
        return self


_DUMMY = _Dummy()


class EngProxy:

    def __init__(self, P, role):
        self._P = P
        self._role = role

    @property
    def live(self):
        return self._P.active == self._role

    def __getattr__(self, name):
        P = self._P
        if P.active == self._role:
            return getattr(P.real, name)

        def _skip(*a, **k):
            return _DUMMY

        return _skip


class Prog:

    def __init__(self, active, real, sems):
        self.active = active
        self.real = real
        self.sems = sems
        self.cnt = {}
        for r in ENGINES:
            setattr(self, r, EngProxy(self, r))

    def inc(self, instr, sem, n):
        if instr is not _DUMMY:
            instr.then_inc(sem, n)
        self.cnt[sem.num] = self.cnt.get(sem.num, 0) + n

    def bump(self, sem, n):
        self.cnt[sem.num] = self.cnt.get(sem.num, 0) + n

    def val(self, sem):
        return self.cnt.get(sem.num, 0)

    def wait(self, proxy, sem, v=None):
        if proxy.live:
            self._wait_raw(sem, self.val(sem) if v is None else v)

    def _wait_raw(self, sem, v):
        self.real.wait_ge(sem, v)


def build_program(debug=False):
    nc = bass.Bass("TRN2", target_bir_lowering=False, debug=False)

    x_in = nc.dram_tensor("x_in", [NSLOT * BT, D], F32, kind="ExternalInput").ap()
    W_in = {nm: nc.dram_tensor(nm, shp, F32, kind="ExternalInput").ap()
            for nm, shp in [("wq", [D, D]), ("wk", [D, D]), ("wv", [D, D]),
                            ("wo", [D, D]), ("w1a", [DFF, D]), ("w2a", [D, DFF])]}
    BCsh = {"bq": 8, "bk": 8, "bo": 8, "b2a": 8, "g2": 8, "be2": 8,
            "g3": 8, "be3": 8, "b1a": 32}
    BC = {nm: nc.dram_tensor(nm, [128, n], F32, kind="ExternalInput").ap()
          for nm, n in BCsh.items()}
    bv_row = nc.dram_tensor("bv_row", [1, D], F32, kind="ExternalInput").ap()
    proj_in = nc.dram_tensor("proj_in", [M, DH], F32, kind="ExternalInput").ap()
    rw_in = nc.dram_tensor("rw_in", [NE, D], F32, kind="ExternalInput").ap()
    w1m_in = nc.dram_tensor("w1m", [NE * D, DFF], F32, kind="ExternalInput").ap()
    b1m_in = nc.dram_tensor("b1m", [NE * 128, 32], F32, kind="ExternalInput").ap()
    w2m_in = nc.dram_tensor("w2m", [NE * DFF, D], F32, kind="ExternalInput").ap()
    b2m_in = nc.dram_tensor("b2m", [NE, D], F32, kind="ExternalInput").ap()
    ident_in = nc.dram_tensor("ident_in", [128, 128], F32, kind="ExternalInput").ap()
    identb_in = nc.dram_tensor("identb_in", [128, 128], BF16, kind="ExternalInput").ap()
    iotam_in = nc.dram_tensor("iotam_in", [128, NE], F32, kind="ExternalInput").ap()
    onescb_in = nc.dram_tensor("onescb_in", [128, 1], BF16, kind="ExternalInput").ap()
    onescf_in = nc.dram_tensor("onescf_in", [128, 1], F32, kind="ExternalInput").ap()
    onesrb_in = nc.dram_tensor("onesrb_in", [1, 128], BF16, kind="ExternalInput").ap()
    onesrf_in = nc.dram_tensor("onesrf_in", [1, 128], F32, kind="ExternalInput").ap()
    indh_in = nc.dram_tensor("indh_in", [128, 128], BF16, kind="ExternalInput").ap()
    tri_i_in = nc.dram_tensor("tri_i_in", [128, 128], BF16, kind="ExternalInput").ap()
    tri_s_in = nc.dram_tensor("tri_s_in", [128, 128], BF16, kind="ExternalInput").ap()
    parity_in = nc.dram_tensor("parity_in", [1, 1], I32, kind="ExternalInput").ap()
    salts_in = nc.dram_tensor("salts_in", [1, 64], I32, kind="ExternalInput").ap()

    out = nc.dram_tensor("out", [BT, D], F32, kind="ExternalOutput").ap()

    dbg = {}

    def dbg_t(name, shape, dt=F32):
        if debug:
            dbg[name] = nc.dram_tensor("dbg_" + name, shape, dt, kind="ExternalOutput").ap()
            return dbg[name]
        return None

    if debug:
        dbg_t("xnT0", [128, 8 * BT], BF16)
        dbg_t("prob", [128, 128])
        dbg_t("idx", [128, 128])
        dbg_t("stab", [1, HEADS])
        dbg_t("ES", [128, 2080])
        dbg_t("vsum", [128, 8])
        dbg_t("ctx", [128, HEADS * 2 * DH], BF16)
        dbg_t("ksum", [128, 2 * HEADS], BF16)
        dbg_t("fe", [128, 128], BF16)
        dbg_t("x2", [128, 8 * BT])
        dbg_t("x3", [128, 8 * BT])
        dbg_t("ya", [128, 8 * BT])
        dbg_t("ym", [BT, D], BF16)
        dbg_t("qTd", [128, 8 * BT], BF16)
        dbg_t("dqc", [128, 16 * HEADS])
        dbg_t("attnT", [128, 8 * 512], BF16)
        dbg_t("s2c", [128, 8 * 512])
        dbg_t("xnd", [128, 8 * BT], BF16)
        dbg_t("maskb", [128, NE * 16])
        dbg_t("xnT0e", [128, 8 * BT], BF16)
        for _s in (1, 2, 3):
            dbg_t(f"xnT0s{_s}", [128, 8 * BT], BF16)
        dbg_t("xnT0m", [128, 8 * BT], BF16)
        dbg_t("diagc", [128, 8 * HEADS])
        dbg_t("dstab", [128, 8 * HEADS])
        dbg_t("ESpre", [128, 2080])
        dbg_t("stabpre", [1, HEADS])

    xhi_d = nc.dram_tensor("xhi_d", [NSLOT * BT, D], BF16).ap()
    xlo_d = nc.dram_tensor("xlo_d", [NSLOT * BT, D], BF16).ap()
    wb_d = {nm: nc.dram_tensor(nm + "b_d", list(W_in[nm].shape), BF16).ap() for nm in W_in}
    x2_d = nc.dram_tensor("x2_d", [128, 8 * BT], F32).ap()
    ya_d = nc.dram_tensor("ya_d", [128, 8 * BT], F32).ap()
    ym_d = nc.dram_tensor("ym_d", [BT, D], BF16).ap()
    xn_own_d = nc.dram_tensor("xn_own_d", [128, 8 * BT], BF16).ap()
    vsum_d = nc.dram_tensor("vsum_d", [1, 1024], F32).ap()
    x3_d = nc.dram_tensor("x3_d", [128, 8 * BT], F32).ap()
    exch_a = nc.dram_tensor("exch_a", [128, 2304], F32, addr_space="Shared").ap()
    exch_b = nc.dram_tensor("exch_b", [128, 2304], F32, addr_space="Shared").ap()
    flag_a = nc.dram_tensor("flag_a", [1, 1], I32, addr_space="Shared").ap()
    flag_b = nc.dram_tensor("flag_b", [1, 1], I32, addr_space="Shared").ap()

    def A(name, shape, dt):
        return nc.alloc_sbuf_tensor(name, shape, dt).ap()

    id_f = A("id_f", [128, 128], F32)
    id_b = A("id_b", [128, 128], BF16)
    iotam = A("iotam", [128, NE], F32)
    onescol_b = A("onescol_b", [128, 1], BF16)
    onescol_f = A("onescol_f", [128, 1], F32)
    onesrow_b = A("onesrow_b", [1, 128], BF16)
    onesrow_f = A("onesrow_f", [1, 128], F32)
    halves = A("halves", [2, 128], F32)
    indh = A("indh", [128, 128], BF16)
    tri_i = A("tri_i", [128, 128], BF16)
    tri_s = A("tri_s", [128, 128], BF16)
    projT = A("projT", [128, M], BF16)
    rwhT = A("rwhT", [128, 64], BF16)
    rwlT = A("rwlT", [128, 64], BF16)
    s1col = A("s1col", [NE, 1], F32)
    bias_c = {nm: A(nm + "_c", [128, BCsh[nm]], F32) for nm in BCsh}
    par_sb = A("par_sb", [1, 1], I32)
    salt_sb = A("salt_sb", [1, 64], I32)
    flg_sb = A("flg_sb", [1, 1], I32)
    ksum_col = A("ksum_col", [128, 2 * HEADS], BF16)
    FCONSTS = [LN_EPS, -0.5, NORM, LNR]
    cvals = A("cvals", [128, len(FCONSTS)], F32)
    for _i, _v in enumerate(FCONSTS):
        nc.const_aps.register(F32, float(_v), cvals[:, _i:_i + 1])

    PBASE = (nc.bytes_allocated(MemorySpace.SBUF) + 127) // 128 * 128
    LIMIT = 229376

    def AT(name, shape, dt, off):
        esz = {F32: 4, BF16: 2, U8: 1, I32: 4}[dt]
        szb = int(np.prod(shape[1:])) * esz
        assert PBASE + off + szb <= LIMIT, (name, PBASE, off, szb, PBASE + off + szb)
        return nc.alloc_sbuf_tensor_at(name, shape, dt, offset=PBASE + off).ap()

    def mkcursor(start):
        state = {"o": start}

        def alloc(name, shape, dt):
            esz = {F32: 4, BF16: 2, U8: 1, I32: 4}[dt]
            szb = (int(np.prod(shape[1:])) * esz + 127) // 128 * 128
            t = AT(name, shape, dt, state["o"])
            state["o"] += szb
            return t

        return alloc, state

    ES = AT("ES", [128, 2080], F32, 160 * KB)
    prob_g = AT("prob_g", [128, 128], F32, 168 * KB + 256)
    idx_g = AT("idx_g", [128, 128], F32, 168 * KB + 768)
    x2T = AT("x2T", [128, 8 * BT], BF16, 169 * KB + 256)
    fe_g = AT("fe_g", [128, 128], BF16, 201 * KB + 256)
    mask_b = AT("mask_b", [128, NE * 16], F32, 201 * KB + 512)
    xnT_own = AT("xnT_own", [128, 8 * BT], BF16, 0)

    goff = {}

    def ga(name, shape, dt):
        goff[name] = gst["o"]
        return _ga(name, shape, dt)

    _ga, gst = mkcursor(32 * KB)
    kT_b = ga("kT_b", [128, 8 * HB], BF16)
    xnT_s = ga("xnT_s", [128, 8 * 512], BF16)
    v_b = ga("v_b", [128, 8 * HEADS * 65], BF16)
    wkT = ga("wkT", [128, 8 * 1024], BF16)
    wvT = ga("wvT", [128, 8 * 1024], BF16)
    xq_g = [ga(f"xq_g{i}", [128, D], F32) for i in range(2)]
    hiT_s = ga("hiT_s", [128, 8 * 512], BF16)
    loT_s = ga("loT_s", [128, 8 * 512], BF16)
    xlo_st = ga("xlo_st", [128, 4 * D], BF16)
    diag_c = ga("diag_c", [128, 8 * HEADS], F32)
    dstab = ga("dstab", [128, 8 * HEADS], F32)
    gsc = ga("gsc", [128, 1536], F32)
    gsb = ga("gsb", [128, 1024], BF16)
    stat_c = ga("stat_c", [128, 8], F32)
    stat_r = ga("stat_r", [1, 512], BF16)
    mu_rf = ga("mu_rf", [1, 512], F32)
    maxc = ga("maxc", [128, HEADS], F32)
    stabc = ga("stabc", [128, HEADS], F32)
    p_sc = ga("p_sc", [128, 520], F32)
    srow = ga("srow", [1, 8 * HEADS], F32)
    stab_run = ga("stab_run", [1, HEADS], F32)
    vsum_sb = ga("vsum_sb", [128, 8], F32)
    bv_bc = ga("bv_bc", [128, D], F32)
    v_c = ga("v_c", [128, D], BF16)
    rw_sb = AT("rw_sb", [NE, D], F32, goff["kT_b"])
    rwh_sb = AT("rwh_sb", [NE, D], BF16, goff["kT_b"] + 4 * KB)
    rwl_sb = AT("rwl_sb", [NE, D], BF16, goff["kT_b"] + 6 * KB)
    kp_t = [AT(f"kp_t{i}", [128, M], BF16, goff["xq_g0"] + 512 * i) for i in range(2)]
    k2_t = AT("k2_t", [128, 512], BF16, goff["xq_g0"] + 1024)
    diag_r = AT("diag_r", [16, 512], F32, goff["gsb"])
    vsr_sb = AT("vsr_sb", [1, 1024], F32, goff["xq_g1"])
    exch_w = AT("exch_w", [128, 2304], F32, goff["diag_c"])
    exch_w = AT("exch_w2", [128, 2304], F32, goff["hiT_s"])
    assert gst["o"] <= 160 * KB, gst["o"]

    aa, ast = mkcursor(32 * KB)
    qT = aa("qT", [128, 8 * BT], BF16)
    wqT = aa("wqT", [128, 8 * 1024], BF16)
    woT = aa("woT", [128, 8 * 1024], BF16)
    ctx_sb = aa("ctx_sb", [128, HEADS * 2 * DH], BF16)
    qp_t = [aa(f"qp_t{i}", [128, M], BF16) for i in range(2)]
    q2_t = aa("q2_t", [128, 512], BF16)
    diagq_r = aa("diagq_r", [16, 512], F32)
    diagq_c = aa("diagq_c", [128, 16 * HEADS], F32)
    attnT = aa("attnT", [128, 8 * 512], BF16)
    s2c = aa("s2c", [128, 8 * 512], F32)
    a1sc = aa("a1sc", [128, 1024], F32)
    a1sb = aa("a1sb", [128, 1024], BF16)
    a1sb2 = aa("a1sb2", [128, 512], BF16)
    a1sc2 = aa("a1sc2", [128, 512], F32)
    a1sc3 = aa("a1sc3", [128, 512], F32)
    statA = aa("statA", [1, 4 * 512], F32)
    rtw = aa("rtw", [128, 1088], F32)
    rtw2 = aa("rtw2", [128, 1088], F32)
    rts = aa("rts", [1, 256], F32)
    assert ast["o"] <= 160 * KB, ast["o"]

    ba, bst = mkcursor(0)
    w1aT = ba("w1aT", [128, 8 * DFF], BF16)
    w2aTq = [ba(f"w2aTq{i}", [128, 8 * 1024], BF16) for i in range(2)]
    Ha = ba("Ha", [128, 32 * 512], BF16)
    x2f = [ba(f"x2f{i}", [128, 512], F32) for i in range(2)]
    x3f = [ba(f"x3f{i}", [128, 512], F32) for i in range(2)]
    a2sc = ba("a2sc", [128, 1024], F32)
    a2sb = ba("a2sb", [128, 1024], BF16)
    statB = ba("statB", [1, 4 * 512], F32)
    assert bst["o"] <= 160 * KB, bst["o"]

    ma, mst = mkcursor(0)
    w1sb = ma("w1sb", [128, 8 * DFF], BF16)
    w2sb = ma("w2sb", [128, 32 * 1024], BF16)
    Hm = ma("Hm", [128, 32 * 512], BF16)
    xnm = [ma(f"xnm{i}", [128, 8 * 512], BF16) for i in range(2)]
    yst = [ma(f"yst{i}", [128, 512], BF16) for i in range(2)]
    b1m_sb = [ma(f"b1m_sb{i}", [128, 32], F32) for i in range(2)]
    b2m_sb = [ma(f"b2m_sb{i}", [1, D], BF16) for i in range(2)]
    zt = ma("zt", [128, 512], BF16)
    assert mst["o"] <= 201 * KB, mst["o"]

    fa, fst = mkcursor(0)
    fx = [fa(f"fx{i}", [128, D], F32) for i in range(2)]
    fya = [fa(f"fya{i}", [128, 8 * 128], F32) for i in range(2)]
    fym = [fa(f"fym{i}", [128, D], BF16) for i in range(2)]
    fout = [fa(f"fout{i}", [128, D], F32) for i in range(2)]
    assert fst["o"] <= 32 * KB

    pb = [nc.alloc_psum_tensor(f"pb{i}", [128, 512], F32).ap() for i in range(3)]
    pE = nc.alloc_psum_tensor("pE", [128, 2560], F32).ap()
    pEb = pE.bitcast(BF16)

    def pES(i):
        b, j = divmod(i, 7)
        return pE[:, b * 512 + j * 65: b * 512 + j * 65 + 65]

    PES_VSUM = pE[:, 4 * 512 + 4 * 65: 4 * 512 + 4 * 65 + 8]
    PES_MU = pE[:, 2048:2560]

    stack = contextlib.ExitStack()
    S_ld = stack.enter_context(nc.semaphore("s_ld"))
    S_gp = stack.enter_context(nc.semaphore("s_gp"))
    S_pe = stack.enter_context(nc.semaphore("s_pe"))
    S_ve = stack.enter_context(nc.semaphore("s_ve"))
    S_ae = stack.enter_context(nc.semaphore("s_ae"))
    S_sp = stack.enter_context(nc.semaphore("s_sp"))
    blk = stack.enter_context(nc.Block())

    env = dict(locals())

    def run_replay(role, real):
        P = Prog(role, real, None)
        emit_program(P, env)

    @blk.sync
    def _(e):
        run_replay("sync", e)

    @blk.gpsimd
    def _(e):
        run_replay("gpsimd", e)

    @blk.tensor
    def _(e):
        run_replay("tensor", e)

    @blk.vector
    def _(e):
        run_replay("vector", e)

    @blk.scalar
    def _(e):
        run_replay("scalar", e)

    stack.close()
    return nc, dbg


def emit_program(P, E):
    ld, gp, pe, ve, ae, sp = (E["S_ld"], E["S_gp"], E["S_pe"], E["S_ve"],
                              E["S_ae"], E["S_sp"])
    sy, gs, tn, vc, sc = P.sync, P.gpsimd, P.tensor, P.vector, P.scalar
    debug = E["debug"]
    dbg = E["dbg"]
    pb, pE, pEb, pES, PES_VSUM = E["pb"], E["pE"], E["pEb"], E["pES"], E["PES_VSUM"]
    N = lambda k: E[k]

    for src, dst in [("ident_in", "id_f"), ("identb_in", "id_b"),
                     ("iotam_in", "iotam"), ("onescb_in", "onescol_b"),
                     ("onescf_in", "onescol_f"), ("onesrb_in", "onesrow_b"),
                     ("onesrf_in", "onesrow_f"), ("indh_in", "indh"),
                     ("tri_i_in", "tri_i"), ("tri_s_in", "tri_s"),
                     ("parity_in", "par_sb"), ("salts_in", "salt_sb"),
                     ("rw_in", "rw_sb")]:
        P.inc(sy.dma_start(out=N(dst), in_=N(src)), ld, 16)
    for nm in E["bias_c"]:
        P.inc(sy.dma_start(out=E["bias_c"][nm], in_=E["BC"][nm]), ld, 16)
    gsc, gsb = N("gsc"), N("gsb")
    P.inc(sy.dma_start(out=gsc[:, 0:64], in_=N("proj_in")[0:128, :]), ld, 16)
    P.inc(sy.dma_start(out=gsc[:, 64:128], in_=N("proj_in")[128:256, :]), ld, 16)
    P.inc(sy.dma_start(out=gsc[0:1, 128:128 + D], in_=N("bv_row")), ld, 16)
    LD0 = P.val(ld)

    for nm in E["W_in"]:
        P.inc(gs.dma_start(out=E["wb_d"][nm], in_=E["W_in"][nm]), gp, 16)
    P.inc(gs.dma_start(out=N("xhi_d"), in_=N("x_in")), gp, 16)
    GP0 = P.val(gp)

    P.wait(sy, gp, GP0)
    wkT, wvT = N("wkT"), N("wvT")
    for kc in range(8):
        P.inc(sy.dma_start(out=wkT[:, kc * 1024:(kc + 1) * 1024],
                           in_=E["wb_d"]["wk"][:, kc * 128:(kc + 1) * 128],
                           transpose=True), ld, 16)
        P.inc(sy.dma_start(out=wvT[:, kc * 1024:(kc + 1) * 1024],
                           in_=E["wb_d"]["wv"][:, kc * 128:(kc + 1) * 128],
                           transpose=True), ld, 16)
    LD_WKV = P.val(ld)

    for _i, _v in enumerate([LN_EPS, -0.5, NORM, LNR]):
        vc.memset(E["cvals"][:, _i:_i + 1], float(_v))
    vc.drain()
    P.wait(vc, ld, LD0)
    rw_sb, rwh_sb, rwl_sb = N("rw_sb"), N("rwh_sb"), N("rwl_sb")
    vc.tensor_copy(rwh_sb, rw_sb)
    vc.drain()
    vc.tensor_tensor(rwl_sb, rw_sb, rwh_sb, ALU.subtract)
    vc.drain()
    vc.reduce_sum(N("s1col"), rw_sb, axis=AX.X)
    vv = N("v_b").rearrange("p (t h c) -> p t h c", h=HEADS, c=65)
    vc.memset(vv[:, :, :, 64:65], 1.0)
    vc.memset(N("ES"), 0.0)
    vc.memset(N("stab_run"), -30000.0)
    P.inc(vc.memset(N("maxc"), -30000.0), ve, 1)
    VE_RW = P.val(ve)

    P.wait(tn, ld, LD0)
    P.wait(tn, ve, VE_RW)
    id_f, id_b = N("id_f"), N("id_b")
    tn.transpose(pb[0][:, 0:128], gsc[:, 0:128], id_f)
    tn.transpose(pb[1][:, 0:128], gsc[:, 0:128], id_f)
    for kc in range(8):
        tn.transpose(pEb[:, kc * 16:kc * 16 + 8], rwh_sb[:, kc * 128:(kc + 1) * 128], id_b[0:8, 0:8])
    for kc in range(8):
        tn.transpose(pEb[:, 128 + kc * 16:128 + kc * 16 + 8], rwl_sb[:, kc * 128:(kc + 1) * 128], id_b[0:8, 0:8])
    tn.matmul(pb[2], N("onesrow_f"), gsc[0:1, 128:128 + 512], start=True, stop=True)
    P.inc(tn.matmul(pE[:, 2048:2560], N("onesrow_f"), gsc[0:1, 128 + 512:128 + D], start=True, stop=True), pe, 1)
    PE_INIT = P.val(pe)

    P.wait(sc, pe, PE_INIT)
    projT, bv_bc = N("projT"), N("bv_bc")
    sc.activation(out=projT[0:64, 0:128], in_=pb[0][0:64, 0:128], func=AF.Identity, scale=NORM)
    sc.activation(out=projT[0:64, 128:256], in_=pb[0][64:128, 0:128], func=AF.Identity, scale=NORM)
    sc.activation(out=projT[64:128, 0:128], in_=pb[1][0:64, 0:128], func=AF.Identity, scale=NORM)
    sc.activation(out=projT[64:128, 128:256], in_=pb[1][64:128, 0:128], func=AF.Identity, scale=NORM)
    sc.activation(out=bv_bc[:, 0:512], in_=pb[2], func=AF.Copy)
    P.inc(sc.activation(out=bv_bc[:, 512:1024], in_=pE[:, 2048:2560], func=AF.Copy), ae, 1)
    P.wait(vc, ae)
    vc.tensor_copy(N("rwhT"), pEb[:, 0:128].rearrange("p (k c) -> p k c", c=16)[:, :, 0:8])
    P.inc(vc.tensor_copy(N("rwlT"), pEb[:, 128:256].rearrange("p (k c) -> p k c", c=16)[:, :, 0:8]), ve, 1)
    VE_CONST = P.val(ve)
    P.wait(tn, ve, VE_CONST)

    emit_g(P, E)


def emit_g(P, E):
    ld, gp, pe, ve, ae, sp = (E["S_ld"], E["S_gp"], E["S_pe"], E["S_ve"],
                              E["S_ae"], E["S_sp"])
    sy, gs, tn, vc, sc = P.sync, P.gpsimd, P.tensor, P.vector, P.scalar
    pb, pE, pEb, pES = E["pb"], E["pE"], E["pEb"], E["pES"]
    N = lambda k: E[k]
    x_in, xhi_d, xlo_d = N("x_in"), N("xhi_d"), N("xlo_d")
    xq_g, gsc, gsb, stat_c = N("xq_g"), N("gsc"), N("gsb"), N("stat_c")
    stat_r, mu_rf = N("stat_r"), N("mu_rf")
    hiT_s, loT_s, xlo_st = N("hiT_s"), N("loT_s"), N("xlo_st")
    xnT_s, xnT_own = N("xnT_s"), N("xnT_own")
    kT_b, v_b, wkT, wvT = N("kT_b"), N("v_b"), N("wkT"), N("wvT")
    kp_t, k2_t = N("kp_t"), N("k2_t")
    diag_r, diag_c, dstab = N("diag_r"), N("diag_c"), N("dstab")
    maxc, stabc, srow = N("maxc"), N("stabc"), N("srow")
    stab_run, ES, vsum_sb = N("stab_run"), N("ES"), N("vsum_sb")
    prob_g, idx_g, iotam = N("prob_g"), N("idx_g"), N("iotam")
    projT, indh = N("projT"), N("indh")
    bias_c, bv_bc = N("bias_c"), N("bv_bc")
    onesrow_f, onesrow_b, onescol_b = N("onesrow_f"), N("onesrow_b"), N("onescol_b")
    id_f, s1col, p_sc = N("id_f"), N("s1col"), N("p_sc")

    for slot in range(NSLOT):
        for half in range(2):
            r0 = slot * BT + half * HB
            ebank_started = set()

            def estart(i):
                b = 3 + (i * 65) // 512 if False else 3 + ((i // 7))
                if b in ebank_started:
                    return False
                ebank_started.add(b)
                return True

            for t5 in range(2):
                tb = r0 + t5 * 512
                lo_half = half * HB + t5 * 512
                for t1 in range(4):
                    buf = xq_g[t1 % 2]
                    P.wait(sy, ve)
                    P.inc(sy.dma_start(out=buf, in_=x_in[tb + t1 * 128: tb + (t1 + 1) * 128, :]), ld, 16)
                    P.wait(sc, ld)
                    P.inc(sc.activation(out=gsc[:, 0:1024], in_=buf, func=AF.Square), ae, 1)
                    P.wait(vc, ld)
                    c0 = stat_c[:, 0:1]
                    c1 = stat_c[:, 1:2]
                    c2 = stat_c[:, 2:3]
                    vc.reduce_sum(c0, buf, axis=AX.X)
                    P.wait(vc, ae)
                    vc.reduce_sum(c1, gsc[:, 0:1024], axis=AX.X)
                    vc.drain()
                    vc.tensor_scalar(c0, c0, 1.0 / D, None, ALU.mult)
                    vc.drain()
                    vc.tensor_tensor(c2, c0, c0, ALU.mult)
                    vc.drain()
                    vc.tensor_scalar(c1, c1, 1.0 / D, None, ALU.mult)
                    vc.drain()
                    P.inc(vc.tensor_tensor(c1, c1, c2, ALU.subtract), ve, 1)
                    P.wait(sc, ve)
                    sc.activation(out=c2, in_=c1, func=AF.Ln, bias=LN_EPS)
                    P.inc(sc.activation(out=c1, in_=c2, func=AF.Exp, scale=-0.5), ae, 1)
                    P.wait(vc, ae)
                    vc.tensor_copy(gsb[:, 0:1024], buf)
                    vc.drain()
                    P.inc(vc.tensor_tensor(xlo_st[:, t1 * D:(t1 + 1) * D], buf, gsb[:, 0:1024], ALU.subtract), ve, 1)
                    P.wait(tn, ae)
                    tn.transpose(pb[2][0:1, 0:128], stat_c[:, 0:1], id_f)
                    P.inc(tn.transpose(pb[2][0:1, 128:256], stat_c[:, 1:2], id_f), pe, 1)
                    P.wait(sc, pe)
                    ro = t1 * 128
                    sc.activation(out=mu_rf[0:1, ro:ro + 128], in_=pb[2][0:1, 0:128], func=AF.Copy)
                    P.inc(sc.activation(out=stat_r[0:1, ro:ro + 128], in_=pb[2][0:1, 128:256], func=AF.Copy), ae, 1)
                P.wait(sy, ve)
                P.inc(sy.dma_start(out=xlo_d[tb:tb + 512, :].rearrange("(a p) d -> p a d", p=128),
                                   in_=xlo_st.rearrange("p (a d) -> p a d", d=D)), ld, 16)
                XLO_ST = P.val(ld)
                P.wait(sy, ld, XLO_ST)
                for kc in range(8):
                    P.inc(sy.dma_start(out=hiT_s[:, kc * 512:(kc + 1) * 512],
                                       in_=xhi_d[tb:tb + 512, kc * 128:(kc + 1) * 128], transpose=True), ld, 16)
                    P.inc(sy.dma_start(out=loT_s[:, kc * 512:(kc + 1) * 512],
                                       in_=xlo_d[tb:tb + 512, kc * 128:(kc + 1) * 128], transpose=True), ld, 16)
                P.wait(tn, ae)
                P.wait(tn, ld)
                P.inc(tn.matmul(pb[2], onesrow_b, stat_r[0:1, 0:512], start=True, stop=True), pe, 1)
                P.wait(sc, pe)
                P.inc(sc.activation(out=gsc[:, 1024:1536], in_=pb[2], func=AF.Copy), ae, 1)
                P.wait(tn, ae)
                P.inc(tn.matmul(pb[2], onesrow_f, mu_rf[0:1, 0:512], start=True, stop=True), pe, 1)
                PE_MU = P.val(pe)
                P.wait(vc, pe, PE_MU)
                for kc in range(8):
                    vc.tensor_tensor(gsb[:, 0:512], hiT_s[:, kc * 512:(kc + 1) * 512], pb[2], ALU.subtract)
                    vc.drain()
                    vc.tensor_tensor(xnT_s[:, kc * 512:(kc + 1) * 512], gsb[:, 0:512], gsc[:, 1024:1536], ALU.mult)
                    vc.drain()
                    if slot == 0:
                        vc.tensor_copy(xnT_own[:, kc * BT + lo_half: kc * BT + lo_half + 512],
                                       xnT_s[:, kc * 512:(kc + 1) * 512])
                        vc.drain()
                P.inc(vc.engine_nop(), ve, 1)
                VE_XN = P.val(ve)
                P.wait(tn, ve, VE_XN)
                first = True
                last_i = None
                for kc in range(8):
                    for lh, rh in [(N("rwhT"), hiT_s), (N("rwhT"), loT_s),
                                   (N("rwlT"), hiT_s), (N("rwlT"), loT_s)]:
                        last_i = tn.matmul(pE[0:8, 0:512], lh[:, kc * 8:(kc + 1) * 8],
                                           rh[:, kc * 512:(kc + 1) * 512],
                                           start=first, stop=(kc == 7 and lh is N("rwlT") and rh is loT_s))
                        first = False
                P.inc(last_i, pe, 1)
                P.inc(tn.matmul(pb[0][0:8, :], onesrow_f[0:1, 0:8],
                               mu_rf[0:1, 0:512], start=True, stop=True), pe, 1)
                PE_LG = P.val(pe)
                P.wait(sc, pe, PE_LG)
                P.inc(sc.activation(out=gsc[0:8, 0:512], in_=pE[0:8, 0:512], func=AF.Copy), ae, 1)
                P.wait(vc, ae)
                vc.tensor_scalar(gsc[0:8, 512:1024], pb[0][0:8, :], s1col, None, ALU.mult)
                vc.drain()
                vc.tensor_tensor(gsc[0:8, 0:512], gsc[0:8, 0:512], gsc[0:8, 512:1024], ALU.subtract)
                vc.drain()
                P.inc(vc.tensor_tensor(gsc[0:8, 0:512], gsc[0:8, 0:512], gsc[0:8, 1024:1536], ALU.mult), ve, 1)
                VE_LG = P.val(ve)
                for t1 in range(4):
                    col = slot * 16 + half * 8 + t5 * 4 + t1
                    P.wait(tn, ve)
                    P.wait(tn, ae)
                    P.inc(tn.transpose(pb[1][:, 0:8], gsc[0:8, t1 * 128:(t1 + 1) * 128], id_f[0:8, 0:8]), pe, 1)
                    P.wait(vc, pe)
                    lt = p_sc[:, 0:8]
                    mx = p_sc[:, 8:9]
                    vc.tensor_copy(lt, pb[1][:, 0:8])
                    vc.drain()
                    vc.reduce_max(mx, lt, axis=AX.X)
                    vc.drain()
                    P.inc(vc.tensor_scalar(p_sc[:, 9:17], lt, mx, None, ALU.subtract), ve, 1)
                    P.wait(sc, ve)
                    P.inc(sc.activation(out=p_sc[:, 17:25], in_=p_sc[:, 9:17], func=AF.Exp), ae, 1)
                    P.wait(vc, ae)
                    vc.reduce_sum(p_sc[:, 25:26], p_sc[:, 17:25], axis=AX.X)
                    vc.drain()
                    vc.reciprocal(prob_g[:, col:col + 1], p_sc[:, 25:26])
                    vc.tensor_scalar(p_sc[:, 26:34], lt, mx, None, ALU.is_equal)
                    vc.drain()
                    vc.tensor_tensor(p_sc[:, 34:42], p_sc[:, 26:34], iotam, ALU.mult)
                    vc.drain()
                    vc.tensor_scalar(p_sc[:, 42:50], p_sc[:, 34:42], BIG, None, ALU.add)
                    vc.drain()
                    P.inc(vc.tensor_reduce(idx_g[:, col:col + 1], p_sc[:, 42:50], axis=AX.X, op=ALU.min), ve, 1)
                P.wait(tn, ve)
                P.wait(tn, ae)
                evac_done = {}
                for dc in range(8):
                    pbuf = pb[dc % 2]
                    if dc >= 2:
                        P.wait(tn, ae, evac_done[dc % 2])
                    lastm = None
                    for kc in range(8):
                        lastm = tn.matmul(pbuf, wkT[:, kc * 1024 + dc * 128: kc * 1024 + (dc + 1) * 128],
                                          xnT_s[:, kc * 512:(kc + 1) * 512], start=(kc == 0), stop=(kc == 7))
                    P.inc(lastm, pe, 1)
                    P.wait(sc, pe)
                    sc.activation(out=kT_b[:, dc * HB + t5 * 512: dc * HB + t5 * 512 + 512],
                                  in_=pbuf, func=AF.Identity, bias=bias_c["bk"][:, dc:dc + 1])
                    P.inc(sc.activation(out=k2_t, in_=pbuf, func=AF.Square, bias=bias_c["bk"][:, dc:dc + 1]), ae, 1)
                    evac_done[dc % 2] = P.val(ae)
                    P.wait(tn, ae)
                    P.inc(tn.matmul(pb[2][0:16, :], indh[:, dc * 16:(dc + 1) * 16], k2_t,
                                    start=(dc == 0), stop=(dc == 7)), pe, 1)
                P.wait(sc, pe)
                P.inc(sc.activation(out=diag_r[:, 0:512], in_=pb[2][0:16, :], func=AF.Copy), ae, 1)
                AE_DIAG = P.val(ae)
                P.wait(tn, ae)
                vdone = {}
                for t1 in range(4):
                    for dh2 in range(2):
                        i = (t1 * 2 + dh2) % 2
                        pbuf = pb[i]
                        if t1 * 2 + dh2 >= 2:
                            P.wait(tn, ve, vdone[i])
                        lastm = None
                        for kc in range(8):
                            lastm = tn.matmul(pbuf,
                                              xnT_s[:, kc * 512 + t1 * 128: kc * 512 + (t1 + 1) * 128],
                                              wvT[:, kc * 1024 + dh2 * 512: kc * 1024 + (dh2 + 1) * 512],
                                              start=(kc == 0), stop=(kc == 7))
                        P.inc(lastm, pe, 1)
                        P.wait(vc, pe)
                        tcg = t5 * 4 + t1
                        vc.tensor_tensor(N("v_c")[:, dh2 * 512:(dh2 + 1) * 512], pbuf,
                                         bv_bc[:, dh2 * 512:(dh2 + 1) * 512], ALU.add)
                        vc.drain()
                        dst = v_b.rearrange("p (t h c) -> p t h c", h=HEADS, c=65)[
                            :, tcg, dh2 * 8:(dh2 + 1) * 8, 0:64]
                        P.inc(vc.tensor_copy(dst, N("v_c")[:, dh2 * 512:(dh2 + 1) * 512].rearrange(
                            "p (h c) -> p h c", c=64)), ve, 1)
                        vdone[i] = P.val(ve)
                    P.wait(tn, ve)
                    P.wait(tn, ae)
                    lastm = None
                    for dc in range(8):
                        lastm = tn.matmul(pb[2][:, dc:dc + 1], N("v_c")[:, dc * 128:(dc + 1) * 128],
                                          N("onescol_b"), start=(dc == 0), stop=(dc == 7))
                    P.inc(lastm, pe, 1)
                    P.wait(vc, pe)
                    vc.tensor_tensor(vsum_sb, vsum_sb, pb[2][:, 0:8], ALU.add)
                    vc.drain()
                    P.inc(vc.engine_nop(), ve, 1)
                P.wait(tn, ae, AE_DIAG)
                P.wait(tn, ve)
                for t1 in range(4):
                    tcg = t5 * 4 + t1
                    P.wait(tn, ae)
                    P.inc(tn.transpose(pb[2][:, 0:16], diag_r[:, t1 * 128: t1 * 128 + 128],
                                       id_f[0:16, 0:16]), pe, 1)
                    P.wait(sc, pe)
                    P.inc(sc.activation(out=diag_c[:, tcg * 16:(tcg + 1) * 16], in_=pb[2][:, 0:16], func=AF.Copy), ae, 1)

            P.wait(tn, ve)
            P.wait(tn, ae)
            xp1_tiles = [pb[0], pb[1], pE[:, 0:512], pE[:, 512:1024]]
            rd_done = {}
            idx = 0
            for h in range(HEADS):
                hb64 = (h % 2) * 64
                for mc in range(2):
                    for t5 in range(2):
                        ptile = xp1_tiles[idx % 4]
                        if idx >= 4:
                            P.wait(tn, ve, rd_done[idx % 4])
                        P.inc(tn.matmul(ptile, projT[hb64:hb64 + 64, mc * 128:(mc + 1) * 128],
                                        kT_b[hb64:hb64 + 64, (h // 2) * HB + t5 * 512:(h // 2) * HB + t5 * 512 + 512],
                                        start=True, stop=True), pe, 1)
                        P.wait(vc, pe)
                        vc.reduce_max(p_sc[:, 50:51], ptile, axis=AX.X)
                        vc.drain()
                        vc.tensor_tensor(maxc[:, h:h + 1], maxc[:, h:h + 1], p_sc[:, 50:51], ALU.max)
                        P.inc(vc.engine_nop(), ve, 1)
                        rd_done[idx % 4] = P.val(ve)
                        idx += 1
            P.wait(tn, ve)
            P.inc(tn.transpose(pb[2][0:16, 0:128], maxc, id_f), pe, 1)
            P.wait(vc, pe)
            P.inc(vc.reduce_max(p_sc[0:16, 51:52], pb[2][0:16, 0:128], axis=AX.X), ve, 1)
            P.wait(tn, ve)
            P.inc(tn.transpose(pb[2][0:1, 128:144], p_sc[0:16, 51:52], id_f[0:16, 0:16]), pe, 1)
            P.wait(sc, pe)
            P.inc(sc.activation(out=srow[0:1, 0:16], in_=pb[2][0:1, 128:144], func=AF.Copy), ae, 1)
            P.wait(vc, ae)
            vc.memset(maxc, -30000.0)
            vc.drain()
            vc.tensor_tensor(srow[0:1, 16:32], stab_run, srow[0:1, 0:16], ALU.max)
            vc.drain()
            P.inc(vc.tensor_tensor(srow[0:1, 32:48], stab_run, srow[0:1, 16:32], ALU.subtract), ve, 1)
            P.wait(sc, ve)
            P.inc(sc.activation(out=srow[0:1, 48:64], in_=srow[0:1, 32:48], func=AF.Exp), ae, 1)
            P.wait(vc, ae)
            P.inc(vc.tensor_copy(stab_run, srow[0:1, 16:32]), ve, 1)
            P.wait(tn, ve)
            P.inc(tn.matmul(pb[2][:, 0:16], onesrow_f, srow[0:1, 48:64], start=True, stop=True), pe, 1)
            P.wait(sc, pe)
            P.inc(sc.activation(out=stabc, in_=pb[2][:, 0:16], func=AF.Copy), ae, 1)
            P.wait(vc, ae)
            for i in range(32):
                h = i // 2
                vc.tensor_scalar(ES[:, i * 65:(i + 1) * 65], ES[:, i * 65:(i + 1) * 65],
                                 stabc[:, h:h + 1], None, ALU.mult)
                vc.drain()
            P.inc(vc.engine_nop(), ve, 1)
            P.wait(tn, ve)
            P.inc(tn.matmul(pb[2][:, 16:32], onesrow_f, srow[0:1, 16:32], start=True, stop=True), pe, 1)
            P.wait(vc, pe)
            for tcg in range(8):
                vc.tensor_scalar(gsc[:, 0:16], pb[2][:, 16:32], -LNR, None, ALU.add)
                vc.drain()
                vc.tensor_scalar(gsc[:, 16:32], diag_c[:, tcg * 16:(tcg + 1) * 16], -1.0, None, ALU.mult)
                vc.drain()
                vc.tensor_tensor(dstab[:, tcg * 16:(tcg + 1) * 16], gsc[:, 16:32], gsc[:, 0:16], ALU.subtract)
                vc.drain()
            P.inc(vc.engine_nop(), ve, 1)
            VE_DSTAB = P.val(ve)
            ebank_started.clear()
            pend = None
            kp_done = {}
            exp_done = {}
            e_done = {}
            for h in range(HEADS):
                hb64 = (h % 2) * 64
                for tc in range(8):
                    i = tc % 2
                    pbuf = pb[i]
                    P.wait(tn, ve, VE_DSTAB)
                    if (h * 8 + tc) >= 2:
                        P.wait(tn, ae, exp_done[i])
                    P.inc(tn.matmul(pbuf[:, 0:256], kT_b[hb64:hb64 + 64, (h // 2) * HB + tc * 128:(h // 2) * HB + (tc + 1) * 128],
                                    projT[hb64:hb64 + 64, :], start=True, stop=True), pe, 1)
                    if pend is not None:
                        ph, ptc, pkp = pend
                        P.wait(tn, ve, kp_done[(ph, ptc)])
                        for mc in range(2):
                            si = ph * 2 + mc
                            st = estart(si)
                            lastm = tn.matmul(pES(si), pkp[:, mc * 128:(mc + 1) * 128],
                                              v_b.rearrange("p (t h c) -> p t h c", h=HEADS, c=65)[:, ptc, ph, :],
                                              start=st, stop=(ptc == 7))
                        P.inc(lastm, pe, 1)
                        e_done[ph * 8 + ptc] = P.val(pe)
                    P.wait(sc, pe)
                    P.wait(sc, ve)
                    P.inc(sc.activation(out=k2_t[:, 0:256], in_=pbuf[:, 0:256], func=AF.Exp,
                                        bias=dstab[:, tc * 16 + h:tc * 16 + h + 1]), ae, 1)
                    exp_done[i] = P.val(ae)
                    P.wait(vc, ae)
                    if (h * 8 + tc) >= 2:
                        P.wait(vc, pe, e_done.get((h * 8 + tc) - 2, 0))
                    P.inc(vc.tensor_copy(kp_t[i], k2_t[:, 0:256]), ve, 1)
                    kp_done[(h, tc)] = P.val(ve)
                    pend = (h, tc, kp_t[i])
            if pend is not None:
                ph, ptc, pkp = pend
                P.wait(tn, ve, kp_done[(ph, ptc)])
                for mc in range(2):
                    si = ph * 2 + mc
                    st = estart(si)
                    lastm = tn.matmul(pES(si), pkp[:, mc * 128:(mc + 1) * 128],
                                      v_b.rearrange("p (t h c) -> p t h c", h=HEADS, c=65)[:, ptc, ph, :],
                                      start=st, stop=True)
                P.inc(lastm, pe, 1)
                e_done[ph * 8 + ptc] = P.val(pe)
            PE_EDONE = P.val(pe)
            P.wait(vc, pe, PE_EDONE)
            for i in range(32):
                vc.tensor_tensor(ES[:, i * 65:(i + 1) * 65], ES[:, i * 65:(i + 1) * 65],
                                 pES(i), ALU.add)
                vc.drain()
            P.inc(vc.engine_nop(), ve, 1)
            P.wait(tn, ve)
            if E["debug"] and half == 1:
                nm = "xnT0e" if slot == 0 else f"xnT0s{slot}"
                P.wait(sy, ve)
                P.inc(sy.dma_start(out=E["dbg"][nm], in_=xnT_own), ld, 16)
                P.wait(sy, ld)
    emit_exchange_combine(P, E)
    if E["debug"]:
        emit_gdebug(P, E)
    emit_a1(P, E)
    emit_routing(P, E)
    emit_a2(P, E)
    emit_moe(P, E)
    emit_final(P, E)
    if E["debug"]:
        emit_fdebug(P, E)


def emit_exchange_combine(P, E):
    ld, gp, pe, ve, ae, sp = (E["S_ld"], E["S_gp"], E["S_pe"], E["S_ve"],
                              E["S_ae"], E["S_sp"])
    sy, gs, tn, vc, sc = P.sync, P.gpsimd, P.tensor, P.vector, P.scalar
    pb, pE, pEb = E["pb"], E["pE"], E["pEb"]
    N = lambda k: E[k]
    ES, vsum_sb, stab_run = N("ES"), N("vsum_sb"), N("stab_run")
    prob_g, idx_g, srow = N("prob_g"), N("idx_g"), N("srow")
    exch_w, stabc, maxc, gsc = N("exch_w"), N("stabc"), N("maxc"), N("gsc")
    ctx_sb, ksum_col = N("ctx_sb"), N("ksum_col")
    onesrow_f = N("onesrow_f")

    P.wait(vc, ae)
    vc.tensor_copy(exch_w[:, 0:2080], ES)
    vc.tensor_copy(exch_w[:, 2080:2088], vsum_sb)
    vc.tensor_copy(exch_w[0:1, 2088:2104], stab_run)
    vc.tensor_copy(exch_w[:, 2104:2168], prob_g[:, 0:64])
    vc.tensor_copy(exch_w[:, 2168:2232], idx_g[:, 0:64])
    vc.drain()
    P.inc(vc.engine_nop(), ve, 1)

    P.wait(gs, ve)
    if gs.live:
        gr = P.real
        with gr.register("p") as rp, gr.register("cnt") as rc, \
             gr.register("fv") as rf, gr.register("sv") as rs, \
             gr.register("tmp") as rt:
            gr.load(rp, N("par_sb"))
            gr.load(rs, N("salt_sb")[0:1, 0:1])
            with gr.If_eq(rp, 0):
                gr.dma_start(out=N("exch_a"), in_=exch_w).then_inc(gp, 16)
            with gr.Else():
                gr.dma_start(out=N("exch_b"), in_=exch_w).then_inc(gp, 16)
            gr.wait_ge(gp, P.val(gp) + 16)
            with gr.If_eq(rp, 0):
                gr.dma_start(out=N("flag_a"), in_=N("salt_sb")[0:1, 0:1]).then_inc(gp, 16)
            with gr.Else():
                gr.dma_start(out=N("flag_b"), in_=N("salt_sb")[0:1, 0:1]).then_inc(gp, 16)
            gr.wait_ge(gp, P.val(gp) + 32)

            def spin(flag):
                def cond():
                    gr.reg_add(rc, rc, 16)
                    gr.dma_start(out=N("flg_sb"), in_=flag).then_inc(sp, 16)
                    gr.wait_ge(sp, rc)
                    gr.load(rf, N("flg_sb"))
                    gr.reg_sub(rt, rf, rs)
                    return rt
                with gr.While(cond):
                    gr.nop()

            gr.reg_mov(rc, 0)
            with gr.If_eq(rp, 0):
                spin(N("flag_b"))
            with gr.Else():
                spin(N("flag_a"))
            with gr.If_eq(rp, 0):
                gr.dma_start(out=exch_w, in_=N("exch_b")).then_inc(gp, 16)
            with gr.Else():
                gr.dma_start(out=exch_w, in_=N("exch_a")).then_inc(gp, 16)
            gr.wait_ge(gp, P.val(gp) + 48)
    P.bump(gp, 48)
    if gs.live:
        P.real.nop().then_inc(gp, 1)
    P.bump(gp, 1)
    GPX = P.val(gp)

    if E["debug"]:
        P.wait(sy, gp, GPX)
        P.inc(sy.dma_start(out=E["dbg"]["ESpre"], in_=ES), ld, 16)
        P.inc(sy.dma_start(out=E["dbg"]["stabpre"], in_=stab_run), ld, 16)
        P.inc(sy.dma_start(out=E["dbg"]["xnT0m"], in_=E["xnT_own"]), ld, 16)
        P.inc(sy.dma_start(out=E["dbg"]["diagc"], in_=E["diag_c"]), ld, 16)
        P.inc(sy.dma_start(out=E["dbg"]["dstab"], in_=E["dstab"]), ld, 16)
        P.wait(sy, ld)
    P.wait(vc, gp, GPX)
    pstab = exch_w[0:1, 2088:2104]
    vc.tensor_tensor(srow[0:1, 0:16], stab_run, pstab, ALU.max)
    vc.drain()
    vc.tensor_tensor(srow[0:1, 16:32], stab_run, srow[0:1, 0:16], ALU.subtract)
    vc.drain()
    P.inc(vc.tensor_tensor(srow[0:1, 48:64], pstab, srow[0:1, 0:16], ALU.subtract), ve, 1)
    P.wait(sc, ve)
    sc.activation(out=srow[0:1, 32:48], in_=srow[0:1, 16:32], func=AF.Exp)
    P.inc(sc.activation(out=srow[0:1, 64:80], in_=srow[0:1, 48:64], func=AF.Exp), ae, 1)
    P.wait(vc, ae)
    P.inc(vc.tensor_copy(stab_run, srow[0:1, 0:16]), ve, 1)
    P.wait(tn, ae)
    tn.matmul(pb[2][:, 0:16], onesrow_f, srow[0:1, 32:48], start=True, stop=True)
    P.inc(tn.matmul(pb[2][:, 16:32], onesrow_f, srow[0:1, 64:80], start=True, stop=True), pe, 1)
    P.wait(sc, pe)
    sc.activation(out=stabc, in_=pb[2][:, 0:16], func=AF.Copy)
    P.inc(sc.activation(out=maxc, in_=pb[2][:, 16:32], func=AF.Copy), ae, 1)
    P.wait(vc, ae)
    for i in range(32):
        h = i // 2
        vc.tensor_scalar(gsc[:, 0:65], exch_w[:, i * 65:(i + 1) * 65], maxc[:, h:h + 1], None, ALU.mult)
        vc.drain()
        vc.scalar_tensor_tensor(ES[:, i * 65:(i + 1) * 65], ES[:, i * 65:(i + 1) * 65],
                                stabc[:, h:h + 1], gsc[:, 0:65], ALU.mult, ALU.add)
        vc.drain()
    vc.tensor_tensor(vsum_sb, vsum_sb, exch_w[:, 2080:2088], ALU.add)
    vc.drain()
    vc.tensor_copy(prob_g[:, 64:128], exch_w[:, 2104:2168])
    vc.tensor_copy(idx_g[:, 64:128], exch_w[:, 2168:2232])
    vc.drain()
    P.inc(vc.engine_nop(), ve, 1)
    P.wait(sy, ve)
    P.inc(sy.dma_start(out=N("vsum_d").rearrange("o (a b) -> o b a", b=128), in_=vsum_sb,
                       allow_slow_non_contiguous=True), ld, 16)
    P.wait(sy, ld)
    P.inc(sy.dma_start(out=N("vsr_sb"), in_=N("vsum_d")), ld, 16)
    VSROW = P.val(ld)
    P.wait(tn, ld, VSROW)
    for h in range(HEADS):
        P.wait(tn, ve)
        P.inc(tn.matmul(pb[2][:, 64:128], onesrow_f, N("vsr_sb")[0:1, h * 64:(h + 1) * 64],
                        start=True, stop=True), pe, 1)
        P.wait(vc, pe)
        for mc in range(2):
            i = h * 2 + mc
            vc.scalar_tensor_tensor(ctx_sb[:, i * 64:(i + 1) * 64], pb[2][:, 64:128],
                                    RATIO * EPS_K, ES[:, i * 65:i * 65 + 64], ALU.mult, ALU.add)
            vc.drain()
        P.inc(vc.engine_nop(), ve, 1)
    esv = ES.rearrange("p (i c) -> p i c", c=65)[:, :, 64:65]
    P.inc(vc.tensor_scalar(ksum_col.rearrange("p (i c) -> p i c", c=1), esv,
                           RATIO * EPS_K * T, None, ALU.add), ve, 1)
    VE_CTX = P.val(ve)
    P.wait(tn, ve, VE_CTX)


def emit_gdebug(P, E):
    ld, ve, ae, pe = E["S_ld"], E["S_ve"], E["S_ae"], E["S_pe"]
    sy = P.sync
    dbg = E["dbg"]
    P.wait(sy, ve)
    P.wait(sy, ae)
    for nm, src in [("xnT0", E["xnT_own"]), ("prob", E["prob_g"]), ("idx", E["idx_g"]),
                    ("stab", E["stab_run"]), ("ES", E["ES"]), ("vsum", E["vsum_sb"]),
                    ("ctx", E["ctx_sb"]), ("ksum", E["ksum_col"])]:
        if nm in dbg:
            P.inc(sy.dma_start(out=dbg[nm], in_=src), ld, 16)
    P.wait(sy, ld)


def emit_a1(P, E):
    ld, gp, pe, ve, ae = (E["S_ld"], E["S_gp"], E["S_pe"], E["S_ve"], E["S_ae"])
    sy, gs, tn, vc, sc = P.sync, P.gpsimd, P.tensor, P.vector, P.scalar
    pb, pE, pEb = E["pb"], E["pE"], E["pEb"]
    N = lambda k: E[k]
    xnT_own, qT, wqT, woT = N("xnT_own"), N("qT"), N("wqT"), N("woT")
    ctx_sb, ksum_col = N("ctx_sb"), N("ksum_col")
    diagq_r, diagq_c = N("diagq_r"), N("diagq_c")
    q2_t, qp_t, a1sc, a1sb = N("q2_t"), N("qp_t"), N("a1sc"), N("a1sb")
    attnT, s2c, statA = N("attnT"), N("s2c"), N("statA")
    x2T, x2_d = N("x2T"), N("x2_d")
    stab_run = N("stab_run")
    bias_c, indh, projT, id_f = N("bias_c"), N("indh"), N("projT"), N("id_f")
    onesrow_f, onesrow_b, onescol_b = N("onesrow_f"), N("onesrow_b"), N("onescol_b")
    p_sc = N("p_sc")

    P.wait(sy, ve)
    for kc in range(8):
        P.inc(sy.dma_start(out=wqT[:, kc * 1024:(kc + 1) * 1024],
                           in_=E["wb_d"]["wq"][:, kc * 128:(kc + 1) * 128], transpose=True), ld, 16)
        P.inc(sy.dma_start(out=woT[:, kc * 1024:(kc + 1) * 1024],
                           in_=E["wb_d"]["wo"][:, kc * 128:(kc + 1) * 128], transpose=True), ld, 16)
    LD_QO = P.val(ld)
    P.wait(tn, ld, LD_QO)

    evac_done = {}
    for t5 in range(4):
        for dc in range(8):
            pbuf = pb[dc % 2]
            key = (t5 * 8 + dc) % 2
            if t5 * 8 + dc >= 2:
                P.wait(tn, ae, evac_done[key])
            lastm = None
            for kc in range(8):
                lastm = tn.matmul(pbuf, wqT[:, kc * 1024 + dc * 128: kc * 1024 + (dc + 1) * 128],
                                  xnT_own[:, kc * BT + t5 * 512: kc * BT + (t5 + 1) * 512],
                                  start=(kc == 0), stop=(kc == 7))
            P.inc(lastm, pe, 1)
            P.wait(sc, pe)
            sc.activation(out=qT[:, dc * BT + t5 * 512: dc * BT + t5 * 512 + 512],
                          in_=pbuf, func=AF.Identity, bias=bias_c["bq"][:, dc:dc + 1])
            P.inc(sc.activation(out=q2_t, in_=pbuf, func=AF.Square, bias=bias_c["bq"][:, dc:dc + 1]), ae, 1)
            evac_done[key] = P.val(ae)
            P.wait(tn, ae)
            P.inc(tn.matmul(pb[2][0:16, :], indh[:, dc * 16:(dc + 1) * 16], q2_t,
                            start=(dc == 0), stop=(dc == 7)), pe, 1)
        P.wait(sc, pe)
        P.inc(sc.activation(out=diagq_r[:, 0:512], in_=pb[2][0:16, :], func=AF.Copy), ae, 1)
        for t1 in range(4):
            tcg = t5 * 4 + t1
            P.wait(tn, ae)
            P.inc(tn.transpose(pb[2][:, 0:16], diagq_r[:, t1 * 128:(t1 + 1) * 128], id_f[0:16, 0:16]), pe, 1)
            P.wait(sc, pe)
            P.inc(sc.activation(out=diagq_c[:, tcg * 16:(tcg + 1) * 16], in_=pb[2][:, 0:16], func=AF.Copy), ae, 1)
    AE_QD = P.val(ae)

    A1_EXP = {0: 0, 1: 0}
    A1_WO = {0: 0, 1: 0}
    X2ST = {0: 0, 1: 0}

    for t5 in range(4):
        for h in range(HEADS):
            hb64 = (h % 2) * 64
            for t1 in range(4):
                tcg = t5 * 4 + t1
                i = t1 % 2
                pbuf = pb[i]
                P.wait(tn, ae, max(A1_EXP[i], AE_QD))
                P.inc(tn.matmul(pbuf[:, 0:256], qT[hb64:hb64 + 64, (h // 2) * BT + tcg * 128:(h // 2) * BT + (tcg + 1) * 128],
                                projT[hb64:hb64 + 64, :], start=True, stop=True), pe, 1)
                P.wait(vc, pe)
                vc.reduce_max(p_sc[:, 0:1], pbuf[:, 0:256], axis=AX.X)
                vc.drain()
                vc.tensor_tensor(p_sc[:, 1:2], p_sc[:, 0:1], diagq_c[:, tcg * 16 + h:tcg * 16 + h + 1], ALU.add)
                vc.drain()
                P.inc(vc.tensor_scalar(p_sc[:, 2:3], p_sc[:, 1:2], -1.0, LNR, ALU.mult, ALU.add), ve, 1)
                P.wait(sc, ve)
                P.inc(sc.activation(out=q2_t[:, 0:256], in_=pbuf[:, 0:256], func=AF.Exp,
                                    bias=p_sc[:, 2:3]), ae, 1)
                A1_EXP[i] = P.val(ae)
                P.wait(vc, ae)
                P.inc(vc.tensor_scalar(qp_t[i], q2_t[:, 0:256], RATIO * EPS_K, None, ALU.add), ve, 1)
                P.wait(tn, ve)
                tn.transpose(pEb[:, 0:128], qp_t[i][:, 0:128], N("id_b"))
                P.inc(tn.transpose(pEb[:, 128:256], qp_t[i][:, 128:256], N("id_b")), pe, 1)
                P.wait(vc, pe)
                vc.tensor_copy(a1sb[:, 0 * 512 + t1 * 128: 0 * 512 + (t1 + 1) * 128], pEb[:, 0:128])
                vc.drain()
                vc.tensor_copy(a1sb[:, 1 * 512 + t1 * 128: 1 * 512 + (t1 + 1) * 128], pEb[:, 128:256])
                vc.drain()
                P.inc(vc.engine_nop(), ve, 1)
            VE_QPT = P.val(ve)
            P.wait(tn, ve, VE_QPT)
            i2 = h * 2
            tn.matmul(pE[0:64, 1024:1536], ctx_sb[:, i2 * 64:(i2 + 1) * 64], a1sb[:, 0:512], start=True, stop=False)
            P.inc(tn.matmul(pE[0:64, 1024:1536], ctx_sb[:, (i2 + 1) * 64:(i2 + 2) * 64], a1sb[:, 512:1024], start=False, stop=True), pe, 1)
            tn.matmul(pE[0:1, 1536:2048], ksum_col[:, i2:i2 + 1], a1sb[:, 0:512], start=True, stop=False)
            P.inc(tn.matmul(pE[0:1, 1536:2048], ksum_col[:, i2 + 1:i2 + 2], a1sb[:, 512:1024], start=False, stop=True), pe, 1)
            P.wait(vc, pe)
            vc.reciprocal(p_sc[0:1, 8:520], pE[0:1, 1536:2048])
            vc.drain()
            P.inc(vc.engine_nop(), ve, 1)
            P.wait(tn, ve)
            P.inc(tn.matmul(pE[0:64, 2048:2560], onesrow_f[0:1, 0:64], p_sc[0:1, 8:520], start=True, stop=True), pe, 1)
            P.wait(sc, pe)
            P.inc(sc.activation(out=a1sc[0:64, 0:512], in_=pE[0:64, 1024:1536], func=AF.Copy), ae, 1)
            P.wait(vc, ae)
            hb64 = (h % 2) * 64
            P.inc(vc.tensor_tensor(attnT[hb64:hb64 + 64, (h // 2) * 512:(h // 2 + 1) * 512],
                                   a1sc[0:64, 0:512], pE[0:64, 2048:2560], ALU.mult), ve, 1)
        P.wait(tn, ve)
        P.wait(tn, ae)
        VE_WORD = 0
        for dc in range(8):
            pbuf = pb[dc % 2]
            P.wait(tn, ve, A1_WO[dc % 2])
            lastm = None
            for kc in range(8):
                lastm = tn.matmul(pbuf, woT[:, kc * 1024 + dc * 128: kc * 1024 + (dc + 1) * 128],
                                  attnT[:, kc * 512:(kc + 1) * 512], start=(kc == 0), stop=(kc == 7))
            P.inc(lastm, pe, 1)
            P.wait(sc, pe)
            P.wait(sc, ve, VE_WORD)
            P.inc(sc.activation(out=a1sc[:, 0:512], in_=pbuf, func=AF.Identity, bias=bias_c["bo"][:, dc:dc + 1]), ae, 1)
            P.wait(vc, ae)
            P.inc(vc.tensor_tensor(s2c[:, dc * 512:(dc + 1) * 512], a1sc[:, 0:512],
                                   xnT_own[:, dc * BT + t5 * 512: dc * BT + (t5 + 1) * 512], ALU.add), ve, 1)
            A1_WO[dc % 2] = P.val(ve)
            VE_WORD = P.val(ve)
        VE_S2 = P.val(ve)
        P.wait(vc, ve, VE_S2)
        STAT_PE = 0
        for dc in range(8):
            P.wait(vc, pe, STAT_PE)
            vc.tensor_copy(a1sb[:, 0:512], s2c[:, dc * 512:(dc + 1) * 512])
            vc.drain()
            P.inc(vc.engine_nop(), ve, 1)
            P.wait(tn, ve)
            P.inc(tn.matmul(pE[0:1, 0:512], onescol_b, a1sb[:, 0:512], start=(dc == 0), stop=(dc == 7)), pe, 1)
            P.wait(sc, ve)
            P.wait(sc, pe, STAT_PE)
            P.inc(sc.activation(out=N("a1sb2"), in_=s2c[:, dc * 512:(dc + 1) * 512], func=AF.Square), ae, 1)
            P.wait(tn, ae)
            P.inc(tn.matmul(pE[0:1, 512:1024], onescol_b, N("a1sb2"), start=(dc == 0), stop=(dc == 7)), pe, 1)
            STAT_PE = P.val(pe)
        P.wait(vc, pe)
        st = statA
        vc.tensor_scalar(st[0:1, 0:512], pE[0:1, 0:512], 1.0 / D, None, ALU.mult)
        vc.drain()
        vc.tensor_scalar(st[0:1, 512:1024], pE[0:1, 512:1024], 1.0 / D, None, ALU.mult)
        vc.drain()
        vc.tensor_tensor(st[0:1, 1024:1536], st[0:1, 0:512], st[0:1, 0:512], ALU.mult)
        vc.drain()
        P.inc(vc.tensor_tensor(st[0:1, 512:1024], st[0:1, 512:1024], st[0:1, 1024:1536], ALU.subtract), ve, 1)
        P.wait(sc, ve)
        sc.activation(out=st[0:1, 1024:1536], in_=st[0:1, 512:1024], func=AF.Ln, bias=LN_EPS)
        P.inc(sc.activation(out=st[0:1, 1536:2048], in_=st[0:1, 1024:1536], func=AF.Exp, scale=-0.5), ae, 1)
        P.wait(tn, ae)
        tn.matmul(pE[:, 0:512], onesrow_f, st[0:1, 0:512], start=True, stop=True)
        P.inc(tn.matmul(pE[:, 512:1024], onesrow_f, st[0:1, 1536:2048], start=True, stop=True), pe, 1)
        P.wait(vc, pe)
        AFF = 0
        for dc in range(8):
            P.wait(vc, ae, AFF)
            vc.tensor_tensor(a1sc[:, 0:512], s2c[:, dc * 512:(dc + 1) * 512], pE[:, 0:512], ALU.subtract)
            vc.drain()
            vc.tensor_tensor(N("a1sc2"), a1sc[:, 0:512], pE[:, 512:1024], ALU.mult)
            vc.drain()
            P.inc(vc.engine_nop(), ve, 1)
            x2buf = N("a1sc3") if dc % 2 else a1sc[:, 512:1024]
            P.wait(sc, ve)
            P.wait(sc, ld, X2ST[dc % 2])
            sc.activation(out=x2buf, in_=N("a1sc2"), func=AF.Identity,
                          scale=bias_c["g2"][:, dc:dc + 1], bias=bias_c["be2"][:, dc:dc + 1])
            P.inc(sc.activation(out=x2T[:, dc * BT + t5 * 512: dc * BT + (t5 + 1) * 512],
                                in_=N("a1sc2"), func=AF.Identity,
                                scale=bias_c["g2"][:, dc:dc + 1], bias=bias_c["be2"][:, dc:dc + 1]), ae, 1)
            AFF = P.val(ae)
            P.wait(sy, ae)
            P.inc(sy.dma_start(out=x2_d[:, dc * BT + t5 * 512: dc * BT + (t5 + 1) * 512], in_=x2buf), ld, 16)
            X2ST[dc % 2] = P.val(ld)
    P.wait(sy, ve)
    P.inc(sy.dma_start(out=N("xn_own_d"), in_=xnT_own), ld, 16)
    P.wait(sy, ld)
    if E["debug"]:
        P.wait(sy, ae)
        P.inc(sy.dma_start(out=E["dbg"]["qTd"], in_=qT), ld, 16)
        P.inc(sy.dma_start(out=E["dbg"]["dqc"], in_=diagq_c), ld, 16)
        P.inc(sy.dma_start(out=E["dbg"]["attnT"], in_=attnT), ld, 16)
        P.inc(sy.dma_start(out=E["dbg"]["s2c"], in_=s2c), ld, 16)
        P.wait(sy, ld)


def emit_routing(P, E):
    ld, gp, pe, ve, ae = (E["S_ld"], E["S_gp"], E["S_pe"], E["S_ve"], E["S_ae"])
    sy, gs, tn, vc, sc = P.sync, P.gpsimd, P.tensor, P.vector, P.scalar
    pb, pE, pEb = E["pb"], E["pE"], E["pEb"]
    N = lambda k: E[k]
    prob_g, idx_g, fe_g = N("prob_g"), N("idx_g"), N("fe_g")
    rtw, rtw2, rts = N("rtw"), N("rtw2"), N("rts")
    mask_b, tri_i, tri_s = N("mask_b"), N("tri_i"), N("tri_s")
    onesrow_f, onescol_b, id_f, id_b = N("onesrow_f"), N("onescol_b"), N("id_f"), N("id_b")
    a1sb, a1sb2 = N("a1sb"), N("a1sb2")
    sc8 = N("a1sc2")
    sc8b = N("a1sc3")
    fsel = N("a1sc")

    def R(c0, c1):
        return rts[0:1, c0:c1]

    def bcast_row(row, dstcols, tag):
        P.wait(tn, ve)
        P.wait(tn, ae)
        P.inc(tn.matmul(pb[2][:, 32:40], onesrow_f, row, start=True, stop=True), pe, 1)
        P.wait(sc, pe)
        P.inc(sc.activation(out=dstcols, in_=pb[2][:, 32:40], func=AF.Copy), ae, 1)

    def counts_of_blocks(dst):
        vc.tensor_reduce(rtw[:, 1024:1032], rtw2[:, 0:1024].rearrange("p (e c) -> p e c", c=128),
                         axis=AX.X, op=ALU.add)
        vc.drain()
        P.inc(vc.tensor_copy(a1sb[:, 0:8], rtw[:, 1024:1032]), ve, 1)
        P.wait(tn, ve)
        P.wait(tn, ae)
        P.inc(tn.matmul(pb[2][0:1, 40:48], onescol_b, a1sb[:, 0:8], start=True, stop=True), pe, 1)
        P.wait(sc, pe)
        P.inc(sc.activation(out=dst, in_=pb[2][0:1, 40:48], func=AF.Copy), ae, 1)

    P.wait(vc, ae)
    P.wait(vc, ld)
    vc.tensor_scalar(rtw[:, 1024:1088], prob_g[:, 0:64], 1.0, None, ALU.add)
    vc.drain()
    vc.tensor_scalar(rtw2[:, 1024:1088], prob_g[:, 64:128], 1.0, None, ALU.add)
    vc.drain()
    P.inc(vc.engine_nop(), ve, 1)
    for e in range(NE):
        vc.tensor_single_scalar(rtw2[:, e * 128:(e + 1) * 128], idx_g, float(e), ALU.is_equal)
        vc.drain()
    P.inc(vc.engine_nop(), ve, 1)
    for e in range(NE):
        vc.tensor_tensor(rtw[:, e * 128:e * 128 + 64], rtw2[:, e * 128:e * 128 + 64],
                         rtw[:, 1024:1088], ALU.mult)
        vc.drain()
        vc.tensor_tensor(rtw[:, e * 128 + 64:(e + 1) * 128], rtw2[:, e * 128 + 64:(e + 1) * 128],
                         rtw2[:, 1024:1088], ALU.mult)
        vc.drain()
    vc.tensor_scalar(rtw[:, 0:1024], rtw[:, 0:1024], -1.0, None, ALU.add)
    vc.drain()
    P.inc(vc.engine_nop(), ve, 1)
    counts_of_blocks(R(0, 8))

    P.wait(vc, ae)
    vc.memset(R(8, 16), 0.0)
    vc.memset(R(16, 24), 1.01)
    vc.drain()
    P.inc(vc.engine_nop(), ve, 1)
    for it in range(27):
        vc.tensor_tensor(R(24, 32), R(8, 16), R(16, 24), ALU.add)
        vc.drain()
        P.inc(vc.tensor_scalar(R(24, 32), R(24, 32), 0.5, None, ALU.mult), ve, 1)
        bcast_row(R(24, 32), sc8[:, 0:8], "mid")
        P.wait(vc, ae)
        for e in range(NE):
            vc.tensor_scalar(rtw2[:, e * 128:(e + 1) * 128], rtw[:, e * 128:(e + 1) * 128],
                             sc8[:, e:e + 1], None, ALU.is_gt)
            vc.drain()
        P.inc(vc.engine_nop(), ve, 1)
        counts_of_blocks(R(32, 40))
        P.wait(vc, ae)
        vc.tensor_single_scalar(R(40, 48), R(32, 40), float(CAP), ALU.is_gt)
        vc.drain()
        vc.tensor_tensor(R(48, 56), R(24, 32), R(8, 16), ALU.subtract)
        vc.drain()
        vc.tensor_tensor(R(48, 56), R(48, 56), R(40, 48), ALU.mult)
        vc.drain()
        vc.tensor_tensor(R(8, 16), R(8, 16), R(48, 56), ALU.add)
        vc.drain()
        vc.tensor_tensor(R(48, 56), R(16, 24), R(24, 32), ALU.subtract)
        vc.drain()
        vc.tensor_tensor(R(48, 56), R(48, 56), R(40, 48), ALU.mult)
        vc.drain()
        vc.tensor_tensor(R(16, 24), R(24, 32), R(48, 56), ALU.add)
        vc.drain()
        P.inc(vc.engine_nop(), ve, 1)

    bcast_row(R(16, 24), sc8[:, 0:8], "hi")
    P.wait(vc, ae)
    for e in range(NE):
        vc.tensor_scalar(rtw2[:, e * 128:(e + 1) * 128], rtw[:, e * 128:(e + 1) * 128],
                         sc8[:, e:e + 1], None, ALU.is_le)
        vc.drain()
        vc.tensor_scalar(sc8b[:, 0:128], rtw[:, e * 128:(e + 1) * 128], 1.0, None, ALU.add)
        vc.drain()
        vc.tensor_tensor(sc8b[:, 128:256], sc8b[:, 0:128], rtw2[:, e * 128:(e + 1) * 128], ALU.mult)
        vc.drain()
        vc.tensor_scalar(rtw2[:, e * 128:(e + 1) * 128], sc8b[:, 128:256], -1.0, None, ALU.add)
        vc.drain()
    vc.tensor_reduce(rtw[:, 1024:1032], rtw2[:, 0:1024].rearrange("p (e c) -> p e c", c=128),
                     axis=AX.X, op=ALU.max)
    vc.drain()
    P.inc(vc.engine_nop(), ve, 1)
    P.wait(tn, ve)
    P.wait(tn, ae)
    P.inc(tn.transpose(pb[2][0:8, 48:176], rtw[:, 1024:1032], id_f), pe, 1)
    P.wait(vc, pe)
    P.inc(vc.reduce_max(rtw[0:8, 1032:1033], pb[2][0:8, 48:176], axis=AX.X), ve, 1)
    P.wait(tn, ve)
    P.inc(tn.transpose(pb[2][0:1, 0:8], rtw[0:8, 1032:1033], id_f[0:8, 0:8]), pe, 1)
    P.wait(sc, pe)
    P.inc(sc.activation(out=R(56, 64), in_=pb[2][0:1, 0:8], func=AF.Copy), ae, 1)

    bcast_row(R(56, 64), sc8[:, 0:8], "theta")
    P.wait(vc, ae)
    for e in range(NE):
        vc.tensor_scalar(rtw2[:, e * 128:(e + 1) * 128], rtw[:, e * 128:(e + 1) * 128],
                         sc8[:, e:e + 1], None, ALU.is_gt)
        vc.drain()
    P.inc(vc.engine_nop(), ve, 1)
    counts_of_blocks(R(64, 72))
    P.wait(vc, ae)
    vc.tensor_scalar(R(72, 80), R(64, 72), -1.0, float(CAP), ALU.mult, ALU.add)
    vc.tensor_single_scalar(R(80, 88), R(0, 8), float(CAP), ALU.is_le)
    vc.drain()
    P.inc(vc.engine_nop(), ve, 1)
    bcast_row(R(72, 80), sc8[:, 8:16], "quota")
    bcast_row(R(80, 88), sc8[:, 16:24], "cond")

    P.wait(vc, ae)
    vc.memset(fsel[:, 0:128], 0.0)
    vc.drain()
    P.inc(vc.engine_nop(), ve, 1)
    for e in range(NE):
        P.wait(vc, ae)
        vc.tensor_scalar(a1sb2[:, 0:128], rtw[:, e * 128:(e + 1) * 128],
                         sc8[:, e:e + 1], None, ALU.is_equal)
        vc.drain()
        P.inc(vc.engine_nop(), ve, 1)
        P.wait(tn, ve)
        P.wait(tn, ae)
        P.inc(tn.matmul(pb[0][:, 0:128], tri_i, a1sb2[:, 0:128], start=True, stop=True), pe, 1)
        P.inc(tn.matmul(pb[1][0:1, 0:128], onescol_b, a1sb2[:, 0:128], start=True, stop=True), pe, 1)
        P.wait(sc, pe)
        P.inc(sc.activation(out=a1sb2[0:1, 128:256], in_=pb[1][0:1, 0:128], func=AF.Copy), ae, 1)
        P.wait(tn, ae)
        P.inc(tn.transpose(pEb[:, 0:1], a1sb2[0:1, 128:256], id_b[0:1, 0:1]), pe, 1)
        P.wait(vc, pe)
        P.inc(vc.tensor_copy(a1sb2[:, 256:257], pEb[:, 0:1]), ve, 1)
        P.wait(tn, ve)
        P.inc(tn.matmul(pb[1][:, 128:129], tri_s, a1sb2[:, 256:257], start=True, stop=True), pe, 1)
        P.wait(vc, pe)
        P.inc(vc.tensor_copy(a1sb2[:, 257:258], pb[1][:, 128:129]), ve, 1)
        P.wait(tn, ve)
        P.inc(tn.transpose(pEb[0:1, 4:132], a1sb2[:, 257:258], id_b), pe, 1)
        P.wait(sc, pe)
        P.inc(sc.activation(out=a1sb2[0:1, 258:386], in_=pEb[0:1, 4:132], func=AF.Copy), ae, 1)
        P.wait(tn, ae)
        P.inc(tn.matmul(pb[1][:, 256:384], N("onesrow_b"), a1sb2[0:1, 258:386], start=True, stop=True), pe, 1)
        P.wait(vc, pe)
        vc.tensor_copy(sc8b[:, 0:128], pb[0][:, 0:128])
        vc.drain()
        vc.tensor_tensor(sc8b[:, 128:256], sc8b[:, 0:128], pb[1][:, 256:384], ALU.add)
        vc.drain()
        vc.tensor_scalar(sc8b[:, 256:384], sc8b[:, 128:256], sc8[:, 8 + e:8 + e + 1], None, ALU.is_le)
        vc.drain()
        vc.tensor_scalar(sc8b[:, 384:512], rtw[:, e * 128:(e + 1) * 128],
                         sc8[:, e:e + 1], None, ALU.is_equal)
        vc.drain()
        vc.tensor_tensor(sc8b[:, 256:384], sc8b[:, 256:384], sc8b[:, 384:512], ALU.mult)
        vc.drain()
        vc.tensor_tensor(sc8b[:, 256:384], sc8b[:, 256:384], rtw2[:, e * 128:(e + 1) * 128], ALU.add)
        vc.drain()
        vc.tensor_scalar(sc8b[:, 384:512], sc8b[:, 256:384], -1.0, 1.0, ALU.mult, ALU.add)
        vc.drain()
        vc.tensor_scalar(sc8b[:, 384:512], sc8b[:, 384:512], sc8[:, 16 + e:16 + e + 1], None, ALU.mult)
        vc.drain()
        vc.tensor_tensor(sc8b[:, 256:384], sc8b[:, 256:384], sc8b[:, 384:512], ALU.add)
        vc.drain()
        vc.tensor_single_scalar(sc8b[:, 384:512], idx_g, float(e), ALU.is_equal)
        vc.drain()
        vc.tensor_tensor(sc8b[:, 384:512], sc8b[:, 384:512], sc8b[:, 256:384], ALU.mult)
        vc.drain()
        vc.tensor_tensor(fsel[:, 0:128], fsel[:, 0:128], sc8b[:, 384:512], ALU.add)
        vc.drain()
        P.inc(vc.engine_nop(), ve, 1)
    vc.tensor_tensor(fe_g, idx_g, fsel[:, 0:128], ALU.mult)
    vc.drain()
    for e in range(NE):
        vc.tensor_single_scalar(mask_b[:, e * 16:(e + 1) * 16], fe_g[:, 0:16], float(e), ALU.is_equal)
        vc.drain()
    P.inc(vc.engine_nop(), ve, 1)


def emit_a2(P, E):
    ld, gp, pe, ve, ae = (E["S_ld"], E["S_gp"], E["S_pe"], E["S_ve"], E["S_ae"])
    sy, gs, tn, vc, sc = P.sync, P.gpsimd, P.tensor, P.vector, P.scalar
    pb, pE = E["pb"], E["pE"]
    N = lambda k: E[k]
    w1aT, w2aTq, Ha = N("w1aT"), N("w2aTq"), N("Ha")
    x2T, x2f, x3f = N("x2T"), N("x2f"), N("x3f")
    a2sc, a2sb, statB = N("a2sc"), N("a2sb"), N("statB")
    bias_c = N("bias_c")
    onesrow_f, onescol_b = N("onesrow_f"), N("onescol_b")
    x2_d, x3_d, ya_d = N("x2_d"), N("x3_d"), N("ya_d")

    P.wait(sy, ve)
    P.wait(sy, ae)
    for kc in range(8):
        P.inc(sy.dma_start(out=w1aT[:, kc * DFF:(kc + 1) * DFF],
                           in_=E["wb_d"]["w1a"][:, kc * 128:(kc + 1) * 128], transpose=True), ld, 16)
    LD_W1A = P.val(ld)
    P.wait(tn, ld, LD_W1A)

    WQ_LD = {}
    for t5 in range(4):
        HEV = {0: 0, 1: 0}
        for ffc in range(32):
            pbuf = pb[ffc % 2]
            P.wait(tn, ae, HEV[ffc % 2])
            lastm = None
            for kc in range(8):
                lastm = tn.matmul(pbuf, w1aT[:, kc * DFF + ffc * 128: kc * DFF + (ffc + 1) * 128],
                                  x2T[:, kc * BT + t5 * 512: kc * BT + (t5 + 1) * 512],
                                  start=(kc == 0), stop=(kc == 7))
            P.inc(lastm, pe, 1)
            P.wait(sc, pe)
            P.inc(sc.activation(out=Ha[:, ffc * 512:(ffc + 1) * 512], in_=pbuf, func=AF.Relu,
                                bias=bias_c["b1a"][:, ffc:ffc + 1]), ae, 1)
            HEV[ffc % 2] = P.val(ae)
        AE_H = P.val(ae)
        P.wait(tn, ae, AE_H)
        X2L = {}
        VE_X3 = [0]
        STATS_DC = [0]
        for grp in range(2):
            dcs = [grp * 4 + k for k in range(4)]
            ytile = {dcs[0]: pb[0], dcs[1]: pb[1], dcs[2]: pE[:, 0:512], dcs[3]: pE[:, 512:1024]}
            P.wait(tn, ae)
            P.wait(tn, ve)
            for q in range(4):
                P.wait(sy, pe)
                for f2 in range(8):
                    ffc = q * 8 + f2
                    P.inc(sy.dma_start(out=w2aTq[q % 2][:, f2 * 1024:(f2 + 1) * 1024],
                                       in_=E["wb_d"]["w2a"][:, ffc * 128:(ffc + 1) * 128],
                                       transpose=True), ld, 16)
                WQL = P.val(ld)
                P.wait(tn, ld, WQL)
                lastm = None
                for dc in dcs:
                    for f2 in range(8):
                        ffc = q * 8 + f2
                        lastm = tn.matmul(ytile[dc], w2aTq[q % 2][:, f2 * 1024 + dc * 128: f2 * 1024 + (dc + 1) * 128],
                                          Ha[:, ffc * 512:(ffc + 1) * 512],
                                          start=(q == 0 and f2 == 0), stop=False)
                P.inc(lastm, pe, 1)
            for dc in dcs:
                P.wait(sy, ve)
                P.inc(sy.dma_start(out=x2f[dc % 2], in_=x2_d[:, dc * BT + t5 * 512: dc * BT + (t5 + 1) * 512]), ld, 16)
                X2L[dc % 2] = P.val(ld)
                P.wait(sc, pe)
                P.wait(sc, ve, VE_X3[0])
                P.inc(sc.activation(out=a2sc[:, 0:512], in_=ytile[dc], func=AF.Identity,
                                    bias=bias_c["b2a"][:, dc:dc + 1]), ae, 1)
                P.wait(vc, ae)
                P.wait(vc, ld, X2L[dc % 2])
                vc.tensor_tensor(x3f[dc % 2], a2sc[:, 0:512], x2f[dc % 2], ALU.add)
                vc.drain()
                vc.tensor_copy(a2sb[:, 0:512], x3f[dc % 2])
                vc.drain()
                P.inc(vc.engine_nop(), ve, 1)
                VE_X3[0] = P.val(ve)
                P.wait(sy, ve)
                P.inc(sy.dma_start(out=x3_d[:, dc * BT + t5 * 512: dc * BT + (t5 + 1) * 512], in_=x3f[dc % 2]), ld, 16)
                P.wait(tn, ve)
                P.inc(tn.matmul(pE[0:1, 1024:1536], onescol_b, a2sb[:, 0:512],
                                start=(dc == 0), stop=(dc == 7)), pe, 1)
                P.wait(sc, ve)
                P.inc(sc.activation(out=a2sb[:, 512:1024], in_=x3f[dc % 2], func=AF.Square), ae, 1)
                P.wait(tn, ae)
                P.inc(tn.matmul(pE[0:1, 1536:2048], onescol_b, a2sb[:, 512:1024],
                                start=(dc == 0), stop=(dc == 7)), pe, 1)
        P.wait(vc, pe)
        st = statB
        vc.tensor_scalar(st[0:1, 0:512], pE[0:1, 1024:1536], 1.0 / D, None, ALU.mult)
        vc.drain()
        vc.tensor_scalar(st[0:1, 512:1024], pE[0:1, 1536:2048], 1.0 / D, None, ALU.mult)
        vc.drain()
        vc.tensor_tensor(st[0:1, 1024:1536], st[0:1, 0:512], st[0:1, 0:512], ALU.mult)
        vc.drain()
        P.inc(vc.tensor_tensor(st[0:1, 512:1024], st[0:1, 512:1024], st[0:1, 1024:1536], ALU.subtract), ve, 1)
        P.wait(sc, ve)
        sc.activation(out=st[0:1, 1024:1536], in_=st[0:1, 512:1024], func=AF.Ln, bias=LN_EPS)
        P.inc(sc.activation(out=st[0:1, 1536:2048], in_=st[0:1, 1024:1536], func=AF.Exp, scale=-0.5), ae, 1)
        P.wait(tn, ae)
        tn.matmul(pE[:, 0:512], onesrow_f, st[0:1, 0:512], start=True, stop=True)
        P.inc(tn.matmul(pE[:, 512:1024], onesrow_f, st[0:1, 1536:2048], start=True, stop=True), pe, 1)
        AFF2 = 0
        for dc in range(8):
            P.wait(sy, ve)
            P.inc(sy.dma_start(out=x3f[dc % 2], in_=x3_d[:, dc * BT + t5 * 512: dc * BT + (t5 + 1) * 512]), ld, 16)
            P.wait(vc, ld)
            P.wait(vc, pe)
            P.wait(vc, ae, AFF2)
            vc.tensor_tensor(a2sc[:, 0:512], x3f[dc % 2], pE[:, 0:512], ALU.subtract)
            vc.drain()
            vc.tensor_tensor(a2sc[:, 512:1024], a2sc[:, 0:512], pE[:, 512:1024], ALU.mult)
            vc.drain()
            P.inc(vc.engine_nop(), ve, 1)
            P.wait(sc, ve)
            P.inc(sc.activation(out=a2sc[:, 0:512], in_=a2sc[:, 512:1024], func=AF.Identity,
                                scale=bias_c["g3"][:, dc:dc + 1], bias=bias_c["be3"][:, dc:dc + 1]), ae, 1)
            AFF2 = P.val(ae)
            P.wait(sy, ae)
            P.inc(sy.dma_start(out=ya_d[:, dc * BT + t5 * 512: dc * BT + (t5 + 1) * 512], in_=a2sc[:, 0:512]), ld, 16)
    P.wait(sy, ld)


def emit_moe(P, E):
    ld, gp, pe, ve, ae = (E["S_ld"], E["S_gp"], E["S_pe"], E["S_ve"], E["S_ae"])
    sy, gs, tn, vc, sc = P.sync, P.gpsimd, P.tensor, P.vector, P.scalar
    pb, pE = E["pb"], E["pE"]
    N = lambda k: E[k]
    w1sb, w2sb, Hm = N("w1sb"), N("w2sb"), N("Hm")
    xnm, yst, zt = N("xnm"), N("yst"), N("zt")
    b1m_sb, b2m_sb = N("b1m_sb"), N("b2m_sb")
    mask_b = N("mask_b")
    xn_own_d, ym_d = N("xn_own_d"), N("ym_d")
    onesrow_b = N("onesrow_b")

    P.wait(gs, pe)
    P.wait(gs, ve)
    P.wait(gs, ae)
    P.wait(gs, ld)
    P.wait(vc, ae)
    P.inc(vc.memset(zt, 0.0), ve, 1)
    P.wait(sy, ve)
    for r in range(16):
        P.inc(sy.dma_start(out=ym_d[r * 128:(r + 1) * 128, 0:512], in_=zt), ld, 16)
        P.inc(sy.dma_start(out=ym_d[r * 128:(r + 1) * 128, 512:1024], in_=zt), ld, 16)
    ZRO = P.val(ld)
    P.wait(gs, ld, ZRO)
    P.inc(gs.dma_start(out=w1sb.rearrange("p (a f) -> p a f", f=DFF), in_=E["w1m_in"][0:D, :].rearrange("(a p) f -> p a f", p=128)), gp, 16)
    P.inc(gs.dma_start(out=w2sb.rearrange("p (a f) -> p a f", f=D), in_=E["w2m_in"][0:DFF, :].rearrange("(a p) f -> p a f", p=128)), gp, 16)
    P.inc(gs.dma_start(out=b1m_sb[0], in_=E["b1m_in"][0:128, :]), gp, 16)
    P.inc(gs.dma_start(out=b2m_sb[0], in_=E["b2m_in"][0:1, :]), gp, 16)
    W_RDY = {0: P.val(gp)}
    XN_LD = {}
    Y_ST = {0: 0, 1: 0}
    Y_DMA = {0: 0, 1: 0}
    for e in range(NE):
        eb = e % 2
        P.wait(tn, gp, W_RDY[e])
        for t5 in range(4):
            i = t5 % 2
            P.wait(sy, ve)
            P.inc(sy.dma_start(out=xnm[i],
                               in_=xn_own_d.rearrange("p (k t) -> p k t", t=BT)[:, :, t5 * 512:(t5 + 1) * 512]),
                  ld, 16)
            XN_LD[i] = P.val(ld)
            HEV = {0: 0, 1: 0}
            P.wait(tn, ld, XN_LD[i])
            for ffc in range(32):
                pbuf = pb[ffc % 2]
                P.wait(tn, ae, HEV[ffc % 2])
                lastm = None
                for kc in range(8):
                    lastm = tn.matmul(pbuf, w1sb[:, kc * DFF + ffc * 128: kc * DFF + (ffc + 1) * 128],
                                      xnm[i][:, kc * 512:(kc + 1) * 512],
                                      start=(kc == 0), stop=(kc == 7))
                P.inc(lastm, pe, 1)
                P.wait(sc, pe)
                P.inc(sc.activation(out=Hm[:, ffc * 512:(ffc + 1) * 512], in_=pbuf, func=AF.Relu,
                                    bias=b1m_sb[eb][:, ffc:ffc + 1]), ae, 1)
                HEV[ffc % 2] = P.val(ae)
            if e + 1 < NE and t5 == 3:
                P.wait(gs, ae)
                P.inc(gs.dma_start(out=w1sb.rearrange("p (a f) -> p a f", f=DFF),
                                   in_=E["w1m_in"][(e + 1) * D:(e + 2) * D, :].rearrange(
                    "(a p) f -> p a f", p=128)), gp, 16)
                P.inc(gs.dma_start(out=b1m_sb[1 - eb], in_=E["b1m_in"][(e + 1) * 128:(e + 2) * 128, :]), gp, 16)
            P.wait(tn, ae)
            for t1 in range(4):
                for dh in range(2):
                    j = (t1 * 2 + dh) % 2
                    pbuf = pb[j]
                    P.wait(tn, ve, Y_ST[j])
                    lastm = None
                    for ffc in range(32):
                        lastm = tn.matmul(pbuf, Hm[:, ffc * 512 + t1 * 128: ffc * 512 + (t1 + 1) * 128],
                                          w2sb[:, ffc * 1024 + dh * 512: ffc * 1024 + (dh + 1) * 512],
                                          start=(ffc == 0), stop=False)
                    lastm = tn.matmul(pbuf, onesrow_b, b2m_sb[eb][0:1, dh * 512:(dh + 1) * 512],
                                      start=False, stop=True)
                    P.inc(lastm, pe, 1)
                    tcg = t5 * 4 + t1
                    P.wait(vc, pe)
                    P.wait(vc, gp, Y_DMA[j])
                    P.inc(vc.tensor_scalar(yst[j], pbuf, mask_b[:, e * 16 + tcg:e * 16 + tcg + 1],
                                           None, ALU.mult), ve, 1)
                    Y_ST[j] = P.val(ve)
                    P.wait(gs, ve, Y_ST[j])
                    P.inc(gs.dma_start(out=ym_d[tcg * 128:(tcg + 1) * 128, dh * 512:(dh + 1) * 512],
                                       in_=yst[j], accum_op=ALU.add), gp, 16)
                    Y_DMA[j] = P.val(gp)
        if e + 1 < NE:
            P.wait(gs, pe)
            P.inc(gs.dma_start(out=w2sb.rearrange("p (a f) -> p a f", f=D),
                               in_=E["w2m_in"][(e + 1) * DFF:(e + 2) * DFF, :].rearrange(
                "(a p) f -> p a f", p=128)), gp, 16)
            P.inc(gs.dma_start(out=b2m_sb[1 - eb], in_=E["b2m_in"][e + 1:e + 2, :]), gp, 16)
            W_RDY[e + 1] = P.val(gp)
    P.wait(gs, gp)


def emit_fdebug(P, E):
    ld, ve = E["S_ld"], E["S_ve"]
    sy = P.sync
    dbg = E["dbg"]
    P.wait(sy, ve)
    P.inc(sy.dma_start(out=dbg["x2"], in_=E["x2_d"]), ld, 16)
    P.inc(sy.dma_start(out=dbg["x3"], in_=E["x3_d"]), ld, 16)
    P.inc(sy.dma_start(out=dbg["ya"], in_=E["ya_d"]), ld, 16)
    P.inc(sy.dma_start(out=dbg["ym"], in_=E["ym_d"]), ld, 16)
    P.inc(sy.dma_start(out=dbg["fe"], in_=E["fe_g"]), ld, 16)
    P.inc(sy.dma_start(out=dbg["xnd"], in_=E["xn_own_d"]), ld, 16)
    P.inc(sy.dma_start(out=dbg["maskb"], in_=E["mask_b"]), ld, 16)
    P.wait(sy, ld)


def emit_final(P, E):
    ld, gp, pe, ve, ae = (E["S_ld"], E["S_gp"], E["S_pe"], E["S_ve"], E["S_ae"])
    sy, gs, tn, vc, sc = P.sync, P.gpsimd, P.tensor, P.vector, P.scalar
    pb = E["pb"]
    N = lambda k: E[k]
    fx, fya, fym, fout = N("fx"), N("fya"), N("fym"), N("fout")
    x_in, ya_d, ym_d, out = N("x_in"), N("ya_d"), N("ym_d"), N("out")
    id_f = N("id_f")

    P.wait(sy, gp)
    P.wait(sy, ve)
    FOUT = {0: 0, 1: 0}
    for t1 in range(16):
        i = t1 % 2
        P.wait(sy, ld, FOUT[i])
        P.inc(sy.dma_start(out=fx[i], in_=x_in[t1 * 128:(t1 + 1) * 128, :]), ld, 16)
        P.inc(sy.dma_start(out=fym[i], in_=ym_d[t1 * 128:(t1 + 1) * 128, :]), ld, 16)
        P.inc(sy.dma_start(out=fya[i],
                           in_=ya_d.rearrange("p (k t) -> p k t", t=BT)[:, :, t1 * 128:(t1 + 1) * 128]), ld, 16)
        LD_F = P.val(ld)
        P.wait(tn, ld, LD_F)
        P.wait(tn, ve)
        for dc in range(8):
            P.wait(tn, ve)
            P.inc(tn.transpose(pb[0][:, 0:128] if dc % 2 == 0 else pb[1][:, 0:128],
                               fya[i][:, dc * 128:(dc + 1) * 128], id_f), pe, 1)
            P.wait(vc, pe)
            vc.tensor_tensor(fout[i][:, dc * 128:(dc + 1) * 128], fx[i][:, dc * 128:(dc + 1) * 128],
                             fym[i][:, dc * 128:(dc + 1) * 128], ALU.add)
            vc.drain()
            vc.tensor_tensor(fout[i][:, dc * 128:(dc + 1) * 128], fout[i][:, dc * 128:(dc + 1) * 128],
                             pb[0][:, 0:128] if dc % 2 == 0 else pb[1][:, 0:128], ALU.add)
            vc.drain()
            P.inc(vc.engine_nop(), ve, 1)
        P.wait(sy, ve)
        P.inc(sy.dma_start(out=out[t1 * 128:(t1 + 1) * 128, :], in_=fout[i]), ld, 16)
        FOUT[i] = P.val(ld)
    P.wait(sy, ld)
    P.wait(gs, ld)


_BUILD_CACHE = {}


def _get_program(debug=False):
    key = bool(debug)
    if key not in _BUILD_CACHE:
        _BUILD_CACHE[key] = build_program(debug=debug)
    return _BUILD_CACHE[key]


def _col_layout(b, nch):
    return np.ascontiguousarray(np.asarray(b, np.float32).reshape(nch, 128).T)


def make_inputs(inputs, debug=False):
    x = np.ascontiguousarray(np.asarray(inputs["L0_attn_video"], np.float32).reshape(T, D))
    ident = np.eye(128, dtype=np.float32)
    identb = np.eye(128, dtype=BFNP)
    iotam = np.tile(np.arange(NE, dtype=np.float32) - BIG, (128, 1))
    onescb = np.ones((128, 1), BFNP)
    onescf = np.ones((128, 1), np.float32)
    onesrb = np.ones((1, 128), BFNP)
    onesrf = np.ones((1, 128), np.float32)
    halves = np.zeros((2, 128), np.float32)
    halves[0, 0:64] = 1.0
    halves[1, 64:128] = 1.0
    indh = np.zeros((128, 128), np.float32)
    for dc in range(8):
        for h16 in range(16):
            if h16 == 2 * dc:
                indh[0:64, dc * 16 + h16] = 0.5 * NORM * NORM
            elif h16 == 2 * dc + 1:
                indh[64:128, dc * 16 + h16] = 0.5 * NORM * NORM
    indh = indh.astype(BFNP)
    k = np.arange(128)
    tri_i = (k[:, None] <= k[None, :]).astype(BFNP)
    tri_s = (k[:, None] < k[None, :]).astype(BFNP)
    salt = int(np.random.randint(1 << 20, 1 << 30))
    salts = (salt + np.arange(64, dtype=np.int64)).astype(np.int32).reshape(1, 64)

    base = {
        "wq": np.asarray(inputs["L0_attn_Wq"], np.float32),
        "wk": np.asarray(inputs["L0_attn_Wk"], np.float32),
        "wv": np.asarray(inputs["L0_attn_Wv"], np.float32),
        "wo": np.asarray(inputs["L0_attn_Wo"], np.float32),
        "w1a": np.asarray(inputs["L0_attn_W1"], np.float32),
        "w2a": np.asarray(inputs["L0_attn_W2"], np.float32),
        "bq": _col_layout(inputs["L0_attn_bq"], 8),
        "bk": _col_layout(inputs["L0_attn_bk"], 8),
        "bo": _col_layout(inputs["L0_attn_bo"], 8),
        "b2a": _col_layout(inputs["L0_attn_b2"], 8),
        "g2": _col_layout(inputs["L0_attn_g2"], 8),
        "be2": _col_layout(inputs["L0_attn_be2"], 8),
        "g3": _col_layout(inputs["L0_attn_g3"], 8),
        "be3": _col_layout(inputs["L0_attn_be3"], 8),
        "b1a": _col_layout(inputs["L0_attn_b1"], 32),
        "bv_row": np.asarray(inputs["L0_attn_bv"], np.float32).reshape(1, D),
        "proj_in": np.asarray(inputs["L0_attn_proj"], np.float32),
        "rw_in": np.asarray(inputs["L0_ffn_router_w"], np.float32),
        "w1m": np.ascontiguousarray(np.asarray(inputs["L0_ffn_W1"], np.float32).reshape(NE * D, DFF)),
        "b1m": np.concatenate([_col_layout(np.asarray(inputs["L0_ffn_b1"], np.float32)[e], 32)
                               for e in range(NE)], axis=0),
        "w2m": np.ascontiguousarray(np.asarray(inputs["L0_ffn_W2"], np.float32).reshape(NE * DFF, D)),
        "b2m": np.asarray(inputs["L0_ffn_b2"], np.float32),
        "ident_in": ident, "identb_in": identb, "iotam_in": iotam,
        "onescb_in": onescb, "onescf_in": onescf, "onesrb_in": onesrb,
        "onesrf_in": onesrf, "halves_in": halves, "indh_in": indh,
        "tri_i_in": tri_i, "tri_s_in": tri_s, "salts_in": salts,
    }
    in_maps = []
    for c in range(NCORES):
        slots = [(c + 2 * j) % 8 for j in range(NSLOT)]
        xi = np.concatenate([x[s * BT:(s + 1) * BT] for s in slots], axis=0)
        m = dict(base)
        m["x_in"] = np.ascontiguousarray(xi)
        m["parity_in"] = np.full((1, 1), c % 2, np.int32)
        in_maps.append(m)
    return in_maps


def kernel(**inputs):
    nc, dbg = _get_program(debug=False)
    in_maps = make_inputs(inputs)
    res = run_bass_kernel_spmd(nc, in_maps, core_ids=list(range(NCORES)))
    outp = np.empty((T, D), np.float32)
    for c in range(NCORES):
        outp[c * BT:(c + 1) * BT] = np.asarray(res.results[c]["out"], np.float32)
    return outp.reshape(1, T, D)


if __name__ == "__main__":
    nc, dbg = build_program(debug=False)
    n = sum(1 for _ in nc.all_instructions())
    print("built ok, instructions:", n)


# version 06
# speedup vs baseline: 1.1546x; 1.1546x over previous
import contextlib
import numpy as np
import ml_dtypes
import concourse.bass as bass
import concourse.mybir as mybir
from concourse.bass_primitives import MemorySpace
from concourse.bass_utils import run_bass_kernel_spmd

F32 = mybir.dt.float32
BF16 = mybir.dt.bfloat16
I32 = mybir.dt.int32
U8 = mybir.dt.uint8
AF = mybir.ActivationFunctionType
ALU = mybir.AluOpType
AX = mybir.AxisListType
BFNP = ml_dtypes.bfloat16

T, D, HEADS, DH, M, DFF, NE = 16384, 1024, 16, 64, 256, 4096, 8
BT = 2048
HB = 1024
NSLOT = 4
CAP = 2560
EPS_K = 1e-4
LN_EPS = 1e-5
NORM = float(DH) ** -0.25
RATIO = float(M) ** -0.5
LNR = float(np.log(RATIO))
BIG = 1.0e4
KB = 1024
NCORES = 8
ENGINES = ("sync", "gpsimd", "tensor", "vector", "scalar")


class _Dummy:
    def then_inc(self, *a, **k):
        return self


_DUMMY = _Dummy()


class EngProxy:

    def __init__(self, P, role):
        self._P = P
        self._role = role

    @property
    def live(self):
        return self._P.active == self._role

    def __getattr__(self, name):
        P = self._P
        if P.active == self._role:
            return getattr(P.real, name)

        def _skip(*a, **k):
            return _DUMMY

        return _skip


class Prog:

    def __init__(self, active, real, sems):
        self.active = active
        self.real = real
        self.sems = sems
        self.cnt = {}
        for r in ENGINES:
            setattr(self, r, EngProxy(self, r))

    def inc(self, instr, sem, n):
        if instr is not _DUMMY:
            instr.then_inc(sem, n)
        self.cnt[sem.num] = self.cnt.get(sem.num, 0) + n

    def bump(self, sem, n):
        self.cnt[sem.num] = self.cnt.get(sem.num, 0) + n

    def val(self, sem):
        return self.cnt.get(sem.num, 0)

    def wait(self, proxy, sem, v=None):
        if proxy.live:
            self._wait_raw(sem, self.val(sem) if v is None else v)

    def _wait_raw(self, sem, v):
        self.real.wait_ge(sem, v)


def build_program(debug=False):
    nc = bass.Bass("TRN2", target_bir_lowering=False, debug=False)

    x_in = nc.dram_tensor("x_in", [NSLOT * BT, D], F32, kind="ExternalInput").ap()
    W_in = {nm: nc.dram_tensor(nm, shp, F32, kind="ExternalInput").ap()
            for nm, shp in [("wq", [D, D]), ("wk", [D, D]), ("wv", [D, D]),
                            ("wo", [D, D]), ("w1a", [DFF, D]), ("w2a", [D, DFF])]}
    BCsh = {"bq": 8, "bk": 8, "bo": 8, "b2a": 8, "g2": 8, "be2": 8,
            "g3": 8, "be3": 8, "b1a": 32}
    BC = {nm: nc.dram_tensor(nm, [128, n], F32, kind="ExternalInput").ap()
          for nm, n in BCsh.items()}
    bv_row = nc.dram_tensor("bv_row", [1, D], F32, kind="ExternalInput").ap()
    proj_in = nc.dram_tensor("proj_in", [M, DH], F32, kind="ExternalInput").ap()
    rw_in = nc.dram_tensor("rw_in", [NE, D], F32, kind="ExternalInput").ap()
    w1m_in = nc.dram_tensor("w1m", [NE * D, DFF], F32, kind="ExternalInput").ap()
    b1m_in = nc.dram_tensor("b1m", [NE * 128, 32], F32, kind="ExternalInput").ap()
    w2m_in = nc.dram_tensor("w2m", [NE * DFF, D], F32, kind="ExternalInput").ap()
    b2m_in = nc.dram_tensor("b2m", [NE, D], F32, kind="ExternalInput").ap()
    ident_in = nc.dram_tensor("ident_in", [128, 128], F32, kind="ExternalInput").ap()
    identb_in = nc.dram_tensor("identb_in", [128, 128], BF16, kind="ExternalInput").ap()
    iotam_in = nc.dram_tensor("iotam_in", [128, NE], F32, kind="ExternalInput").ap()
    iotam4_in = nc.dram_tensor("iotam4_in", [128, 4 * NE], F32, kind="ExternalInput").ap()
    onescb_in = nc.dram_tensor("onescb_in", [128, 1], BF16, kind="ExternalInput").ap()
    onescf_in = nc.dram_tensor("onescf_in", [128, 1], F32, kind="ExternalInput").ap()
    onesrb_in = nc.dram_tensor("onesrb_in", [1, 128], BF16, kind="ExternalInput").ap()
    onesrf_in = nc.dram_tensor("onesrf_in", [1, 128], F32, kind="ExternalInput").ap()
    indh_in = nc.dram_tensor("indh_in", [128, 128], BF16, kind="ExternalInput").ap()
    tri_i_in = nc.dram_tensor("tri_i_in", [128, 128], BF16, kind="ExternalInput").ap()
    tri_s_in = nc.dram_tensor("tri_s_in", [128, 128], BF16, kind="ExternalInput").ap()
    parity_in = nc.dram_tensor("parity_in", [1, 1], I32, kind="ExternalInput").ap()
    salts_in = nc.dram_tensor("salts_in", [1, 64], I32, kind="ExternalInput").ap()

    out = nc.dram_tensor("out", [BT, D], F32, kind="ExternalOutput").ap()

    dbg = {}

    def dbg_t(name, shape, dt=F32):
        if debug:
            dbg[name] = nc.dram_tensor("dbg_" + name, shape, dt, kind="ExternalOutput").ap()
            return dbg[name]
        return None

    if debug:
        dbg_t("xnT0", [128, 8 * BT], BF16)
        dbg_t("prob", [128, 128])
        dbg_t("idx", [128, 128])
        dbg_t("stab", [1, HEADS])
        dbg_t("ES", [128, 2080])
        dbg_t("vsum", [128, 8])
        dbg_t("ctx", [128, HEADS * 2 * DH], BF16)
        dbg_t("ksum", [128, 2 * HEADS], BF16)
        dbg_t("fe", [128, 128], BF16)
        dbg_t("x2", [128, 8 * BT])
        dbg_t("x3", [128, 8 * BT])
        dbg_t("ya", [128, 8 * BT])
        dbg_t("ym", [BT, D], BF16)
        dbg_t("qTd", [128, 8 * BT], BF16)
        dbg_t("dqc", [128, 16 * HEADS])
        dbg_t("attnT", [128, 8 * 512], BF16)
        dbg_t("s2c", [128, 8 * 512])
        dbg_t("xnd", [128, 8 * BT], BF16)
        dbg_t("maskb", [128, NE * 16])
        dbg_t("xnT0e", [128, 8 * BT], BF16)
        for _s in (1, 2, 3):
            dbg_t(f"xnT0s{_s}", [128, 8 * BT], BF16)
        dbg_t("xnT0m", [128, 8 * BT], BF16)
        dbg_t("diagc", [128, 8 * HEADS])
        dbg_t("dstab", [128, 8 * HEADS])
        dbg_t("ESpre", [128, 2080])
        dbg_t("stabpre", [1, HEADS])

    xhi_d = nc.dram_tensor("xhi_d", [NSLOT * BT, D], BF16).ap()
    xlo_d = nc.dram_tensor("xlo_d", [NSLOT * BT, D], BF16).ap()
    wb_d = {nm: nc.dram_tensor(nm + "b_d", list(W_in[nm].shape), BF16).ap() for nm in W_in}
    x2_d = nc.dram_tensor("x2_d", [128, 8 * BT], F32).ap()
    ya_d = nc.dram_tensor("ya_d", [128, 8 * BT], F32).ap()
    ym_d = nc.dram_tensor("ym_d", [BT, D], BF16).ap()
    xn_own_d = nc.dram_tensor("xn_own_d", [128, 8 * BT], BF16).ap()
    vsum_d = nc.dram_tensor("vsum_d", [1, 1024], F32).ap()
    x3_d = nc.dram_tensor("x3_d", [128, 8 * BT], F32).ap()
    exch_a = nc.dram_tensor("exch_a", [128, 2304], F32, addr_space="Shared").ap()
    exch_b = nc.dram_tensor("exch_b", [128, 2304], F32, addr_space="Shared").ap()
    flag_a = nc.dram_tensor("flag_a", [1, 1], I32, addr_space="Shared").ap()
    flag_b = nc.dram_tensor("flag_b", [1, 1], I32, addr_space="Shared").ap()

    def A(name, shape, dt):
        return nc.alloc_sbuf_tensor(name, shape, dt).ap()

    id_f = A("id_f", [128, 128], F32)
    id_b = A("id_b", [128, 128], BF16)
    iotam = A("iotam", [128, NE], F32)
    iotam4 = A("iotam4", [128, 4 * NE], F32)
    onescol_b = A("onescol_b", [128, 1], BF16)
    onescol_f = A("onescol_f", [128, 1], F32)
    onesrow_b = A("onesrow_b", [1, 128], BF16)
    onesrow_f = A("onesrow_f", [1, 128], F32)
    halves = A("halves", [2, 128], F32)
    indh = A("indh", [128, 128], BF16)
    tri_i = A("tri_i", [128, 128], BF16)
    tri_s = A("tri_s", [128, 128], BF16)
    projT = A("projT", [128, M], BF16)
    rwhT = A("rwhT", [128, 64], BF16)
    rwlT = A("rwlT", [128, 64], BF16)
    s1col = A("s1col", [NE, 1], F32)
    bias_c = {nm: A(nm + "_c", [128, BCsh[nm]], F32) for nm in BCsh}
    par_sb = A("par_sb", [1, 1], I32)
    salt_sb = A("salt_sb", [1, 64], I32)
    flg_sb = A("flg_sb", [1, 1], I32)
    ksum_col = A("ksum_col", [128, 2 * HEADS], BF16)
    epsrow = A("epsrow", [1, 512], BF16)
    FCONSTS = [LN_EPS, -0.5, NORM, LNR, RATIO * EPS_K, -1.0]
    cvals = A("cvals", [128, len(FCONSTS)], F32)
    for _i, _v in enumerate(FCONSTS):
        nc.const_aps.register(F32, float(_v), cvals[:, _i:_i + 1])

    PBASE = (nc.bytes_allocated(MemorySpace.SBUF) + 127) // 128 * 128
    LIMIT = 229376

    def AT(name, shape, dt, off):
        esz = {F32: 4, BF16: 2, U8: 1, I32: 4}[dt]
        szb = int(np.prod(shape[1:])) * esz
        assert PBASE + off + szb <= LIMIT, (name, PBASE, off, szb, PBASE + off + szb)
        return nc.alloc_sbuf_tensor_at(name, shape, dt, offset=PBASE + off).ap()

    def mkcursor(start):
        state = {"o": start}

        def alloc(name, shape, dt):
            esz = {F32: 4, BF16: 2, U8: 1, I32: 4}[dt]
            szb = (int(np.prod(shape[1:])) * esz + 127) // 128 * 128
            t = AT(name, shape, dt, state["o"])
            state["o"] += szb
            return t

        return alloc, state

    ES = AT("ES", [128, 2080], F32, 160 * KB)
    prob_g = AT("prob_g", [128, 128], F32, 168 * KB + 256)
    idx_g = AT("idx_g", [128, 128], F32, 168 * KB + 768)
    x2T = AT("x2T", [128, 8 * BT], BF16, 169 * KB + 256)
    fe_g = AT("fe_g", [128, 128], BF16, 201 * KB + 256)
    mask_b = AT("mask_b", [128, NE * 16], F32, 201 * KB + 512)
    xnT_own = AT("xnT_own", [128, 8 * BT], BF16, 0)

    goff = {}

    def ga(name, shape, dt):
        goff[name] = gst["o"]
        return _ga(name, shape, dt)

    _ga, gst = mkcursor(32 * KB)
    kT_b = ga("kT_b", [128, 8 * HB], BF16)
    xnT_s = ga("xnT_s", [128, 8 * 512], BF16)
    v_b = ga("v_b", [128, 8 * HEADS * 65], BF16)
    wkT = ga("wkT", [128, 8 * 1024], BF16)
    wvT = ga("wvT", [128, 8 * 1024], BF16)
    xq_g = [ga(f"xq_g{i}", [128, D], F32) for i in range(2)]
    hiT_s = ga("hiT_s", [128, 8 * 512], BF16)
    loT_s = ga("loT_s", [128, 8 * 512], BF16)
    xlo_st = ga("xlo_st", [128, 4 * D], BF16)
    diag_c = ga("diag_c", [128, 8 * HEADS], F32)
    dstab = ga("dstab", [128, 8 * HEADS], F32)
    gsc = ga("gsc", [128, 1536], F32)
    gsb = ga("gsb", [128, 1024], BF16)
    stat_c = ga("stat_c", [128, 12], F32)
    stat_r = ga("stat_r", [1, 512], BF16)
    mu_rf = ga("mu_rf", [1, 512], F32)
    maxc = ga("maxc", [128, HEADS], F32)
    stabc = ga("stabc", [128, HEADS], F32)
    p_sc = ga("p_sc", [128, 520], F32)
    srow = ga("srow", [1, 8 * HEADS], F32)
    stab_run = ga("stab_run", [1, HEADS], F32)
    vsum_sb = ga("vsum_sb", [128, 8], F32)
    bv_bc = ga("bv_bc", [128, D], F32)
    v_c = ga("v_c", [128, D], BF16)
    rw_sb = AT("rw_sb", [NE, D], F32, goff["kT_b"])
    rwh_sb = AT("rwh_sb", [NE, D], BF16, goff["kT_b"] + 4 * KB)
    rwl_sb = AT("rwl_sb", [NE, D], BF16, goff["kT_b"] + 6 * KB)
    kp_t = [AT(f"kp_t{i}", [128, M], BF16, goff["xq_g0"] + 512 * i) for i in range(2)]
    k2_t = AT("k2_t", [128, 512], BF16, goff["xq_g0"] + 1024)
    diag_r = AT("diag_r", [16, 512], F32, goff["gsb"])
    vsr_sb = AT("vsr_sb", [1, 1024], F32, goff["xq_g1"])
    exch_w = AT("exch_w", [128, 2304], F32, goff["diag_c"])
    exch_w = AT("exch_w2", [128, 2304], F32, goff["hiT_s"])
    assert gst["o"] <= 160 * KB, gst["o"]

    aoff = {}
    _aa, ast = mkcursor(32 * KB)

    def aa(name, shape, dt):
        aoff[name] = ast["o"]
        return _aa(name, shape, dt)

    qT = aa("qT", [128, 8 * BT], BF16)
    wqT = aa("wqT", [128, 8 * 1024], BF16)
    woT = aa("woT", [128, 8 * 1024], BF16)
    ctx_sb = aa("ctx_sb", [128, HEADS * 2 * DH], BF16)
    qp_t = [aa(f"qp_t{i}", [128, M], BF16) for i in range(2)]
    q2_t = aa("q2_t", [128, 512], BF16)
    diagq_r = aa("diagq_r", [16, 512], F32)
    diagq_c = aa("diagq_c", [128, 16 * HEADS], F32)
    attnT = aa("attnT", [128, 8 * 512], BF16)
    s2c = aa("s2c", [128, 8 * 512], F32)
    a1sc = aa("a1sc", [128, 1024], F32)
    a1sb = aa("a1sb", [128, 1024], BF16)
    a1sb2 = aa("a1sb2", [128, 512], BF16)
    a1sc2 = aa("a1sc2", [128, 512], F32)
    a1sc3 = aa("a1sc3", [128, 512], F32)
    statA = aa("statA", [1, 4 * 512], F32)
    rtw = aa("rtw", [128, 1088], F32)
    rtw2 = aa("rtw2", [128, 1088], F32)
    rts = aa("rts", [1, 256], F32)
    ecol_b = aa("ecol_b", [1, 1024], BF16)
    kse_b = aa("kse_b", [1, 16], BF16)
    kse2_b = aa("kse2_b", [1, 32], BF16)
    qp4 = AT("qp4", [128, 1024], BF16, aoff["qp_t0"])
    assert ast["o"] <= 160 * KB, ast["o"]

    ba, bst = mkcursor(0)
    w1aT = ba("w1aT", [128, 8 * DFF], BF16)
    w2aTq = [ba(f"w2aTq{i}", [128, 8 * 1024], BF16) for i in range(2)]
    Ha = ba("Ha", [128, 32 * 512], BF16)
    x2f = [ba(f"x2f{i}", [128, 512], F32) for i in range(2)]
    x3f = [ba(f"x3f{i}", [128, 512], F32) for i in range(2)]
    a2sc = ba("a2sc", [128, 1024], F32)
    a2sb = ba("a2sb", [128, 1024], BF16)
    statB = ba("statB", [1, 4 * 512], F32)
    assert bst["o"] <= 160 * KB, bst["o"]

    ma, mst = mkcursor(0)
    w1sb = ma("w1sb", [128, 8 * DFF], BF16)
    w2sb = ma("w2sb", [128, 32 * 1024], BF16)
    Hm = ma("Hm", [128, 32 * 512], BF16)
    xnm = [ma(f"xnm{i}", [128, 8 * 512], BF16) for i in range(2)]
    yst = [ma(f"yst{i}", [128, 512], BF16) for i in range(2)]
    b1m_sb = [ma(f"b1m_sb{i}", [128, 32], F32) for i in range(2)]
    b2m_sb = [ma(f"b2m_sb{i}", [1, D], BF16) for i in range(2)]
    zt = ma("zt", [128, 512], BF16)
    assert mst["o"] <= 201 * KB, mst["o"]

    fa, fst = mkcursor(0)
    fx = [fa(f"fx{i}", [128, D], F32) for i in range(2)]
    fya = [fa(f"fya{i}", [128, 8 * 128], F32) for i in range(2)]
    fym = [fa(f"fym{i}", [128, D], BF16) for i in range(2)]
    fout = [fa(f"fout{i}", [128, D], F32) for i in range(2)]
    assert fst["o"] <= 32 * KB

    pb = [nc.alloc_psum_tensor(f"pb{i}", [128, 512], F32).ap() for i in range(3)]
    pE = nc.alloc_psum_tensor("pE", [128, 2560], F32).ap()
    pEb = pE.bitcast(BF16)

    def pES(i):
        b, j = divmod(i, 7)
        return pE[:, b * 512 + j * 65: b * 512 + j * 65 + 65]

    PES_VSUM = pE[:, 4 * 512 + 4 * 65: 4 * 512 + 4 * 65 + 8]
    PES_MU = pE[:, 2048:2560]

    stack = contextlib.ExitStack()
    S_ld = stack.enter_context(nc.semaphore("s_ld"))
    S_gp = stack.enter_context(nc.semaphore("s_gp"))
    S_pe = stack.enter_context(nc.semaphore("s_pe"))
    S_ve = stack.enter_context(nc.semaphore("s_ve"))
    S_ae = stack.enter_context(nc.semaphore("s_ae"))
    S_sp = stack.enter_context(nc.semaphore("s_sp"))
    blk = stack.enter_context(nc.Block())

    env = dict(locals())

    def run_replay(role, real):
        P = Prog(role, real, None)
        emit_program(P, env)

    @blk.sync
    def _(e):
        run_replay("sync", e)

    @blk.gpsimd
    def _(e):
        run_replay("gpsimd", e)

    @blk.tensor
    def _(e):
        run_replay("tensor", e)

    @blk.vector
    def _(e):
        run_replay("vector", e)

    @blk.scalar
    def _(e):
        run_replay("scalar", e)

    stack.close()
    return nc, dbg


class _Scope:

    def __init__(self, E, name):
        self.E, self.name = E, name
        self.sid, _ = E["nc"].enter_named_scope(name, False)

    def end(self):
        self.E["nc"].leave_named_scope(self.name, self.sid, False)


def emit_program(P, E):
    ld, gp, pe, ve, ae, sp = (E["S_ld"], E["S_gp"], E["S_pe"], E["S_ve"],
                              E["S_ae"], E["S_sp"])
    sy, gs, tn, vc, sc = P.sync, P.gpsimd, P.tensor, P.vector, P.scalar
    debug = E["debug"]
    dbg = E["dbg"]
    pb, pE, pEb, pES, PES_VSUM = E["pb"], E["pE"], E["pEb"], E["pES"], E["PES_VSUM"]
    N = lambda k: E[k]
    _s0 = _Scope(E, "Pinit")

    for src, dst in [("ident_in", "id_f"), ("identb_in", "id_b"),
                     ("iotam_in", "iotam"), ("iotam4_in", "iotam4"),
                     ("onescb_in", "onescol_b"),
                     ("onescf_in", "onescol_f"), ("onesrb_in", "onesrow_b"),
                     ("onesrf_in", "onesrow_f"), ("indh_in", "indh"),
                     ("tri_i_in", "tri_i"), ("tri_s_in", "tri_s"),
                     ("parity_in", "par_sb"), ("salts_in", "salt_sb"),
                     ("rw_in", "rw_sb")]:
        P.inc(sy.dma_start(out=N(dst), in_=N(src)), ld, 16)
    for nm in E["bias_c"]:
        P.inc(sy.dma_start(out=E["bias_c"][nm], in_=E["BC"][nm]), ld, 16)
    gsc, gsb = N("gsc"), N("gsb")
    P.inc(sy.dma_start(out=gsc[:, 0:64], in_=N("proj_in")[0:128, :]), ld, 16)
    P.inc(sy.dma_start(out=gsc[:, 64:128], in_=N("proj_in")[128:256, :]), ld, 16)
    P.inc(sy.dma_start(out=gsc[0:1, 128:128 + D], in_=N("bv_row")), ld, 16)
    LD0 = P.val(ld)

    for nm in E["W_in"]:
        P.inc(gs.dma_start(out=E["wb_d"][nm], in_=E["W_in"][nm]), gp, 16)
    P.inc(gs.dma_start(out=N("xhi_d"), in_=N("x_in")), gp, 16)
    GP0 = P.val(gp)

    P.wait(sy, gp, GP0)
    wkT, wvT = N("wkT"), N("wvT")
    for kc in range(8):
        P.inc(sy.dma_start(out=wkT[:, kc * 1024:(kc + 1) * 1024],
                           in_=E["wb_d"]["wk"][:, kc * 128:(kc + 1) * 128],
                           transpose=True), ld, 16)
        P.inc(sy.dma_start(out=wvT[:, kc * 1024:(kc + 1) * 1024],
                           in_=E["wb_d"]["wv"][:, kc * 128:(kc + 1) * 128],
                           transpose=True), ld, 16)
    LD_WKV = P.val(ld)

    for _i, _v in enumerate([LN_EPS, -0.5, NORM, LNR, RATIO * EPS_K, -1.0]):
        vc.memset(E["cvals"][:, _i:_i + 1], float(_v))
    vc.memset(N("epsrow"), RATIO * EPS_K)
    vc.drain()
    P.wait(vc, ld, LD0)
    rw_sb, rwh_sb, rwl_sb = N("rw_sb"), N("rwh_sb"), N("rwl_sb")
    vc.tensor_copy(rwh_sb, rw_sb)
    vc.drain()
    vc.tensor_tensor(rwl_sb, rw_sb, rwh_sb, ALU.subtract)
    vc.drain()
    vc.reduce_sum(N("s1col"), rw_sb, axis=AX.X)
    vv = N("v_b").rearrange("p (t h c) -> p t h c", h=HEADS, c=65)
    vc.memset(vv[:, :, :, 64:65], 1.0)
    vc.memset(N("ES"), 0.0)
    vc.memset(N("stab_run"), -30000.0)
    P.inc(vc.memset(N("maxc"), -30000.0), ve, 1)
    VE_RW = P.val(ve)

    P.wait(tn, ld, LD0)
    P.wait(tn, ve, VE_RW)
    id_f, id_b = N("id_f"), N("id_b")
    tn.transpose(pb[0][:, 0:128], gsc[:, 0:128], id_f)
    tn.transpose(pb[1][:, 0:128], gsc[:, 0:128], id_f)
    for kc in range(8):
        tn.transpose(pEb[:, kc * 16:kc * 16 + 8], rwh_sb[:, kc * 128:(kc + 1) * 128], id_b[0:8, 0:8])
    for kc in range(8):
        tn.transpose(pEb[:, 128 + kc * 16:128 + kc * 16 + 8], rwl_sb[:, kc * 128:(kc + 1) * 128], id_b[0:8, 0:8])
    tn.matmul(pb[2], N("onesrow_f"), gsc[0:1, 128:128 + 512], start=True, stop=True)
    P.inc(tn.matmul(pE[:, 2048:2560], N("onesrow_f"), gsc[0:1, 128 + 512:128 + D], start=True, stop=True), pe, 1)
    PE_INIT = P.val(pe)

    P.wait(sc, pe, PE_INIT)
    projT, bv_bc = N("projT"), N("bv_bc")
    sc.activation(out=projT[0:64, 0:128], in_=pb[0][0:64, 0:128], func=AF.Identity, scale=NORM)
    sc.activation(out=projT[0:64, 128:256], in_=pb[0][64:128, 0:128], func=AF.Identity, scale=NORM)
    sc.activation(out=projT[64:128, 0:128], in_=pb[1][0:64, 0:128], func=AF.Identity, scale=NORM)
    sc.activation(out=projT[64:128, 128:256], in_=pb[1][64:128, 0:128], func=AF.Identity, scale=NORM)
    sc.activation(out=bv_bc[:, 0:512], in_=pb[2], func=AF.Copy)
    P.inc(sc.activation(out=bv_bc[:, 512:1024], in_=pE[:, 2048:2560], func=AF.Copy), ae, 1)
    P.wait(vc, ae)
    vc.tensor_copy(N("rwhT"), pEb[:, 0:128].rearrange("p (k c) -> p k c", c=16)[:, :, 0:8])
    P.inc(vc.tensor_copy(N("rwlT"), pEb[:, 128:256].rearrange("p (k c) -> p k c", c=16)[:, :, 0:8]), ve, 1)
    VE_CONST = P.val(ve)
    P.wait(tn, ve, VE_CONST)

    _s0.end()
    emit_g(P, E)


def emit_g(P, E):
    ld, gp, pe, ve, ae, sp = (E["S_ld"], E["S_gp"], E["S_pe"], E["S_ve"],
                              E["S_ae"], E["S_sp"])
    sy, gs, tn, vc, sc = P.sync, P.gpsimd, P.tensor, P.vector, P.scalar
    pb, pE, pEb, pES = E["pb"], E["pE"], E["pEb"], E["pES"]
    _sg = _Scope(E, "Pg")
    E["G_PE_LG"] = 0
    E["G_PE_KV"] = 0
    N = lambda k: E[k]
    x_in, xhi_d, xlo_d = N("x_in"), N("xhi_d"), N("xlo_d")
    xq_g, gsc, gsb, stat_c = N("xq_g"), N("gsc"), N("gsb"), N("stat_c")
    stat_r, mu_rf = N("stat_r"), N("mu_rf")
    hiT_s, loT_s, xlo_st = N("hiT_s"), N("loT_s"), N("xlo_st")
    xnT_s, xnT_own = N("xnT_s"), N("xnT_own")
    kT_b, v_b, wkT, wvT = N("kT_b"), N("v_b"), N("wkT"), N("wvT")
    kp_t, k2_t = N("kp_t"), N("k2_t")
    diag_r, diag_c, dstab = N("diag_r"), N("diag_c"), N("dstab")
    maxc, stabc, srow = N("maxc"), N("stabc"), N("srow")
    stab_run, ES, vsum_sb = N("stab_run"), N("ES"), N("vsum_sb")
    prob_g, idx_g, iotam = N("prob_g"), N("idx_g"), N("iotam")
    projT, indh = N("projT"), N("indh")
    bias_c, bv_bc = N("bias_c"), N("bv_bc")
    onesrow_f, onesrow_b, onescol_b = N("onesrow_f"), N("onesrow_b"), N("onescol_b")
    id_f, s1col, p_sc = N("id_f"), N("s1col"), N("p_sc")

    for slot in range(NSLOT):
        for half in range(2):
            r0 = slot * BT + half * HB
            ebank_started = set()

            def estart(i):
                b = 3 + (i * 65) // 512 if False else 3 + ((i // 7))
                if b in ebank_started:
                    return False
                ebank_started.add(b)
                return True

            for t5 in range(2):
                tb = r0 + t5 * 512
                lo_half = half * HB + t5 * 512
                for t1 in range(4):
                    buf = xq_g[t1 % 2]
                    P.wait(sy, ve)
                    P.inc(sy.dma_start(out=buf, in_=x_in[tb + t1 * 128: tb + (t1 + 1) * 128, :]), ld, 16)
                    P.wait(sc, ld)
                    P.inc(sc.activation(out=gsc[:, 0:1024], in_=buf, func=AF.Square), ae, 1)
                    P.wait(vc, ld)
                    vc.reduce_sum(stat_c[:, t1:t1 + 1], buf, axis=AX.X)
                    P.wait(vc, ae)
                    vc.reduce_sum(stat_c[:, 4 + t1:5 + t1], gsc[:, 0:1024], axis=AX.X)
                    vc.tensor_copy(gsb[:, 0:1024], buf)
                    P.inc(vc.tensor_tensor(xlo_st[:, t1 * D:(t1 + 1) * D], buf, gsb[:, 0:1024], ALU.subtract), ve, 1)
                vc.drain()
                vc.tensor_scalar(stat_c[:, 0:4], stat_c[:, 0:4], 1.0 / D, None, ALU.mult)
                vc.tensor_scalar(stat_c[:, 4:8], stat_c[:, 4:8], 1.0 / D, None, ALU.mult)
                vc.drain()
                vc.tensor_tensor(stat_c[:, 8:12], stat_c[:, 0:4], stat_c[:, 0:4], ALU.mult)
                vc.drain()
                P.inc(vc.tensor_tensor(stat_c[:, 4:8], stat_c[:, 4:8], stat_c[:, 8:12], ALU.subtract), ve, 1)
                P.wait(sc, ve)
                sc.activation(out=stat_c[:, 8:12], in_=stat_c[:, 4:8], func=AF.Ln, bias=LN_EPS)
                P.inc(sc.activation(out=stat_c[:, 4:8], in_=stat_c[:, 8:12], func=AF.Exp, scale=-0.5), ae, 1)
                P.wait(tn, ae)
                lastt = None
                for t1 in range(4):
                    tn.transpose(pb[2][0:1, t1 * 128:(t1 + 1) * 128], stat_c[:, t1:t1 + 1], id_f)
                    lastt = tn.transpose(pb[0][0:1, t1 * 128:(t1 + 1) * 128], stat_c[:, 4 + t1:5 + t1], id_f)
                P.inc(lastt, pe, 1)
                P.wait(sc, pe)
                sc.activation(out=mu_rf[0:1, 0:512], in_=pb[2][0:1, 0:512], func=AF.Copy)
                P.inc(sc.activation(out=stat_r[0:1, 0:512], in_=pb[0][0:1, 0:512], func=AF.Copy), ae, 1)
                P.wait(sy, ve)
                P.inc(sy.dma_start(out=xlo_d[tb:tb + 512, :].rearrange("(a p) d -> p a d", p=128),
                                   in_=xlo_st.rearrange("p (a d) -> p a d", d=D)), ld, 16)
                XLO_ST = P.val(ld)
                P.wait(sy, ld, XLO_ST)
                P.wait(sy, pe, E.get("G_PE_LG", 0))
                for kc in range(8):
                    P.inc(sy.dma_start(out=hiT_s[:, kc * 512:(kc + 1) * 512],
                                       in_=xhi_d[tb:tb + 512, kc * 128:(kc + 1) * 128], transpose=True), ld, 16)
                    P.inc(sy.dma_start(out=loT_s[:, kc * 512:(kc + 1) * 512],
                                       in_=xlo_d[tb:tb + 512, kc * 128:(kc + 1) * 128], transpose=True), ld, 16)
                LD_T5 = P.val(ld)
                P.wait(tn, ae)
                P.inc(tn.matmul(pb[2], onesrow_b, stat_r[0:1, 0:512], start=True, stop=True), pe, 1)
                P.wait(sc, pe)
                P.inc(sc.activation(out=gsc[:, 1024:1536], in_=pb[2], func=AF.Copy), ae, 1)
                P.wait(tn, ae)
                P.inc(tn.matmul(pb[2], onesrow_f, mu_rf[0:1, 0:512], start=True, stop=True), pe, 1)
                PE_MU = P.val(pe)
                P.wait(vc, pe, PE_MU)
                P.wait(vc, pe, E.get("G_PE_KV", 0))
                P.wait(vc, ld, LD_T5)
                for kc in range(8):
                    vc.tensor_tensor(gsb[:, 0:512], hiT_s[:, kc * 512:(kc + 1) * 512], pb[2], ALU.subtract)
                    vc.tensor_tensor(xnT_s[:, kc * 512:(kc + 1) * 512], gsb[:, 0:512], gsc[:, 1024:1536], ALU.mult)
                    if slot == 0:
                        vc.tensor_copy(xnT_own[:, kc * BT + lo_half: kc * BT + lo_half + 512],
                                       xnT_s[:, kc * 512:(kc + 1) * 512])
                P.inc(vc.engine_nop(), ve, 1)
                VE_XN = P.val(ve)
                P.wait(tn, ve, VE_XN)
                P.wait(tn, ld, LD_T5)
                first = True
                last_i = None
                for kc in range(8):
                    for lh, rh in [(N("rwhT"), hiT_s), (N("rwhT"), loT_s),
                                   (N("rwlT"), hiT_s), (N("rwlT"), loT_s)]:
                        last_i = tn.matmul(pE[0:8, 0:512], lh[:, kc * 8:(kc + 1) * 8],
                                           rh[:, kc * 512:(kc + 1) * 512],
                                           start=first, stop=(kc == 7 and lh is N("rwlT") and rh is loT_s))
                        first = False
                P.inc(last_i, pe, 1)
                P.inc(tn.matmul(pb[0][0:8, :], onesrow_f[0:1, 0:8],
                               mu_rf[0:1, 0:512], start=True, stop=True), pe, 1)
                PE_LG = P.val(pe)
                E["G_PE_LG"] = PE_LG
                P.wait(sc, pe, PE_LG)
                P.inc(sc.activation(out=gsc[0:8, 0:512], in_=pE[0:8, 0:512], func=AF.Copy), ae, 1)
                P.wait(vc, ae)
                vc.tensor_scalar(gsc[0:8, 512:1024], pb[0][0:8, :], s1col, None, ALU.mult)
                vc.drain()
                vc.tensor_tensor(gsc[0:8, 0:512], gsc[0:8, 0:512], gsc[0:8, 512:1024], ALU.subtract)
                vc.drain()
                P.inc(vc.tensor_tensor(gsc[0:8, 0:512], gsc[0:8, 0:512], gsc[0:8, 1024:1536], ALU.mult), ve, 1)
                VE_LG = P.val(ve)
                cols0 = slot * 16 + half * 8 + t5 * 4
                P.wait(tn, ve, VE_LG)
                P.wait(tn, ae)
                lastt = None
                for t1 in range(4):
                    lastt = tn.transpose(pb[1][:, t1 * 8:(t1 + 1) * 8],
                                         gsc[0:8, t1 * 128:(t1 + 1) * 128], id_f[0:8, 0:8])
                P.inc(lastt, pe, 1)
                P.wait(vc, pe)
                lt = p_sc[:, 0:32]
                P.inc(vc.tensor_copy(lt, pb[1][:, 0:32]), ve, 1)
                VE_SMC = P.val(ve)
                vc.drain()
                vc.tensor_reduce(p_sc[:, 32:36], lt.rearrange("p (a b) -> p a b", b=8),
                                 axis=AX.X, op=ALU.max)
                vc.drain()
                for t1 in range(4):
                    vc.tensor_scalar(p_sc[:, 36 + t1 * 8:44 + t1 * 8], lt[:, t1 * 8:(t1 + 1) * 8],
                                     p_sc[:, 32 + t1:33 + t1], None, ALU.subtract)
                P.inc(vc.engine_nop(), ve, 1)
                P.wait(sc, ve)
                P.inc(sc.activation(out=p_sc[:, 68:100], in_=p_sc[:, 36:68], func=AF.Exp), ae, 1)
                P.wait(vc, ae)
                vc.tensor_reduce(p_sc[:, 100:104], p_sc[:, 68:100].rearrange("p (a b) -> p a b", b=8),
                                 axis=AX.X, op=ALU.add)
                vc.drain()
                vc.reciprocal(prob_g[:, cols0:cols0 + 4], p_sc[:, 100:104])
                for t1 in range(4):
                    vc.tensor_scalar(p_sc[:, 104 + t1 * 8:112 + t1 * 8], lt[:, t1 * 8:(t1 + 1) * 8],
                                     p_sc[:, 32 + t1:33 + t1], None, ALU.is_equal)
                vc.drain()
                vc.tensor_tensor(p_sc[:, 136:168], p_sc[:, 104:136], N("iotam4"), ALU.mult)
                vc.drain()
                vc.tensor_scalar(p_sc[:, 136:168], p_sc[:, 136:168], BIG, None, ALU.add)
                vc.drain()
                P.inc(vc.tensor_reduce(idx_g[:, cols0:cols0 + 4],
                                       p_sc[:, 136:168].rearrange("p (a b) -> p a b", b=8),
                                       axis=AX.X, op=ALU.min), ve, 1)
                P.wait(tn, ve, VE_SMC)
                evac_done = {}
                for dc in range(8):
                    pbuf = pb[dc % 2]
                    if dc >= 2:
                        P.wait(tn, ae, evac_done[dc % 2])
                    lastm = None
                    for kc in range(8):
                        lastm = tn.matmul(pbuf, wkT[:, kc * 1024 + dc * 128: kc * 1024 + (dc + 1) * 128],
                                          xnT_s[:, kc * 512:(kc + 1) * 512], start=(kc == 0), stop=(kc == 7))
                    P.inc(lastm, pe, 1)
                    P.wait(sc, pe)
                    sc.activation(out=kT_b[:, dc * HB + t5 * 512: dc * HB + t5 * 512 + 512],
                                  in_=pbuf, func=AF.Identity, bias=bias_c["bk"][:, dc:dc + 1])
                    P.inc(sc.activation(out=k2_t, in_=pbuf, func=AF.Square, bias=bias_c["bk"][:, dc:dc + 1]), ae, 1)
                    evac_done[dc % 2] = P.val(ae)
                    P.wait(tn, ae)
                    P.inc(tn.matmul(pb[2][0:16, :], indh[:, dc * 16:(dc + 1) * 16], k2_t,
                                    start=(dc == 0), stop=(dc == 7)), pe, 1)
                P.wait(sc, pe)
                P.inc(sc.activation(out=diag_r[:, 0:512], in_=pb[2][0:16, :], func=AF.Copy), ae, 1)
                AE_DIAG = P.val(ae)
                P.wait(tn, ae)
                vdone = {}
                for t1 in range(4):
                    for dh2 in range(2):
                        i = (t1 * 2 + dh2) % 2
                        pbuf = pb[i]
                        if t1 * 2 + dh2 >= 2:
                            P.wait(tn, ve, vdone[i])
                        lastm = None
                        for kc in range(8):
                            lastm = tn.matmul(pbuf,
                                              xnT_s[:, kc * 512 + t1 * 128: kc * 512 + (t1 + 1) * 128],
                                              wvT[:, kc * 1024 + dh2 * 512: kc * 1024 + (dh2 + 1) * 512],
                                              start=(kc == 0), stop=(kc == 7))
                        P.inc(lastm, pe, 1)
                        P.wait(vc, pe)
                        tcg = t5 * 4 + t1
                        vc.tensor_tensor(N("v_c")[:, dh2 * 512:(dh2 + 1) * 512], pbuf,
                                         bv_bc[:, dh2 * 512:(dh2 + 1) * 512], ALU.add)
                        vc.drain()
                        dst = v_b.rearrange("p (t h c) -> p t h c", h=HEADS, c=65)[
                            :, tcg, dh2 * 8:(dh2 + 1) * 8, 0:64]
                        P.inc(vc.tensor_copy(dst, N("v_c")[:, dh2 * 512:(dh2 + 1) * 512].rearrange(
                            "p (h c) -> p h c", c=64)), ve, 1)
                        vdone[i] = P.val(ve)
                    P.wait(tn, ve)
                    P.wait(tn, ae)
                    lastm = None
                    for dc in range(8):
                        lastm = tn.matmul(pb[2][:, dc:dc + 1], N("v_c")[:, dc * 128:(dc + 1) * 128],
                                          N("onescol_b"), start=(dc == 0), stop=(dc == 7))
                    P.inc(lastm, pe, 1)
                    P.wait(vc, pe)
                    vc.tensor_tensor(vsum_sb, vsum_sb, pb[2][:, 0:8], ALU.add)
                    vc.drain()
                    P.inc(vc.engine_nop(), ve, 1)
                E["G_PE_KV"] = P.val(pe)
                P.wait(tn, ae, AE_DIAG)
                P.wait(tn, ve)
                for t1 in range(4):
                    tcg = t5 * 4 + t1
                    P.wait(tn, ae)
                    P.inc(tn.transpose(pb[2][:, 0:16], diag_r[:, t1 * 128: t1 * 128 + 128],
                                       id_f[0:16, 0:16]), pe, 1)
                    P.wait(sc, pe)
                    P.inc(sc.activation(out=diag_c[:, tcg * 16:(tcg + 1) * 16], in_=pb[2][:, 0:16], func=AF.Copy), ae, 1)

            P.wait(tn, ve)
            P.wait(tn, ae)
            xp1_tiles = [pb[0], pb[1], pE[:, 0:512], pE[:, 512:1024]]
            rd_done = {}
            idx = 0
            for h in range(HEADS):
                hb64 = (h % 2) * 64
                for mc in range(2):
                    for t5 in range(2):
                        ptile = xp1_tiles[idx % 4]
                        if idx >= 4:
                            P.wait(tn, ve, rd_done[idx % 4])
                        P.inc(tn.matmul(ptile, projT[hb64:hb64 + 64, mc * 128:(mc + 1) * 128],
                                        kT_b[hb64:hb64 + 64, (h // 2) * HB + t5 * 512:(h // 2) * HB + t5 * 512 + 512],
                                        start=True, stop=True), pe, 1)
                        P.wait(vc, pe)
                        vc.reduce_max(p_sc[:, 50:51], ptile, axis=AX.X)
                        vc.drain()
                        vc.tensor_tensor(maxc[:, h:h + 1], maxc[:, h:h + 1], p_sc[:, 50:51], ALU.max)
                        P.inc(vc.engine_nop(), ve, 1)
                        rd_done[idx % 4] = P.val(ve)
                        idx += 1
            P.wait(tn, ve)
            P.inc(tn.transpose(pb[2][0:16, 0:128], maxc, id_f), pe, 1)
            P.wait(vc, pe)
            P.inc(vc.reduce_max(p_sc[0:16, 51:52], pb[2][0:16, 0:128], axis=AX.X), ve, 1)
            P.wait(tn, ve)
            P.inc(tn.transpose(pb[2][0:1, 128:144], p_sc[0:16, 51:52], id_f[0:16, 0:16]), pe, 1)
            P.wait(sc, pe)
            P.inc(sc.activation(out=srow[0:1, 0:16], in_=pb[2][0:1, 128:144], func=AF.Copy), ae, 1)
            P.wait(vc, ae)
            vc.memset(maxc, -30000.0)
            vc.drain()
            vc.tensor_tensor(srow[0:1, 16:32], stab_run, srow[0:1, 0:16], ALU.max)
            vc.drain()
            P.inc(vc.tensor_tensor(srow[0:1, 32:48], stab_run, srow[0:1, 16:32], ALU.subtract), ve, 1)
            P.wait(sc, ve)
            P.inc(sc.activation(out=srow[0:1, 48:64], in_=srow[0:1, 32:48], func=AF.Exp), ae, 1)
            P.wait(vc, ae)
            P.inc(vc.tensor_copy(stab_run, srow[0:1, 16:32]), ve, 1)
            P.wait(tn, ve)
            P.inc(tn.matmul(pb[2][:, 0:16], onesrow_f, srow[0:1, 48:64], start=True, stop=True), pe, 1)
            P.wait(sc, pe)
            P.inc(sc.activation(out=stabc, in_=pb[2][:, 0:16], func=AF.Copy), ae, 1)
            P.wait(vc, ae)
            for i in range(32):
                h = i // 2
                vc.tensor_scalar(ES[:, i * 65:(i + 1) * 65], ES[:, i * 65:(i + 1) * 65],
                                 stabc[:, h:h + 1], None, ALU.mult)
            P.inc(vc.engine_nop(), ve, 1)
            P.wait(tn, ve)
            P.inc(tn.matmul(pb[2][:, 16:32], onesrow_f, srow[0:1, 16:32], start=True, stop=True), pe, 1)
            P.wait(vc, pe)
            vc.tensor_scalar(gsc[:, 0:16], pb[2][:, 16:32], -LNR, None, ALU.add)
            vc.tensor_scalar(gsc[:, 512:640], diag_c, -1.0, None, ALU.mult)
            vc.drain()
            for tcg in range(8):
                vc.tensor_tensor(dstab[:, tcg * 16:(tcg + 1) * 16],
                                 gsc[:, 512 + tcg * 16:512 + (tcg + 1) * 16], gsc[:, 0:16], ALU.subtract)
            P.inc(vc.engine_nop(), ve, 1)
            VE_DSTAB = P.val(ve)
            ebank_started.clear()
            pend = None
            kp_done = {}
            exp_done = {}
            e_done = {}
            for h in range(HEADS):
                hb64 = (h % 2) * 64
                for tc in range(8):
                    i = tc % 2
                    pbuf = pb[i]
                    P.wait(tn, ve, VE_DSTAB)
                    if (h * 8 + tc) >= 2:
                        P.wait(tn, ae, exp_done[i])
                    P.inc(tn.matmul(pbuf[:, 0:256], kT_b[hb64:hb64 + 64, (h // 2) * HB + tc * 128:(h // 2) * HB + (tc + 1) * 128],
                                    projT[hb64:hb64 + 64, :], start=True, stop=True), pe, 1)
                    if pend is not None:
                        ph, ptc, pkp = pend
                        P.wait(tn, ae, kp_done[(ph, ptc)])
                        for mc in range(2):
                            si = ph * 2 + mc
                            st = estart(si)
                            lastm = tn.matmul(pES(si), pkp[:, mc * 128:(mc + 1) * 128],
                                              v_b.rearrange("p (t h c) -> p t h c", h=HEADS, c=65)[:, ptc, ph, :],
                                              start=st, stop=(ptc == 7))
                        P.inc(lastm, pe, 1)
                        e_done[ph * 8 + ptc] = P.val(pe)
                    P.wait(sc, pe)
                    P.inc(sc.activation(out=kp_t[i], in_=pbuf[:, 0:256], func=AF.Exp,
                                        bias=dstab[:, tc * 16 + h:tc * 16 + h + 1]), ae, 1)
                    exp_done[i] = P.val(ae)
                    kp_done[(h, tc)] = P.val(ae)
                    pend = (h, tc, kp_t[i])
            if pend is not None:
                ph, ptc, pkp = pend
                P.wait(tn, ae, kp_done[(ph, ptc)])
                for mc in range(2):
                    si = ph * 2 + mc
                    st = estart(si)
                    lastm = tn.matmul(pES(si), pkp[:, mc * 128:(mc + 1) * 128],
                                      v_b.rearrange("p (t h c) -> p t h c", h=HEADS, c=65)[:, ptc, ph, :],
                                      start=st, stop=True)
                P.inc(lastm, pe, 1)
                e_done[ph * 8 + ptc] = P.val(pe)
            PE_EDONE = P.val(pe)
            P.wait(vc, pe, PE_EDONE)
            for i in range(32):
                vc.tensor_tensor(ES[:, i * 65:(i + 1) * 65], ES[:, i * 65:(i + 1) * 65],
                                 pES(i), ALU.add)
            P.inc(vc.engine_nop(), ve, 1)
            P.wait(tn, ve)
            if E["debug"] and half == 1:
                nm = "xnT0e" if slot == 0 else f"xnT0s{slot}"
                P.wait(sy, ve)
                P.inc(sy.dma_start(out=E["dbg"][nm], in_=xnT_own), ld, 16)
                P.wait(sy, ld)
    _sg.end()
    _s = _Scope(E, "Pexch")
    emit_exchange_combine(P, E)
    _s.end()
    if E["debug"]:
        emit_gdebug(P, E)
    _s = _Scope(E, "Pa1")
    emit_a1(P, E)
    _s.end()
    _s = _Scope(E, "Proute")
    emit_routing(P, E)
    _s.end()
    _s = _Scope(E, "Pa2")
    emit_a2(P, E)
    _s.end()
    _s = _Scope(E, "Pmoe")
    emit_moe(P, E)
    _s.end()
    _s = _Scope(E, "Pfin")
    emit_final(P, E)
    _s.end()
    if E["debug"]:
        emit_fdebug(P, E)


def emit_exchange_combine(P, E):
    ld, gp, pe, ve, ae, sp = (E["S_ld"], E["S_gp"], E["S_pe"], E["S_ve"],
                              E["S_ae"], E["S_sp"])
    sy, gs, tn, vc, sc = P.sync, P.gpsimd, P.tensor, P.vector, P.scalar
    pb, pE, pEb = E["pb"], E["pE"], E["pEb"]
    N = lambda k: E[k]
    ES, vsum_sb, stab_run = N("ES"), N("vsum_sb"), N("stab_run")
    prob_g, idx_g, srow = N("prob_g"), N("idx_g"), N("srow")
    exch_w, stabc, maxc, gsc = N("exch_w"), N("stabc"), N("maxc"), N("gsc")
    ctx_sb, ksum_col = N("ctx_sb"), N("ksum_col")
    onesrow_f = N("onesrow_f")

    P.wait(vc, ae)
    vc.tensor_copy(exch_w[:, 0:2080], ES)
    vc.tensor_copy(exch_w[:, 2080:2088], vsum_sb)
    vc.tensor_copy(exch_w[0:1, 2088:2104], stab_run)
    vc.tensor_copy(exch_w[:, 2104:2168], prob_g[:, 0:64])
    vc.tensor_copy(exch_w[:, 2168:2232], idx_g[:, 0:64])
    vc.drain()
    P.inc(vc.engine_nop(), ve, 1)

    P.wait(gs, ve)
    if gs.live:
        gr = P.real
        with gr.register("p") as rp, gr.register("cnt") as rc, \
             gr.register("fv") as rf, gr.register("sv") as rs, \
             gr.register("tmp") as rt:
            gr.load(rp, N("par_sb"))
            gr.load(rs, N("salt_sb")[0:1, 0:1])
            with gr.If_eq(rp, 0):
                gr.dma_start(out=N("exch_a"), in_=exch_w).then_inc(gp, 16)
            with gr.Else():
                gr.dma_start(out=N("exch_b"), in_=exch_w).then_inc(gp, 16)
            gr.wait_ge(gp, P.val(gp) + 16)
            with gr.If_eq(rp, 0):
                gr.dma_start(out=N("flag_a"), in_=N("salt_sb")[0:1, 0:1]).then_inc(gp, 16)
            with gr.Else():
                gr.dma_start(out=N("flag_b"), in_=N("salt_sb")[0:1, 0:1]).then_inc(gp, 16)
            gr.wait_ge(gp, P.val(gp) + 32)

            def spin(flag):
                def cond():
                    gr.reg_add(rc, rc, 16)
                    gr.dma_start(out=N("flg_sb"), in_=flag).then_inc(sp, 16)
                    gr.wait_ge(sp, rc)
                    gr.load(rf, N("flg_sb"))
                    gr.reg_sub(rt, rf, rs)
                    return rt
                with gr.While(cond):
                    gr.nop()

            gr.reg_mov(rc, 0)
            with gr.If_eq(rp, 0):
                spin(N("flag_b"))
            with gr.Else():
                spin(N("flag_a"))
            with gr.If_eq(rp, 0):
                gr.dma_start(out=exch_w, in_=N("exch_b")).then_inc(gp, 16)
            with gr.Else():
                gr.dma_start(out=exch_w, in_=N("exch_a")).then_inc(gp, 16)
            gr.wait_ge(gp, P.val(gp) + 48)
    P.bump(gp, 48)
    if gs.live:
        P.real.nop().then_inc(gp, 1)
    P.bump(gp, 1)
    GPX = P.val(gp)

    if E["debug"]:
        P.wait(sy, gp, GPX)
        P.inc(sy.dma_start(out=E["dbg"]["ESpre"], in_=ES), ld, 16)
        P.inc(sy.dma_start(out=E["dbg"]["stabpre"], in_=stab_run), ld, 16)
        P.inc(sy.dma_start(out=E["dbg"]["xnT0m"], in_=E["xnT_own"]), ld, 16)
        P.inc(sy.dma_start(out=E["dbg"]["diagc"], in_=E["diag_c"]), ld, 16)
        P.inc(sy.dma_start(out=E["dbg"]["dstab"], in_=E["dstab"]), ld, 16)
        P.wait(sy, ld)
    P.wait(vc, gp, GPX)
    pstab = exch_w[0:1, 2088:2104]
    vc.tensor_tensor(srow[0:1, 0:16], stab_run, pstab, ALU.max)
    vc.drain()
    vc.tensor_tensor(srow[0:1, 16:32], stab_run, srow[0:1, 0:16], ALU.subtract)
    vc.drain()
    P.inc(vc.tensor_tensor(srow[0:1, 48:64], pstab, srow[0:1, 0:16], ALU.subtract), ve, 1)
    P.wait(sc, ve)
    sc.activation(out=srow[0:1, 32:48], in_=srow[0:1, 16:32], func=AF.Exp)
    P.inc(sc.activation(out=srow[0:1, 64:80], in_=srow[0:1, 48:64], func=AF.Exp), ae, 1)
    P.wait(vc, ae)
    P.inc(vc.tensor_copy(stab_run, srow[0:1, 0:16]), ve, 1)
    P.wait(tn, ae)
    tn.matmul(pb[2][:, 0:16], onesrow_f, srow[0:1, 32:48], start=True, stop=True)
    P.inc(tn.matmul(pb[2][:, 16:32], onesrow_f, srow[0:1, 64:80], start=True, stop=True), pe, 1)
    P.wait(sc, pe)
    sc.activation(out=stabc, in_=pb[2][:, 0:16], func=AF.Copy)
    P.inc(sc.activation(out=maxc, in_=pb[2][:, 16:32], func=AF.Copy), ae, 1)
    P.wait(vc, ae)
    for i in range(32):
        h = i // 2
        vc.tensor_scalar(gsc[:, 0:65], exch_w[:, i * 65:(i + 1) * 65], maxc[:, h:h + 1], None, ALU.mult)
        vc.drain()
        vc.scalar_tensor_tensor(ES[:, i * 65:(i + 1) * 65], ES[:, i * 65:(i + 1) * 65],
                                stabc[:, h:h + 1], gsc[:, 0:65], ALU.mult, ALU.add)
        vc.drain()
    vc.tensor_tensor(vsum_sb, vsum_sb, exch_w[:, 2080:2088], ALU.add)
    vc.drain()
    vc.tensor_copy(prob_g[:, 64:128], exch_w[:, 2104:2168])
    vc.tensor_copy(idx_g[:, 64:128], exch_w[:, 2168:2232])
    vc.drain()
    P.inc(vc.engine_nop(), ve, 1)
    P.wait(sy, ve)
    P.inc(sy.dma_start(out=N("vsum_d").rearrange("o (a b) -> o b a", b=128), in_=vsum_sb,
                       allow_slow_non_contiguous=True), ld, 16)
    P.wait(sy, ld)
    P.inc(sy.dma_start(out=N("vsr_sb"), in_=N("vsum_d")), ld, 16)
    VSROW = P.val(ld)
    P.wait(tn, ld, VSROW)
    for h in range(HEADS):
        P.wait(tn, ve)
        P.inc(tn.matmul(pb[2][:, 64:128], onesrow_f, N("vsr_sb")[0:1, h * 64:(h + 1) * 64],
                        start=True, stop=True), pe, 1)
        P.wait(vc, pe)
        for mc in range(2):
            i = h * 2 + mc
            vc.scalar_tensor_tensor(ctx_sb[:, i * 64:(i + 1) * 64], pb[2][:, 64:128],
                                    RATIO * EPS_K, ES[:, i * 65:i * 65 + 64], ALU.mult, ALU.add)
            vc.drain()
        P.inc(vc.engine_nop(), ve, 1)
    esv = ES.rearrange("p (i c) -> p i c", c=65)[:, :, 64:65]
    P.inc(vc.tensor_scalar(ksum_col.rearrange("p (i c) -> p i c", c=1), esv,
                           RATIO * EPS_K * T, None, ALU.add), ve, 1)
    VE_CTX = P.val(ve)
    P.wait(tn, ve, VE_CTX)


def emit_gdebug(P, E):
    ld, ve, ae, pe = E["S_ld"], E["S_ve"], E["S_ae"], E["S_pe"]
    sy = P.sync
    dbg = E["dbg"]
    P.wait(sy, ve)
    P.wait(sy, ae)
    for nm, src in [("xnT0", E["xnT_own"]), ("prob", E["prob_g"]), ("idx", E["idx_g"]),
                    ("stab", E["stab_run"]), ("ES", E["ES"]), ("vsum", E["vsum_sb"]),
                    ("ctx", E["ctx_sb"]), ("ksum", E["ksum_col"])]:
        if nm in dbg:
            P.inc(sy.dma_start(out=dbg[nm], in_=src), ld, 16)
    P.wait(sy, ld)


def emit_a1(P, E):
    ld, gp, pe, ve, ae = (E["S_ld"], E["S_gp"], E["S_pe"], E["S_ve"], E["S_ae"])
    sy, gs, tn, vc, sc = P.sync, P.gpsimd, P.tensor, P.vector, P.scalar
    pb, pE, pEb = E["pb"], E["pE"], E["pEb"]
    N = lambda k: E[k]
    xnT_own, qT, wqT, woT = N("xnT_own"), N("qT"), N("wqT"), N("woT")
    ctx_sb, ksum_col = N("ctx_sb"), N("ksum_col")
    diagq_r, diagq_c = N("diagq_r"), N("diagq_c")
    q2_t, qp_t, a1sc, a1sb = N("q2_t"), N("qp_t"), N("a1sc"), N("a1sb")
    attnT, s2c, statA = N("attnT"), N("s2c"), N("statA")
    x2T, x2_d = N("x2T"), N("x2_d")
    stab_run = N("stab_run")
    bias_c, indh, projT, id_f = N("bias_c"), N("indh"), N("projT"), N("id_f")
    onesrow_f, onesrow_b, onescol_b = N("onesrow_f"), N("onesrow_b"), N("onescol_b")
    p_sc = N("p_sc")

    P.wait(sy, ve)
    for kc in range(8):
        P.inc(sy.dma_start(out=wqT[:, kc * 1024:(kc + 1) * 1024],
                           in_=E["wb_d"]["wq"][:, kc * 128:(kc + 1) * 128], transpose=True), ld, 16)
        P.inc(sy.dma_start(out=woT[:, kc * 1024:(kc + 1) * 1024],
                           in_=E["wb_d"]["wo"][:, kc * 128:(kc + 1) * 128], transpose=True), ld, 16)
    LD_QO = P.val(ld)
    P.wait(tn, ld, LD_QO)

    evac_done = {}
    for t5 in range(4):
        for dc in range(8):
            pbuf = pb[dc % 2]
            key = (t5 * 8 + dc) % 2
            if t5 * 8 + dc >= 2:
                P.wait(tn, ae, evac_done[key])
            lastm = None
            for kc in range(8):
                lastm = tn.matmul(pbuf, wqT[:, kc * 1024 + dc * 128: kc * 1024 + (dc + 1) * 128],
                                  xnT_own[:, kc * BT + t5 * 512: kc * BT + (t5 + 1) * 512],
                                  start=(kc == 0), stop=(kc == 7))
            P.inc(lastm, pe, 1)
            P.wait(sc, pe)
            sc.activation(out=qT[:, dc * BT + t5 * 512: dc * BT + t5 * 512 + 512],
                          in_=pbuf, func=AF.Identity, bias=bias_c["bq"][:, dc:dc + 1])
            P.inc(sc.activation(out=q2_t, in_=pbuf, func=AF.Square, bias=bias_c["bq"][:, dc:dc + 1]), ae, 1)
            evac_done[key] = P.val(ae)
            P.wait(tn, ae)
            P.inc(tn.matmul(pb[2][0:16, :], indh[:, dc * 16:(dc + 1) * 16], q2_t,
                            start=(dc == 0), stop=(dc == 7)), pe, 1)
        P.wait(sc, pe)
        P.inc(sc.activation(out=diagq_r[:, 0:512], in_=pb[2][0:16, :], func=AF.Copy), ae, 1)
        for t1 in range(4):
            tcg = t5 * 4 + t1
            P.wait(tn, ae)
            P.inc(tn.transpose(pb[2][:, 0:16], diagq_r[:, t1 * 128:(t1 + 1) * 128], id_f[0:16, 0:16]), pe, 1)
            P.wait(sc, pe)
            P.inc(sc.activation(out=diagq_c[:, tcg * 16:(tcg + 1) * 16], in_=pb[2][:, 0:16], func=AF.Copy), ae, 1)
    AE_QD = P.val(ae)

    A1_EXP = 0
    A1_WO = {0: 0, 1: 0}
    X2ST = {0: 0, 1: 0}
    qp4, epsrow = N("qp4"), N("epsrow")
    ecol, kse = N("ecol_b"), N("kse_b")
    P.wait(tn, ae, AE_QD)
    first = True
    for h in range(HEADS):
        for mc in range(2):
            i2 = h * 2 + mc
            tn.matmul(pE[0:1, h * 64:(h + 1) * 64], N("onescol_b"),
                      ctx_sb[:, i2 * 64:(i2 + 1) * 64], start=first,
                      stop=(h == HEADS - 1 and mc == 1))
            first = False
    P.inc(tn.matmul(pb[2][0:1, 0:32], N("onescol_b"), ksum_col, start=True, stop=True), pe, 1)
    P.wait(sc, pe)
    sc.activation(out=ecol, in_=pE[0:1, 0:1024], func=AF.Identity, scale=RATIO * EPS_K)
    P.inc(sc.activation(out=N("kse2_b"), in_=pb[2][0:1, 0:32], func=AF.Identity,
                        scale=RATIO * EPS_K), ae, 1)
    P.wait(vc, ae)
    k2v = N("kse2_b").rearrange("o (h m) -> o h m", m=2)
    P.inc(vc.tensor_tensor(kse.rearrange("o (h m) -> o h m", m=1), k2v[:, :, 0:1],
                           k2v[:, :, 1:2], ALU.add), ve, 1)
    VE_KSE = P.val(ve)
    P.wait(tn, ve, VE_KSE)

    for t5 in range(4):
        for h in range(HEADS):
            hb64 = (h % 2) * 64
            P.wait(tn, ae, max(A1_EXP, AE_QD))
            for t1 in range(4):
                tcg = t5 * 4 + t1
                pbuf = pb[t1 // 2][:, (t1 % 2) * 256:(t1 % 2) * 256 + 256]
                lastm = tn.matmul(pbuf, qT[hb64:hb64 + 64, (h // 2) * BT + tcg * 128:(h // 2) * BT + (tcg + 1) * 128],
                                  projT[hb64:hb64 + 64, :], start=True, stop=True)
            P.inc(lastm, pe, 1)
            P.wait(vc, pe)
            for t1 in range(4):
                vc.reduce_max(p_sc[:, 40 + t1:41 + t1], pb[t1 // 2][:, (t1 % 2) * 256:(t1 % 2) * 256 + 256], axis=AX.X)
            vc.drain()
            vc.tensor_tensor(p_sc[:, 44:48].rearrange("p (a b) -> p a b", b=1),
                             p_sc[:, 40:44].rearrange("p (a b) -> p a b", b=1),
                             diagq_c.rearrange("p (t c) -> p t c", c=16)[:, t5 * 4:(t5 + 1) * 4, h:h + 1],
                             ALU.add)
            vc.drain()
            P.inc(vc.tensor_scalar(p_sc[:, 48:52], p_sc[:, 44:48], -1.0, LNR, ALU.mult, ALU.add), ve, 1)
            P.wait(sc, ve)
            lasta = None
            for t1 in range(4):
                lasta = sc.activation(out=qp4[:, t1 * 256:(t1 + 1) * 256],
                                      in_=pb[t1 // 2][:, (t1 % 2) * 256:(t1 % 2) * 256 + 256],
                                      func=AF.Exp, bias=p_sc[:, 48 + t1:49 + t1])
            P.inc(lasta, ae, 1)
            A1_EXP = P.val(ae)
            P.wait(tn, ae, A1_EXP)
            lastt = None
            for k in range(8):
                lastt = tn.transpose(pEb[:, k * 128:(k + 1) * 128], qp4[:, k * 128:(k + 1) * 128], N("id_b"))
            P.inc(lastt, pe, 1)
            P.wait(vc, pe)
            for t1 in range(4):
                for mc in range(2):
                    vc.tensor_copy(a1sb[:, mc * 512 + t1 * 128: mc * 512 + (t1 + 1) * 128],
                                   pEb[:, (t1 * 2 + mc) * 128:(t1 * 2 + mc + 1) * 128])
            P.inc(vc.engine_nop(), ve, 1)
            VE_QPT = P.val(ve)
            P.wait(tn, ve, VE_QPT)
            i2 = h * 2
            tn.matmul(pE[0:64, 1024:1536], ctx_sb[:, i2 * 64:(i2 + 1) * 64], a1sb[:, 0:512], start=True, stop=False)
            tn.matmul(pE[0:64, 1024:1536], ecol[0:1, h * 64:(h + 1) * 64], epsrow, start=False, stop=False)
            P.inc(tn.matmul(pE[0:64, 1024:1536], ctx_sb[:, (i2 + 1) * 64:(i2 + 2) * 64], a1sb[:, 512:1024], start=False, stop=True), pe, 1)
            tn.matmul(pE[0:1, 1536:2048], ksum_col[:, i2:i2 + 1], a1sb[:, 0:512], start=True, stop=False)
            tn.matmul(pE[0:1, 1536:2048], kse[0:1, h:h + 1], epsrow, start=False, stop=False)
            P.inc(tn.matmul(pE[0:1, 1536:2048], ksum_col[:, i2 + 1:i2 + 2], a1sb[:, 512:1024], start=False, stop=True), pe, 1)
            P.wait(sc, pe)
            sc.activation(out=N("a1sc2")[0:1, :], in_=pE[0:1, 1536:2048], func=AF.Ln)
            P.inc(sc.activation(out=p_sc[0:1, 8:520], in_=N("a1sc2")[0:1, :], func=AF.Exp,
                                scale=-1.0), ae, 1)
            P.wait(tn, ae)
            P.inc(tn.matmul(pE[0:64, 2048:2560], onesrow_f[0:1, 0:64], p_sc[0:1, 8:520], start=True, stop=True), pe, 1)
            P.wait(sc, pe)
            P.inc(sc.activation(out=a1sc[0:64, 0:512], in_=pE[0:64, 1024:1536], func=AF.Copy), ae, 1)
            P.wait(vc, ae)
            hb64 = (h % 2) * 64
            P.inc(vc.tensor_tensor(attnT[hb64:hb64 + 64, (h // 2) * 512:(h // 2 + 1) * 512],
                                   a1sc[0:64, 0:512], pE[0:64, 2048:2560], ALU.mult), ve, 1)
        P.wait(tn, ve)
        P.wait(tn, ae)
        VE_WORD = 0
        for dc in range(8):
            pbuf = pb[dc % 2]
            P.wait(tn, ve, A1_WO[dc % 2])
            lastm = None
            for kc in range(8):
                lastm = tn.matmul(pbuf, woT[:, kc * 1024 + dc * 128: kc * 1024 + (dc + 1) * 128],
                                  attnT[:, kc * 512:(kc + 1) * 512], start=(kc == 0), stop=(kc == 7))
            P.inc(lastm, pe, 1)
            P.wait(sc, pe)
            P.wait(sc, ve, VE_WORD)
            P.inc(sc.activation(out=a1sc[:, 0:512], in_=pbuf, func=AF.Identity, bias=bias_c["bo"][:, dc:dc + 1]), ae, 1)
            P.wait(vc, ae)
            P.inc(vc.tensor_tensor(s2c[:, dc * 512:(dc + 1) * 512], a1sc[:, 0:512],
                                   xnT_own[:, dc * BT + t5 * 512: dc * BT + (t5 + 1) * 512], ALU.add), ve, 1)
            A1_WO[dc % 2] = P.val(ve)
            VE_WORD = P.val(ve)
        VE_S2 = P.val(ve)
        P.wait(vc, ve, VE_S2)
        STAT_PE = 0
        for dc in range(8):
            P.wait(vc, pe, STAT_PE)
            vc.tensor_copy(a1sb[:, 0:512], s2c[:, dc * 512:(dc + 1) * 512])
            vc.drain()
            P.inc(vc.engine_nop(), ve, 1)
            P.wait(tn, ve)
            P.inc(tn.matmul(pE[0:1, 0:512], onescol_b, a1sb[:, 0:512], start=(dc == 0), stop=(dc == 7)), pe, 1)
            P.wait(sc, ve)
            P.wait(sc, pe, STAT_PE)
            P.inc(sc.activation(out=N("a1sb2"), in_=s2c[:, dc * 512:(dc + 1) * 512], func=AF.Square), ae, 1)
            P.wait(tn, ae)
            P.inc(tn.matmul(pE[0:1, 512:1024], onescol_b, N("a1sb2"), start=(dc == 0), stop=(dc == 7)), pe, 1)
            STAT_PE = P.val(pe)
        P.wait(vc, pe)
        st = statA
        vc.tensor_scalar(st[0:1, 0:512], pE[0:1, 0:512], 1.0 / D, None, ALU.mult)
        vc.drain()
        vc.tensor_scalar(st[0:1, 512:1024], pE[0:1, 512:1024], 1.0 / D, None, ALU.mult)
        vc.drain()
        vc.tensor_tensor(st[0:1, 1024:1536], st[0:1, 0:512], st[0:1, 0:512], ALU.mult)
        vc.drain()
        P.inc(vc.tensor_tensor(st[0:1, 512:1024], st[0:1, 512:1024], st[0:1, 1024:1536], ALU.subtract), ve, 1)
        P.wait(sc, ve)
        sc.activation(out=st[0:1, 1024:1536], in_=st[0:1, 512:1024], func=AF.Ln, bias=LN_EPS)
        P.inc(sc.activation(out=st[0:1, 1536:2048], in_=st[0:1, 1024:1536], func=AF.Exp, scale=-0.5), ae, 1)
        P.wait(tn, ae)
        tn.matmul(pE[:, 0:512], onesrow_f, st[0:1, 0:512], start=True, stop=True)
        P.inc(tn.matmul(pE[:, 512:1024], onesrow_f, st[0:1, 1536:2048], start=True, stop=True), pe, 1)
        P.wait(vc, pe)
        AFF = 0
        for dc in range(8):
            P.wait(vc, ae, AFF)
            vc.tensor_tensor(a1sc[:, 0:512], s2c[:, dc * 512:(dc + 1) * 512], pE[:, 0:512], ALU.subtract)
            vc.tensor_tensor(N("a1sc2"), a1sc[:, 0:512], pE[:, 512:1024], ALU.mult)
            P.inc(vc.engine_nop(), ve, 1)
            x2buf = N("a1sc3") if dc % 2 else a1sc[:, 512:1024]
            P.wait(sc, ve)
            P.wait(sc, ld, X2ST[dc % 2])
            sc.activation(out=x2buf, in_=N("a1sc2"), func=AF.Identity,
                          scale=bias_c["g2"][:, dc:dc + 1], bias=bias_c["be2"][:, dc:dc + 1])
            P.inc(sc.activation(out=x2T[:, dc * BT + t5 * 512: dc * BT + (t5 + 1) * 512],
                                in_=N("a1sc2"), func=AF.Identity,
                                scale=bias_c["g2"][:, dc:dc + 1], bias=bias_c["be2"][:, dc:dc + 1]), ae, 1)
            AFF = P.val(ae)
            P.wait(sy, ae)
            P.inc(sy.dma_start(out=x2_d[:, dc * BT + t5 * 512: dc * BT + (t5 + 1) * 512], in_=x2buf), ld, 16)
            X2ST[dc % 2] = P.val(ld)
    P.wait(sy, ve)
    P.inc(sy.dma_start(out=N("xn_own_d"), in_=xnT_own), ld, 16)
    P.wait(sy, ld)
    if E["debug"]:
        P.wait(sy, ae)
        P.inc(sy.dma_start(out=E["dbg"]["qTd"], in_=qT), ld, 16)
        P.inc(sy.dma_start(out=E["dbg"]["dqc"], in_=diagq_c), ld, 16)
        P.inc(sy.dma_start(out=E["dbg"]["attnT"], in_=attnT), ld, 16)
        P.inc(sy.dma_start(out=E["dbg"]["s2c"], in_=s2c), ld, 16)
        P.wait(sy, ld)


def emit_routing(P, E):
    ld, gp, pe, ve, ae = (E["S_ld"], E["S_gp"], E["S_pe"], E["S_ve"], E["S_ae"])
    sy, gs, tn, vc, sc = P.sync, P.gpsimd, P.tensor, P.vector, P.scalar
    pb, pE, pEb = E["pb"], E["pE"], E["pEb"]
    N = lambda k: E[k]
    prob_g, idx_g, fe_g = N("prob_g"), N("idx_g"), N("fe_g")
    rtw, rtw2, rts = N("rtw"), N("rtw2"), N("rts")
    mask_b, tri_i, tri_s = N("mask_b"), N("tri_i"), N("tri_s")
    onesrow_f, onescol_b, id_f, id_b = N("onesrow_f"), N("onescol_b"), N("id_f"), N("id_b")
    a1sb, a1sb2 = N("a1sb"), N("a1sb2")
    sc8 = N("a1sc2")
    sc8b = N("a1sc3")
    fsel = N("a1sc")

    def R(c0, c1):
        return rts[0:1, c0:c1]

    def bcast_row(row, dstcols, tag):
        P.wait(tn, ve)
        P.wait(tn, ae)
        P.inc(tn.matmul(pb[2][:, 32:40], onesrow_f, row, start=True, stop=True), pe, 1)
        P.wait(sc, pe)
        P.inc(sc.activation(out=dstcols, in_=pb[2][:, 32:40], func=AF.Copy), ae, 1)

    def counts_of_blocks(dst):
        vc.tensor_reduce(rtw[:, 1024:1032], rtw2[:, 0:1024].rearrange("p (e c) -> p e c", c=128),
                         axis=AX.X, op=ALU.add)
        vc.drain()
        P.inc(vc.tensor_copy(a1sb[:, 0:8], rtw[:, 1024:1032]), ve, 1)
        P.wait(tn, ve)
        P.wait(tn, ae)
        P.inc(tn.matmul(pb[2][0:1, 40:48], onescol_b, a1sb[:, 0:8], start=True, stop=True), pe, 1)
        P.wait(sc, pe)
        P.inc(sc.activation(out=dst, in_=pb[2][0:1, 40:48], func=AF.Copy), ae, 1)

    P.wait(vc, ae)
    P.wait(vc, ld)
    vc.tensor_scalar(rtw[:, 1024:1088], prob_g[:, 0:64], 1.0, None, ALU.add)
    vc.drain()
    vc.tensor_scalar(rtw2[:, 1024:1088], prob_g[:, 64:128], 1.0, None, ALU.add)
    vc.drain()
    P.inc(vc.engine_nop(), ve, 1)
    for e in range(NE):
        vc.tensor_single_scalar(rtw2[:, e * 128:(e + 1) * 128], idx_g, float(e), ALU.is_equal)
        vc.drain()
    P.inc(vc.engine_nop(), ve, 1)
    for e in range(NE):
        vc.tensor_tensor(rtw[:, e * 128:e * 128 + 64], rtw2[:, e * 128:e * 128 + 64],
                         rtw[:, 1024:1088], ALU.mult)
        vc.drain()
        vc.tensor_tensor(rtw[:, e * 128 + 64:(e + 1) * 128], rtw2[:, e * 128 + 64:(e + 1) * 128],
                         rtw2[:, 1024:1088], ALU.mult)
        vc.drain()
    vc.tensor_scalar(rtw[:, 0:1024], rtw[:, 0:1024], -1.0, None, ALU.add)
    vc.drain()
    P.inc(vc.engine_nop(), ve, 1)
    counts_of_blocks(R(0, 8))

    P.wait(vc, ae)
    vc.memset(R(8, 16), 0.0)
    vc.memset(R(16, 24), 1.01)
    vc.drain()
    P.inc(vc.engine_nop(), ve, 1)
    for it in range(27):
        vc.tensor_tensor(R(24, 32), R(8, 16), R(16, 24), ALU.add)
        vc.drain()
        P.inc(vc.tensor_scalar(R(24, 32), R(24, 32), 0.5, None, ALU.mult), ve, 1)
        bcast_row(R(24, 32), sc8[:, 0:8], "mid")
        P.wait(vc, ae)
        for e in range(NE):
            vc.tensor_scalar(rtw2[:, e * 128:(e + 1) * 128], rtw[:, e * 128:(e + 1) * 128],
                             sc8[:, e:e + 1], None, ALU.is_gt)
            vc.drain()
        P.inc(vc.engine_nop(), ve, 1)
        counts_of_blocks(R(32, 40))
        P.wait(vc, ae)
        vc.tensor_single_scalar(R(40, 48), R(32, 40), float(CAP), ALU.is_gt)
        vc.drain()
        vc.tensor_tensor(R(48, 56), R(24, 32), R(8, 16), ALU.subtract)
        vc.drain()
        vc.tensor_tensor(R(48, 56), R(48, 56), R(40, 48), ALU.mult)
        vc.drain()
        vc.tensor_tensor(R(8, 16), R(8, 16), R(48, 56), ALU.add)
        vc.drain()
        vc.tensor_tensor(R(48, 56), R(16, 24), R(24, 32), ALU.subtract)
        vc.drain()
        vc.tensor_tensor(R(48, 56), R(48, 56), R(40, 48), ALU.mult)
        vc.drain()
        vc.tensor_tensor(R(16, 24), R(24, 32), R(48, 56), ALU.add)
        vc.drain()
        P.inc(vc.engine_nop(), ve, 1)

    bcast_row(R(16, 24), sc8[:, 0:8], "hi")
    P.wait(vc, ae)
    for e in range(NE):
        vc.tensor_scalar(rtw2[:, e * 128:(e + 1) * 128], rtw[:, e * 128:(e + 1) * 128],
                         sc8[:, e:e + 1], None, ALU.is_le)
        vc.drain()
        vc.tensor_scalar(sc8b[:, 0:128], rtw[:, e * 128:(e + 1) * 128], 1.0, None, ALU.add)
        vc.drain()
        vc.tensor_tensor(sc8b[:, 128:256], sc8b[:, 0:128], rtw2[:, e * 128:(e + 1) * 128], ALU.mult)
        vc.drain()
        vc.tensor_scalar(rtw2[:, e * 128:(e + 1) * 128], sc8b[:, 128:256], -1.0, None, ALU.add)
        vc.drain()
    vc.tensor_reduce(rtw[:, 1024:1032], rtw2[:, 0:1024].rearrange("p (e c) -> p e c", c=128),
                     axis=AX.X, op=ALU.max)
    vc.drain()
    P.inc(vc.engine_nop(), ve, 1)
    P.wait(tn, ve)
    P.wait(tn, ae)
    P.inc(tn.transpose(pb[2][0:8, 48:176], rtw[:, 1024:1032], id_f), pe, 1)
    P.wait(vc, pe)
    P.inc(vc.reduce_max(rtw[0:8, 1032:1033], pb[2][0:8, 48:176], axis=AX.X), ve, 1)
    P.wait(tn, ve)
    P.inc(tn.transpose(pb[2][0:1, 0:8], rtw[0:8, 1032:1033], id_f[0:8, 0:8]), pe, 1)
    P.wait(sc, pe)
    P.inc(sc.activation(out=R(56, 64), in_=pb[2][0:1, 0:8], func=AF.Copy), ae, 1)

    bcast_row(R(56, 64), sc8[:, 0:8], "theta")
    P.wait(vc, ae)
    for e in range(NE):
        vc.tensor_scalar(rtw2[:, e * 128:(e + 1) * 128], rtw[:, e * 128:(e + 1) * 128],
                         sc8[:, e:e + 1], None, ALU.is_gt)
        vc.drain()
    P.inc(vc.engine_nop(), ve, 1)
    counts_of_blocks(R(64, 72))
    P.wait(vc, ae)
    vc.tensor_scalar(R(72, 80), R(64, 72), -1.0, float(CAP), ALU.mult, ALU.add)
    vc.tensor_single_scalar(R(80, 88), R(0, 8), float(CAP), ALU.is_le)
    vc.drain()
    P.inc(vc.engine_nop(), ve, 1)
    bcast_row(R(72, 80), sc8[:, 8:16], "quota")
    bcast_row(R(80, 88), sc8[:, 16:24], "cond")

    P.wait(vc, ae)
    vc.memset(fsel[:, 0:128], 0.0)
    vc.drain()
    P.inc(vc.engine_nop(), ve, 1)
    for e in range(NE):
        P.wait(vc, ae)
        vc.tensor_scalar(a1sb2[:, 0:128], rtw[:, e * 128:(e + 1) * 128],
                         sc8[:, e:e + 1], None, ALU.is_equal)
        vc.drain()
        P.inc(vc.engine_nop(), ve, 1)
        P.wait(tn, ve)
        P.wait(tn, ae)
        P.inc(tn.matmul(pb[0][:, 0:128], tri_i, a1sb2[:, 0:128], start=True, stop=True), pe, 1)
        P.inc(tn.matmul(pb[1][0:1, 0:128], onescol_b, a1sb2[:, 0:128], start=True, stop=True), pe, 1)
        P.wait(sc, pe)
        P.inc(sc.activation(out=a1sb2[0:1, 128:256], in_=pb[1][0:1, 0:128], func=AF.Copy), ae, 1)
        P.wait(tn, ae)
        P.inc(tn.transpose(pEb[:, 0:1], a1sb2[0:1, 128:256], id_b[0:1, 0:1]), pe, 1)
        P.wait(vc, pe)
        P.inc(vc.tensor_copy(a1sb2[:, 256:257], pEb[:, 0:1]), ve, 1)
        P.wait(tn, ve)
        P.inc(tn.matmul(pb[1][:, 128:129], tri_s, a1sb2[:, 256:257], start=True, stop=True), pe, 1)
        P.wait(vc, pe)
        P.inc(vc.tensor_copy(a1sb2[:, 257:258], pb[1][:, 128:129]), ve, 1)
        P.wait(tn, ve)
        P.inc(tn.transpose(pEb[0:1, 4:132], a1sb2[:, 257:258], id_b), pe, 1)
        P.wait(sc, pe)
        P.inc(sc.activation(out=a1sb2[0:1, 258:386], in_=pEb[0:1, 4:132], func=AF.Copy), ae, 1)
        P.wait(tn, ae)
        P.inc(tn.matmul(pb[1][:, 256:384], N("onesrow_b"), a1sb2[0:1, 258:386], start=True, stop=True), pe, 1)
        P.wait(vc, pe)
        vc.tensor_copy(sc8b[:, 0:128], pb[0][:, 0:128])
        vc.drain()
        vc.tensor_tensor(sc8b[:, 128:256], sc8b[:, 0:128], pb[1][:, 256:384], ALU.add)
        vc.drain()
        vc.tensor_scalar(sc8b[:, 256:384], sc8b[:, 128:256], sc8[:, 8 + e:8 + e + 1], None, ALU.is_le)
        vc.drain()
        vc.tensor_scalar(sc8b[:, 384:512], rtw[:, e * 128:(e + 1) * 128],
                         sc8[:, e:e + 1], None, ALU.is_equal)
        vc.drain()
        vc.tensor_tensor(sc8b[:, 256:384], sc8b[:, 256:384], sc8b[:, 384:512], ALU.mult)
        vc.drain()
        vc.tensor_tensor(sc8b[:, 256:384], sc8b[:, 256:384], rtw2[:, e * 128:(e + 1) * 128], ALU.add)
        vc.drain()
        vc.tensor_scalar(sc8b[:, 384:512], sc8b[:, 256:384], -1.0, 1.0, ALU.mult, ALU.add)
        vc.drain()
        vc.tensor_scalar(sc8b[:, 384:512], sc8b[:, 384:512], sc8[:, 16 + e:16 + e + 1], None, ALU.mult)
        vc.drain()
        vc.tensor_tensor(sc8b[:, 256:384], sc8b[:, 256:384], sc8b[:, 384:512], ALU.add)
        vc.drain()
        vc.tensor_single_scalar(sc8b[:, 384:512], idx_g, float(e), ALU.is_equal)
        vc.drain()
        vc.tensor_tensor(sc8b[:, 384:512], sc8b[:, 384:512], sc8b[:, 256:384], ALU.mult)
        vc.drain()
        vc.tensor_tensor(fsel[:, 0:128], fsel[:, 0:128], sc8b[:, 384:512], ALU.add)
        vc.drain()
        P.inc(vc.engine_nop(), ve, 1)
    vc.tensor_tensor(fe_g, idx_g, fsel[:, 0:128], ALU.mult)
    vc.drain()
    for e in range(NE):
        vc.tensor_single_scalar(mask_b[:, e * 16:(e + 1) * 16], fe_g[:, 0:16], float(e), ALU.is_equal)
        vc.drain()
    P.inc(vc.engine_nop(), ve, 1)


def emit_a2(P, E):
    ld, gp, pe, ve, ae = (E["S_ld"], E["S_gp"], E["S_pe"], E["S_ve"], E["S_ae"])
    sy, gs, tn, vc, sc = P.sync, P.gpsimd, P.tensor, P.vector, P.scalar
    pb, pE = E["pb"], E["pE"]
    N = lambda k: E[k]
    w1aT, w2aTq, Ha = N("w1aT"), N("w2aTq"), N("Ha")
    x2T, x2f, x3f = N("x2T"), N("x2f"), N("x3f")
    a2sc, a2sb, statB = N("a2sc"), N("a2sb"), N("statB")
    bias_c = N("bias_c")
    onesrow_f, onescol_b = N("onesrow_f"), N("onescol_b")
    x2_d, x3_d, ya_d = N("x2_d"), N("x3_d"), N("ya_d")

    P.wait(sy, ve)
    P.wait(sy, ae)
    for kc in range(8):
        P.inc(sy.dma_start(out=w1aT[:, kc * DFF:(kc + 1) * DFF],
                           in_=E["wb_d"]["w1a"][:, kc * 128:(kc + 1) * 128], transpose=True), ld, 16)
    LD_W1A = P.val(ld)
    P.wait(tn, ld, LD_W1A)

    WQ_PE = {}
    for t5 in range(4):
        HEV = {0: 0, 1: 0}
        for ffc in range(32):
            pbuf = pb[ffc % 2]
            P.wait(tn, ae, HEV[ffc % 2])
            lastm = None
            for kc in range(8):
                lastm = tn.matmul(pbuf, w1aT[:, kc * DFF + ffc * 128: kc * DFF + (ffc + 1) * 128],
                                  x2T[:, kc * BT + t5 * 512: kc * BT + (t5 + 1) * 512],
                                  start=(kc == 0), stop=(kc == 7))
            P.inc(lastm, pe, 1)
            P.wait(sc, pe)
            P.inc(sc.activation(out=Ha[:, ffc * 512:(ffc + 1) * 512], in_=pbuf, func=AF.Relu,
                                bias=bias_c["b1a"][:, ffc:ffc + 1]), ae, 1)
            HEV[ffc % 2] = P.val(ae)
        AE_H = P.val(ae)
        ytile = {0: pb[0], 1: pb[1], 2: pb[2], 3: pE[:, 0:512], 4: pE[:, 512:1024],
                 5: pE[:, 1024:1536], 6: pE[:, 1536:2048], 7: pE[:, 2048:2560]}
        P.wait(tn, ae, AE_H)
        P.wait(tn, ve)
        X2L = {}
        VE_X3 = [0]
        for q in range(4):
            P.wait(sy, pe, WQ_PE.get(q % 2, 0))
            for f2 in range(8):
                ffc = q * 8 + f2
                P.inc(sy.dma_start(out=w2aTq[q % 2][:, f2 * 1024:(f2 + 1) * 1024],
                                   in_=E["wb_d"]["w2a"][:, ffc * 128:(ffc + 1) * 128],
                                   transpose=True), ld, 16)
            WQL = P.val(ld)
            P.wait(tn, ld, WQL)
            lastm = None
            for dc in range(8):
                for f2 in range(8):
                    ffc = q * 8 + f2
                    lastm = tn.matmul(ytile[dc], w2aTq[q % 2][:, f2 * 1024 + dc * 128: f2 * 1024 + (dc + 1) * 128],
                                      Ha[:, ffc * 512:(ffc + 1) * 512],
                                      start=(q == 0 and f2 == 0), stop=(q == 3 and f2 == 7))
            P.inc(lastm, pe, 1)
            WQ_PE[q % 2] = P.val(pe)
        for dc in range(8):
            P.wait(sy, ve)
            P.inc(sy.dma_start(out=x2f[dc % 2], in_=x2_d[:, dc * BT + t5 * 512: dc * BT + (t5 + 1) * 512]), ld, 16)
            X2L[dc % 2] = P.val(ld)
            P.wait(sc, pe)
            P.wait(sc, ve, VE_X3[0])
            P.inc(sc.activation(out=a2sc[:, 0:512], in_=ytile[dc], func=AF.Identity,
                                bias=bias_c["b2a"][:, dc:dc + 1]), ae, 1)
            P.wait(vc, ae)
            P.wait(vc, ld, X2L[dc % 2])
            vc.tensor_tensor(x3f[dc % 2], a2sc[:, 0:512], x2f[dc % 2], ALU.add)
            vc.drain()
            vc.tensor_copy(Ha[:, dc * 512:(dc + 1) * 512], x3f[dc % 2])
            P.inc(vc.engine_nop(), ve, 1)
            VE_X3[0] = P.val(ve)
            P.wait(sy, ve)
            P.inc(sy.dma_start(out=x3_d[:, dc * BT + t5 * 512: dc * BT + (t5 + 1) * 512], in_=x3f[dc % 2]), ld, 16)
            P.wait(sc, ve)
            P.inc(sc.activation(out=Ha[:, 4096 + dc * 512:4096 + (dc + 1) * 512],
                                in_=x3f[dc % 2], func=AF.Square), ae, 1)
        P.wait(tn, ve)
        P.wait(tn, ae)
        for dc in range(8):
            tn.matmul(pb[0][0:1, :], onescol_b, Ha[:, dc * 512:(dc + 1) * 512],
                      start=(dc == 0), stop=(dc == 7))
            lastm = tn.matmul(pb[1][0:1, :], onescol_b, Ha[:, 4096 + dc * 512:4096 + (dc + 1) * 512],
                              start=(dc == 0), stop=(dc == 7))
        P.inc(lastm, pe, 1)
        P.wait(vc, pe)
        st = statB
        vc.tensor_scalar(st[0:1, 0:512], pb[0][0:1, :], 1.0 / D, None, ALU.mult)
        vc.tensor_scalar(st[0:1, 512:1024], pb[1][0:1, :], 1.0 / D, None, ALU.mult)
        vc.drain()
        vc.tensor_tensor(st[0:1, 1024:1536], st[0:1, 0:512], st[0:1, 0:512], ALU.mult)
        vc.drain()
        P.inc(vc.tensor_tensor(st[0:1, 512:1024], st[0:1, 512:1024], st[0:1, 1024:1536], ALU.subtract), ve, 1)
        P.wait(sc, ve)
        sc.activation(out=st[0:1, 1024:1536], in_=st[0:1, 512:1024], func=AF.Ln, bias=LN_EPS)
        P.inc(sc.activation(out=st[0:1, 1536:2048], in_=st[0:1, 1024:1536], func=AF.Exp, scale=-0.5), ae, 1)
        P.wait(tn, ae)
        tn.matmul(pE[:, 0:512], onesrow_f, st[0:1, 0:512], start=True, stop=True)
        P.inc(tn.matmul(pE[:, 512:1024], onesrow_f, st[0:1, 1536:2048], start=True, stop=True), pe, 1)
        AFF2 = 0
        for dc in range(8):
            P.wait(sy, ve)
            P.inc(sy.dma_start(out=x3f[dc % 2], in_=x3_d[:, dc * BT + t5 * 512: dc * BT + (t5 + 1) * 512]), ld, 16)
            P.wait(vc, ld)
            P.wait(vc, pe)
            P.wait(vc, ae, AFF2)
            vc.tensor_tensor(a2sc[:, 0:512], x3f[dc % 2], pE[:, 0:512], ALU.subtract)
            vc.tensor_tensor(a2sc[:, 512:1024], a2sc[:, 0:512], pE[:, 512:1024], ALU.mult)
            P.inc(vc.engine_nop(), ve, 1)
            P.wait(sc, ve)
            P.inc(sc.activation(out=a2sc[:, 0:512], in_=a2sc[:, 512:1024], func=AF.Identity,
                                scale=bias_c["g3"][:, dc:dc + 1], bias=bias_c["be3"][:, dc:dc + 1]), ae, 1)
            AFF2 = P.val(ae)
            P.wait(sy, ae)
            P.inc(sy.dma_start(out=ya_d[:, dc * BT + t5 * 512: dc * BT + (t5 + 1) * 512], in_=a2sc[:, 0:512]), ld, 16)
    P.wait(sy, ld)


def emit_moe(P, E):
    ld, gp, pe, ve, ae = (E["S_ld"], E["S_gp"], E["S_pe"], E["S_ve"], E["S_ae"])
    sy, gs, tn, vc, sc = P.sync, P.gpsimd, P.tensor, P.vector, P.scalar
    pb, pE = E["pb"], E["pE"]
    N = lambda k: E[k]
    w1sb, w2sb, Hm = N("w1sb"), N("w2sb"), N("Hm")
    xnm, yst, zt = N("xnm"), N("yst"), N("zt")
    b1m_sb, b2m_sb = N("b1m_sb"), N("b2m_sb")
    mask_b = N("mask_b")
    xn_own_d, ym_d = N("xn_own_d"), N("ym_d")
    onesrow_b = N("onesrow_b")

    P.wait(gs, pe)
    P.wait(gs, ve)
    P.wait(gs, ae)
    P.wait(gs, ld)
    P.wait(vc, ae)
    P.inc(vc.memset(zt, 0.0), ve, 1)
    P.wait(sy, ve)
    for r in range(16):
        P.inc(sy.dma_start(out=ym_d[r * 128:(r + 1) * 128, 0:512], in_=zt), ld, 16)
        P.inc(sy.dma_start(out=ym_d[r * 128:(r + 1) * 128, 512:1024], in_=zt), ld, 16)
    ZRO = P.val(ld)
    P.inc(gs.dma_start(out=w1sb.rearrange("p (a f) -> p a f", f=DFF), in_=E["w1m_in"][0:D, :].rearrange("(a p) f -> p a f", p=128)), gp, 16)
    P.inc(gs.dma_start(out=b1m_sb[0], in_=E["b1m_in"][0:128, :]), gp, 16)
    W1_RDY = {0: P.val(gp)}
    P.inc(gs.dma_start(out=w2sb.rearrange("p (a f) -> p a f", f=D), in_=E["w2m_in"][0:DFF, :].rearrange("(a p) f -> p a f", p=128)), gp, 16)
    P.inc(gs.dma_start(out=b2m_sb[0], in_=E["b2m_in"][0:1, :]), gp, 16)
    W2_RDY = {0: P.val(gp)}
    P.wait(gs, ld, ZRO)
    XN_LD = {}
    Y_ST = {0: 0, 1: 0}
    Y_DMA = {0: 0, 1: 0}
    for e in range(NE):
        eb = e % 2
        for t5 in range(4):
            i = t5 % 2
            P.wait(sy, ve)
            P.inc(sy.dma_start(out=xnm[i],
                               in_=xn_own_d.rearrange("p (k t) -> p k t", t=BT)[:, :, t5 * 512:(t5 + 1) * 512]),
                  ld, 16)
            XN_LD[i] = P.val(ld)
            HEV = {0: 0, 1: 0}
            P.wait(tn, ld, XN_LD[i])
            P.wait(tn, gp, W1_RDY[e])
            for ffc in range(32):
                pbuf = pb[ffc % 2]
                P.wait(tn, ae, HEV[ffc % 2])
                lastm = None
                for kc in range(8):
                    lastm = tn.matmul(pbuf, w1sb[:, kc * DFF + ffc * 128: kc * DFF + (ffc + 1) * 128],
                                      xnm[i][:, kc * 512:(kc + 1) * 512],
                                      start=(kc == 0), stop=(kc == 7))
                P.inc(lastm, pe, 1)
                P.wait(sc, pe)
                P.inc(sc.activation(out=Hm[:, ffc * 512:(ffc + 1) * 512], in_=pbuf, func=AF.Relu,
                                    bias=b1m_sb[eb][:, ffc:ffc + 1]), ae, 1)
                HEV[ffc % 2] = P.val(ae)
            if e + 1 < NE and t5 == 3:
                P.wait(gs, ae)
                P.inc(gs.dma_start(out=w1sb.rearrange("p (a f) -> p a f", f=DFF),
                                   in_=E["w1m_in"][(e + 1) * D:(e + 2) * D, :].rearrange(
                    "(a p) f -> p a f", p=128)), gp, 16)
                P.inc(gs.dma_start(out=b1m_sb[1 - eb], in_=E["b1m_in"][(e + 1) * 128:(e + 2) * 128, :]), gp, 16)
                W1_RDY[e + 1] = P.val(gp)
            P.wait(tn, ae)
            P.wait(tn, gp, W2_RDY[e])
            for t1 in range(4):
                for dh in range(2):
                    j = (t1 * 2 + dh) % 2
                    pbuf = pb[j]
                    P.wait(tn, ve, Y_ST[j])
                    lastm = None
                    for ffc in range(32):
                        lastm = tn.matmul(pbuf, Hm[:, ffc * 512 + t1 * 128: ffc * 512 + (t1 + 1) * 128],
                                          w2sb[:, ffc * 1024 + dh * 512: ffc * 1024 + (dh + 1) * 512],
                                          start=(ffc == 0), stop=False)
                    lastm = tn.matmul(pbuf, onesrow_b, b2m_sb[eb][0:1, dh * 512:(dh + 1) * 512],
                                      start=False, stop=True)
                    P.inc(lastm, pe, 1)
                    tcg = t5 * 4 + t1
                    P.wait(vc, pe)
                    P.wait(vc, gp, Y_DMA[j])
                    P.inc(vc.tensor_scalar(yst[j], pbuf, mask_b[:, e * 16 + tcg:e * 16 + tcg + 1],
                                           None, ALU.mult), ve, 1)
                    Y_ST[j] = P.val(ve)
                    P.wait(gs, ve, Y_ST[j])
                    P.inc(gs.dma_start(out=ym_d[tcg * 128:(tcg + 1) * 128, dh * 512:(dh + 1) * 512],
                                       in_=yst[j], accum_op=ALU.add), gp, 16)
                    Y_DMA[j] = P.val(gp)
        if e + 1 < NE:
            P.wait(gs, pe)
            P.inc(gs.dma_start(out=w2sb.rearrange("p (a f) -> p a f", f=D),
                               in_=E["w2m_in"][(e + 1) * DFF:(e + 2) * DFF, :].rearrange(
                "(a p) f -> p a f", p=128)), gp, 16)
            P.inc(gs.dma_start(out=b2m_sb[1 - eb], in_=E["b2m_in"][e + 1:e + 2, :]), gp, 16)
            W2_RDY[e + 1] = P.val(gp)
    P.wait(gs, gp)


def emit_fdebug(P, E):
    ld, ve = E["S_ld"], E["S_ve"]
    sy = P.sync
    dbg = E["dbg"]
    P.wait(sy, ve)
    P.inc(sy.dma_start(out=dbg["x2"], in_=E["x2_d"]), ld, 16)
    P.inc(sy.dma_start(out=dbg["x3"], in_=E["x3_d"]), ld, 16)
    P.inc(sy.dma_start(out=dbg["ya"], in_=E["ya_d"]), ld, 16)
    P.inc(sy.dma_start(out=dbg["ym"], in_=E["ym_d"]), ld, 16)
    P.inc(sy.dma_start(out=dbg["fe"], in_=E["fe_g"]), ld, 16)
    P.inc(sy.dma_start(out=dbg["xnd"], in_=E["xn_own_d"]), ld, 16)
    P.inc(sy.dma_start(out=dbg["maskb"], in_=E["mask_b"]), ld, 16)
    P.wait(sy, ld)


def emit_final(P, E):
    ld, gp, pe, ve, ae = (E["S_ld"], E["S_gp"], E["S_pe"], E["S_ve"], E["S_ae"])
    sy, gs, tn, vc, sc = P.sync, P.gpsimd, P.tensor, P.vector, P.scalar
    pb = E["pb"]
    N = lambda k: E[k]
    fx, fya, fym, fout = N("fx"), N("fya"), N("fym"), N("fout")
    x_in, ya_d, ym_d, out = N("x_in"), N("ya_d"), N("ym_d"), N("out")
    id_f = N("id_f")

    P.wait(sy, gp)
    P.wait(sy, ve)
    FOUT = {0: 0, 1: 0}
    for t1 in range(16):
        i = t1 % 2
        P.wait(sy, ld, FOUT[i])
        P.inc(sy.dma_start(out=fx[i], in_=x_in[t1 * 128:(t1 + 1) * 128, :]), ld, 16)
        P.inc(sy.dma_start(out=fym[i], in_=ym_d[t1 * 128:(t1 + 1) * 128, :]), ld, 16)
        P.inc(sy.dma_start(out=fya[i],
                           in_=ya_d.rearrange("p (k t) -> p k t", t=BT)[:, :, t1 * 128:(t1 + 1) * 128]), ld, 16)
        LD_F = P.val(ld)
        P.wait(vc, ld, LD_F)
        P.inc(vc.tensor_tensor(fout[i], fx[i], fym[i], ALU.add), ve, 1)
        P.wait(tn, ld, LD_F)
        P.wait(tn, ve)
        for dc in range(8):
            P.wait(tn, ve)
            P.inc(tn.transpose(pb[0][:, 0:128] if dc % 2 == 0 else pb[1][:, 0:128],
                               fya[i][:, dc * 128:(dc + 1) * 128], id_f), pe, 1)
            P.wait(vc, pe)
            P.inc(vc.tensor_tensor(fout[i][:, dc * 128:(dc + 1) * 128], fout[i][:, dc * 128:(dc + 1) * 128],
                                   pb[0][:, 0:128] if dc % 2 == 0 else pb[1][:, 0:128], ALU.add), ve, 1)
        P.wait(sy, ve)
        P.inc(sy.dma_start(out=out[t1 * 128:(t1 + 1) * 128, :], in_=fout[i]), ld, 16)
        FOUT[i] = P.val(ld)
    P.wait(sy, ld)
    P.wait(gs, ld)


_BUILD_CACHE = {}


def _get_program(debug=False):
    key = bool(debug)
    if key not in _BUILD_CACHE:
        _BUILD_CACHE[key] = build_program(debug=debug)
    return _BUILD_CACHE[key]


def _col_layout(b, nch):
    return np.ascontiguousarray(np.asarray(b, np.float32).reshape(nch, 128).T)


def make_inputs(inputs, debug=False):
    x = np.ascontiguousarray(np.asarray(inputs["L0_attn_video"], np.float32).reshape(T, D))
    ident = np.eye(128, dtype=np.float32)
    identb = np.eye(128, dtype=BFNP)
    iotam = np.tile(np.arange(NE, dtype=np.float32) - BIG, (128, 1))
    onescb = np.ones((128, 1), BFNP)
    onescf = np.ones((128, 1), np.float32)
    onesrb = np.ones((1, 128), BFNP)
    onesrf = np.ones((1, 128), np.float32)
    halves = np.zeros((2, 128), np.float32)
    halves[0, 0:64] = 1.0
    halves[1, 64:128] = 1.0
    indh = np.zeros((128, 128), np.float32)
    for dc in range(8):
        for h16 in range(16):
            if h16 == 2 * dc:
                indh[0:64, dc * 16 + h16] = 0.5 * NORM * NORM
            elif h16 == 2 * dc + 1:
                indh[64:128, dc * 16 + h16] = 0.5 * NORM * NORM
    indh = indh.astype(BFNP)
    k = np.arange(128)
    tri_i = (k[:, None] <= k[None, :]).astype(BFNP)
    tri_s = (k[:, None] < k[None, :]).astype(BFNP)
    salt = int(np.random.randint(1 << 20, 1 << 30))
    salts = (salt + np.arange(64, dtype=np.int64)).astype(np.int32).reshape(1, 64)

    base = {
        "wq": np.asarray(inputs["L0_attn_Wq"], np.float32),
        "wk": np.asarray(inputs["L0_attn_Wk"], np.float32),
        "wv": np.asarray(inputs["L0_attn_Wv"], np.float32),
        "wo": np.asarray(inputs["L0_attn_Wo"], np.float32),
        "w1a": np.asarray(inputs["L0_attn_W1"], np.float32),
        "w2a": np.asarray(inputs["L0_attn_W2"], np.float32),
        "bq": _col_layout(inputs["L0_attn_bq"], 8),
        "bk": _col_layout(inputs["L0_attn_bk"], 8),
        "bo": _col_layout(inputs["L0_attn_bo"], 8),
        "b2a": _col_layout(inputs["L0_attn_b2"], 8),
        "g2": _col_layout(inputs["L0_attn_g2"], 8),
        "be2": _col_layout(inputs["L0_attn_be2"], 8),
        "g3": _col_layout(inputs["L0_attn_g3"], 8),
        "be3": _col_layout(inputs["L0_attn_be3"], 8),
        "b1a": _col_layout(inputs["L0_attn_b1"], 32),
        "bv_row": np.asarray(inputs["L0_attn_bv"], np.float32).reshape(1, D),
        "proj_in": np.asarray(inputs["L0_attn_proj"], np.float32),
        "rw_in": np.asarray(inputs["L0_ffn_router_w"], np.float32),
        "w1m": np.ascontiguousarray(np.asarray(inputs["L0_ffn_W1"], np.float32).reshape(NE * D, DFF)),
        "b1m": np.concatenate([_col_layout(np.asarray(inputs["L0_ffn_b1"], np.float32)[e], 32)
                               for e in range(NE)], axis=0),
        "w2m": np.ascontiguousarray(np.asarray(inputs["L0_ffn_W2"], np.float32).reshape(NE * DFF, D)),
        "b2m": np.asarray(inputs["L0_ffn_b2"], np.float32),
        "ident_in": ident, "identb_in": identb, "iotam_in": iotam,
        "iotam4_in": np.tile(iotam, (1, 4)),
        "onescb_in": onescb, "onescf_in": onescf, "onesrb_in": onesrb,
        "onesrf_in": onesrf, "halves_in": halves, "indh_in": indh,
        "tri_i_in": tri_i, "tri_s_in": tri_s, "salts_in": salts,
    }
    in_maps = []
    for c in range(NCORES):
        slots = [(c + 2 * j) % 8 for j in range(NSLOT)]
        xi = np.concatenate([x[s * BT:(s + 1) * BT] for s in slots], axis=0)
        m = dict(base)
        m["x_in"] = np.ascontiguousarray(xi)
        m["parity_in"] = np.full((1, 1), c % 2, np.int32)
        in_maps.append(m)
    return in_maps


def kernel(**inputs):
    nc, dbg = _get_program(debug=False)
    in_maps = make_inputs(inputs)
    res = run_bass_kernel_spmd(nc, in_maps, core_ids=list(range(NCORES)))
    outp = np.empty((T, D), np.float32)
    for c in range(NCORES):
        outp[c * BT:(c + 1) * BT] = np.asarray(res.results[c]["out"], np.float32)
    return outp.reshape(1, T, D)


if __name__ == "__main__":
    nc, dbg = build_program(debug=False)
    n = sum(1 for _ in nc.all_instructions())
    print("built ok, instructions:", n)


# version 07
# speedup vs baseline: 1.3047x; 1.1300x over previous
import contextlib
import numpy as np
import ml_dtypes
import concourse.bass as bass
import concourse.mybir as mybir
from concourse.bass_primitives import MemorySpace
from concourse.bass_utils import run_bass_kernel_spmd

F32 = mybir.dt.float32
BF16 = mybir.dt.bfloat16
I32 = mybir.dt.int32
U8 = mybir.dt.uint8
AF = mybir.ActivationFunctionType
ALU = mybir.AluOpType
AX = mybir.AxisListType
BFNP = ml_dtypes.bfloat16

T, D, HEADS, DH, M, DFF, NE = 16384, 1024, 16, 64, 256, 4096, 8
BT = 2048
HB = 1024
NSLOT = 4
CAP = 2560
EPS_K = 1e-4
LN_EPS = 1e-5
NORM = float(DH) ** -0.25
RATIO = float(M) ** -0.5
LNR = float(np.log(RATIO))
BIG = 1.0e4
KB = 1024
NCORES = 8
ENGINES = ("sync", "gpsimd", "tensor", "vector", "scalar")


class _Dummy:
    def then_inc(self, *a, **k):
        return self


_DUMMY = _Dummy()


class EngProxy:

    def __init__(self, P, role):
        self._P = P
        self._role = role

    @property
    def live(self):
        return self._P.active == self._role

    def __getattr__(self, name):
        P = self._P
        if P.active == self._role:
            return getattr(P.real, name)

        def _skip(*a, **k):
            return _DUMMY

        return _skip


class Prog:

    def __init__(self, active, real, sems):
        self.active = active
        self.real = real
        self.sems = sems
        self.cnt = {}
        for r in ENGINES:
            setattr(self, r, EngProxy(self, r))

    def inc(self, instr, sem, n):
        if instr is not _DUMMY:
            instr.then_inc(sem, n)
        self.cnt[sem.num] = self.cnt.get(sem.num, 0) + n

    def bump(self, sem, n):
        self.cnt[sem.num] = self.cnt.get(sem.num, 0) + n

    def val(self, sem):
        return self.cnt.get(sem.num, 0)

    def wait(self, proxy, sem, v=None):
        if proxy.live:
            self._wait_raw(sem, self.val(sem) if v is None else v)

    def _wait_raw(self, sem, v):
        self.real.wait_ge(sem, v)


def build_program(debug=False):
    nc = bass.Bass("TRN2", target_bir_lowering=False, debug=False)

    x_in = nc.dram_tensor("x_in", [NSLOT * BT, D], F32, kind="ExternalInput").ap()
    W_in = {nm: nc.dram_tensor(nm, shp, F32, kind="ExternalInput").ap()
            for nm, shp in [("wq", [D, D]), ("wk", [D, D]), ("wv", [D, D]),
                            ("wo", [D, D]), ("w1a", [DFF, D]), ("w2a", [D, DFF])]}
    BCsh = {"bq": 8, "bk": 8, "bo": 8, "b2a": 8, "g2": 8, "be2": 8,
            "g3": 8, "be3": 8, "b1a": 32}
    BC = {nm: nc.dram_tensor(nm, [128, n], F32, kind="ExternalInput").ap()
          for nm, n in BCsh.items()}
    bv_row = nc.dram_tensor("bv_row", [1, D], F32, kind="ExternalInput").ap()
    proj_in = nc.dram_tensor("proj_in", [M, DH], F32, kind="ExternalInput").ap()
    rw_in = nc.dram_tensor("rw_in", [NE, D], F32, kind="ExternalInput").ap()
    w1m_in = nc.dram_tensor("w1m", [NE * D, DFF], F32, kind="ExternalInput").ap()
    b1m_in = nc.dram_tensor("b1m", [NE * 128, 32], F32, kind="ExternalInput").ap()
    w2m_in = nc.dram_tensor("w2m", [NE * DFF, D], F32, kind="ExternalInput").ap()
    b2m_in = nc.dram_tensor("b2m", [NE, D], F32, kind="ExternalInput").ap()
    ident_in = nc.dram_tensor("ident_in", [128, 128], F32, kind="ExternalInput").ap()
    identb_in = nc.dram_tensor("identb_in", [128, 128], BF16, kind="ExternalInput").ap()
    iotam_in = nc.dram_tensor("iotam_in", [128, NE], F32, kind="ExternalInput").ap()
    iotam4_in = nc.dram_tensor("iotam4_in", [128, 4 * NE], F32, kind="ExternalInput").ap()
    onescb_in = nc.dram_tensor("onescb_in", [128, 1], BF16, kind="ExternalInput").ap()
    onescf_in = nc.dram_tensor("onescf_in", [128, 1], F32, kind="ExternalInput").ap()
    onesrb_in = nc.dram_tensor("onesrb_in", [1, 128], BF16, kind="ExternalInput").ap()
    onesrf_in = nc.dram_tensor("onesrf_in", [1, 128], F32, kind="ExternalInput").ap()
    indh_in = nc.dram_tensor("indh_in", [128, 128], BF16, kind="ExternalInput").ap()
    tri_i_in = nc.dram_tensor("tri_i_in", [128, 128], BF16, kind="ExternalInput").ap()
    tri_s_in = nc.dram_tensor("tri_s_in", [128, 128], BF16, kind="ExternalInput").ap()
    parity_in = nc.dram_tensor("parity_in", [1, 1], I32, kind="ExternalInput").ap()
    salts_in = nc.dram_tensor("salts_in", [1, 64], I32, kind="ExternalInput").ap()

    out = nc.dram_tensor("out", [BT, D], F32, kind="ExternalOutput").ap()

    dbg = {}

    def dbg_t(name, shape, dt=F32):
        if debug:
            dbg[name] = nc.dram_tensor("dbg_" + name, shape, dt, kind="ExternalOutput").ap()
            return dbg[name]
        return None

    if debug:
        dbg_t("xnT0", [128, 8 * BT], BF16)
        dbg_t("prob", [128, 128])
        dbg_t("idx", [128, 128])
        dbg_t("stab", [1, HEADS])
        dbg_t("ES", [128, 2080])
        dbg_t("vsum", [128, 8])
        dbg_t("ctx", [128, HEADS * 2 * DH], BF16)
        dbg_t("ksum", [128, 2 * HEADS], BF16)
        dbg_t("fe", [128, 128], BF16)
        dbg_t("x2", [128, 8 * BT])
        dbg_t("x3", [128, 8 * BT])
        dbg_t("ya", [128, 8 * BT])
        dbg_t("ym", [BT, D], BF16)
        dbg_t("qTd", [128, 8 * BT], BF16)
        dbg_t("dqc", [128, 16 * HEADS])
        dbg_t("attnT", [128, 8 * 512], BF16)
        dbg_t("s2c", [128, 8 * 512])
        dbg_t("xnd", [128, 8 * BT], BF16)
        dbg_t("maskb", [128, NE * 16])
        dbg_t("xnT0e", [128, 8 * BT], BF16)
        for _s in (1, 2, 3):
            dbg_t(f"xnT0s{_s}", [128, 8 * BT], BF16)
        dbg_t("xnT0m", [128, 8 * BT], BF16)
        dbg_t("diagc", [128, 8 * HEADS])
        dbg_t("dstab", [128, 8 * HEADS])
        dbg_t("ESpre", [128, 2080])
        dbg_t("stabpre", [1, HEADS])

    xhi_d = nc.dram_tensor("xhi_d", [NSLOT * BT, D], BF16).ap()
    xlo_d = nc.dram_tensor("xlo_d", [NSLOT * BT, D], BF16).ap()
    wb_d = {nm: nc.dram_tensor(nm + "b_d", list(W_in[nm].shape), BF16).ap() for nm in W_in}
    x2_d = nc.dram_tensor("x2_d", [128, 8 * BT], F32).ap()
    ya_d = nc.dram_tensor("ya_d", [128, 8 * BT], F32).ap()
    ym_d = nc.dram_tensor("ym_d", [BT, D], BF16).ap()
    xn_own_d = nc.dram_tensor("xn_own_d", [128, 8 * BT], BF16).ap()
    vsum_d = nc.dram_tensor("vsum_d", [1, 1024], F32).ap()
    x3_d = nc.dram_tensor("x3_d", [128, 8 * BT], F32).ap()
    exch_a = nc.dram_tensor("exch_a", [128, 2304], F32, addr_space="Shared").ap()
    exch_b = nc.dram_tensor("exch_b", [128, 2304], F32, addr_space="Shared").ap()
    flag_a = nc.dram_tensor("flag_a", [1, 1], I32, addr_space="Shared").ap()
    flag_b = nc.dram_tensor("flag_b", [1, 1], I32, addr_space="Shared").ap()

    def A(name, shape, dt):
        return nc.alloc_sbuf_tensor(name, shape, dt).ap()

    id_f = A("id_f", [128, 128], F32)
    id_b = A("id_b", [128, 128], BF16)
    iotam = A("iotam", [128, NE], F32)
    iotam4 = A("iotam4", [128, 4 * NE], F32)
    onescol_b = A("onescol_b", [128, 1], BF16)
    onescol_f = A("onescol_f", [128, 1], F32)
    onesrow_b = A("onesrow_b", [1, 128], BF16)
    onesrow_f = A("onesrow_f", [1, 128], F32)
    halves = A("halves", [2, 128], F32)
    indh = A("indh", [128, 128], BF16)
    tri_i = A("tri_i", [128, 128], BF16)
    tri_s = A("tri_s", [128, 128], BF16)
    projT = A("projT", [128, M], BF16)
    rwhT = A("rwhT", [128, 64], BF16)
    rwlT = A("rwlT", [128, 64], BF16)
    s1col = A("s1col", [NE, 1], F32)
    bias_c = {nm: A(nm + "_c", [128, BCsh[nm]], F32) for nm in BCsh}
    par_sb = A("par_sb", [1, 1], I32)
    salt_sb = A("salt_sb", [1, 64], I32)
    flg_sb = A("flg_sb", [1, 1], I32)
    ksum_col = A("ksum_col", [128, 2 * HEADS], BF16)
    epsrow = A("epsrow", [1, 512], BF16)
    FCONSTS = [LN_EPS, -0.5, NORM, LNR, RATIO * EPS_K, -1.0]
    cvals = A("cvals", [128, len(FCONSTS)], F32)
    for _i, _v in enumerate(FCONSTS):
        nc.const_aps.register(F32, float(_v), cvals[:, _i:_i + 1])

    PBASE = (nc.bytes_allocated(MemorySpace.SBUF) + 127) // 128 * 128
    LIMIT = 229376

    def AT(name, shape, dt, off):
        esz = {F32: 4, BF16: 2, U8: 1, I32: 4}[dt]
        szb = int(np.prod(shape[1:])) * esz
        assert PBASE + off + szb <= LIMIT, (name, PBASE, off, szb, PBASE + off + szb)
        return nc.alloc_sbuf_tensor_at(name, shape, dt, offset=PBASE + off).ap()

    def mkcursor(start):
        state = {"o": start}

        def alloc(name, shape, dt):
            esz = {F32: 4, BF16: 2, U8: 1, I32: 4}[dt]
            szb = (int(np.prod(shape[1:])) * esz + 127) // 128 * 128
            t = AT(name, shape, dt, state["o"])
            state["o"] += szb
            return t

        return alloc, state

    ES = AT("ES", [128, 2080], F32, 160 * KB)
    prob_g = AT("prob_g", [128, 128], F32, 168 * KB + 256)
    idx_g = AT("idx_g", [128, 128], F32, 168 * KB + 768)
    x2T = AT("x2T", [128, 8 * BT], BF16, 169 * KB + 256)
    fe_g = AT("fe_g", [128, 128], BF16, 201 * KB + 256)
    mask_b = AT("mask_b", [128, NE * 16], F32, 201 * KB + 512)
    xnT_own = AT("xnT_own", [128, 8 * BT], BF16, 0)

    goff = {}

    def ga(name, shape, dt):
        goff[name] = gst["o"]
        return _ga(name, shape, dt)

    _ga, gst = mkcursor(32 * KB)
    kT_b = ga("kT_b", [128, 8 * HB], BF16)
    xnT_s = ga("xnT_s", [128, 8 * 512], BF16)
    v_b = ga("v_b", [128, 8 * HEADS * 65], BF16)
    wkT = ga("wkT", [128, 8 * 1024], BF16)
    wvT = ga("wvT", [128, 8 * 1024], BF16)
    xq_g = [ga(f"xq_g{i}", [128, D], F32) for i in range(2)]
    hiT_s = ga("hiT_s", [128, 8 * 512], BF16)
    loT_s = ga("loT_s", [128, 8 * 512], BF16)
    xlo_st = ga("xlo_st", [128, 4 * D], BF16)
    diag_c = ga("diag_c", [128, 8 * HEADS], F32)
    dstab = ga("dstab", [128, 8 * HEADS], F32)
    gsc = ga("gsc", [128, 1536], F32)
    gsb = ga("gsb", [128, 1024], BF16)
    stat_c = ga("stat_c", [128, 12], F32)
    stat_r = ga("stat_r", [1, 512], BF16)
    mu_rf = ga("mu_rf", [1, 512], F32)
    maxc = ga("maxc", [128, HEADS], F32)
    stabc = ga("stabc", [128, HEADS], F32)
    p_sc = ga("p_sc", [128, 520], F32)
    srow = ga("srow", [1, 8 * HEADS], F32)
    stab_run = ga("stab_run", [1, HEADS], F32)
    vsum_sb = ga("vsum_sb", [128, 8], F32)
    bv_bc = ga("bv_bc", [128, D], F32)
    v_c = ga("v_c", [128, D], BF16)
    rw_sb = AT("rw_sb", [NE, D], F32, goff["kT_b"])
    rwh_sb = AT("rwh_sb", [NE, D], BF16, goff["kT_b"] + 4 * KB)
    rwl_sb = AT("rwl_sb", [NE, D], BF16, goff["kT_b"] + 6 * KB)
    kp_t = [AT(f"kp_t{i}", [128, M], BF16, goff["xq_g0"] + 512 * i) for i in range(2)]
    k2_t = AT("k2_t", [128, 512], BF16, goff["xq_g0"] + 1024)
    diag_r = AT("diag_r", [16, 512], F32, goff["gsb"])
    vsr_sb = AT("vsr_sb", [1, 1024], F32, goff["xq_g1"])
    exch_w = AT("exch_w", [128, 2304], F32, goff["diag_c"])
    exch_w = AT("exch_w2", [128, 2304], F32, goff["hiT_s"])
    assert gst["o"] <= 160 * KB, gst["o"]

    aoff = {}
    _aa, ast = mkcursor(32 * KB)

    def aa(name, shape, dt):
        aoff[name] = ast["o"]
        return _aa(name, shape, dt)

    qT = aa("qT", [128, 8 * BT], BF16)
    wqT = aa("wqT", [128, 8 * 1024], BF16)
    woT = aa("woT", [128, 8 * 1024], BF16)
    ctx_sb = aa("ctx_sb", [128, HEADS * 2 * DH], BF16)
    qp_t = [aa(f"qp_t{i}", [128, M], BF16) for i in range(2)]
    q2_t = aa("q2_t", [128, 512], BF16)
    diagq_r = aa("diagq_r", [16, 512], F32)
    diagq_c = aa("diagq_c", [128, 16 * HEADS], F32)
    attnT = aa("attnT", [128, 8 * 512], BF16)
    s2c = aa("s2c", [128, 8 * 512], F32)
    a1sc = aa("a1sc", [128, 1024], F32)
    a1sb = aa("a1sb", [128, 1024], BF16)
    a1sb2 = aa("a1sb2", [128, 512], BF16)
    a1sc2 = aa("a1sc2", [128, 512], F32)
    a1sc3 = aa("a1sc3", [128, 512], F32)
    statA = aa("statA", [1, 4 * 512], F32)
    rtw = aa("rtw", [128, 1088], F32)
    rtw2 = aa("rtw2", [128, 1088], F32)
    rts = aa("rts", [1, 256], F32)
    ecol_b = aa("ecol_b", [1, 1024], BF16)
    kse_b = aa("kse_b", [1, 16], BF16)
    kse2_b = aa("kse2_b", [1, 32], BF16)
    qp4 = AT("qp4", [128, 1024], BF16, aoff["qp_t0"])
    assert ast["o"] <= 160 * KB, ast["o"]

    ba, bst = mkcursor(0)
    w1aT = ba("w1aT", [128, 8 * DFF], BF16)
    w2aTq = [ba(f"w2aTq{i}", [128, 8 * 1024], BF16) for i in range(2)]
    Ha = ba("Ha", [128, 32 * 512], BF16)
    x2f = [ba(f"x2f{i}", [128, 512], F32) for i in range(2)]
    x3f = [ba(f"x3f{i}", [128, 512], F32) for i in range(2)]
    a2sc = ba("a2sc", [128, 1024], F32)
    a2sb = ba("a2sb", [128, 1024], BF16)
    statB = ba("statB", [1, 4 * 512], F32)
    assert bst["o"] <= 160 * KB, bst["o"]

    ma, mst = mkcursor(0)
    w1sb = ma("w1sb", [128, 8 * DFF], BF16)
    w2sb = ma("w2sb", [128, 32 * 1024], BF16)
    Hm = ma("Hm", [128, 32 * 512], BF16)
    xnm = [ma(f"xnm{i}", [128, 8 * 512], BF16) for i in range(2)]
    yst = [ma(f"yst{i}", [128, 512], BF16) for i in range(2)]
    b1m_sb = [ma(f"b1m_sb{i}", [128, 32], F32) for i in range(2)]
    b2m_sb = [ma(f"b2m_sb{i}", [1, D], BF16) for i in range(2)]
    zt = ma("zt", [128, 512], BF16)
    assert mst["o"] <= 201 * KB, mst["o"]

    fa, fst = mkcursor(0)
    fx = [fa(f"fx{i}", [128, D], F32) for i in range(2)]
    fya = [fa(f"fya{i}", [128, 8 * 128], F32) for i in range(2)]
    fym = [fa(f"fym{i}", [128, D], BF16) for i in range(2)]
    fout = [fa(f"fout{i}", [128, D], F32) for i in range(2)]
    assert fst["o"] <= 32 * KB

    pb = [nc.alloc_psum_tensor(f"pb{i}", [128, 512], F32).ap() for i in range(3)]
    pE = nc.alloc_psum_tensor("pE", [128, 2560], F32).ap()
    pEb = pE.bitcast(BF16)

    def pES(i):
        b, j = divmod(i, 7)
        return pE[:, b * 512 + j * 65: b * 512 + j * 65 + 65]

    PES_VSUM = pE[:, 4 * 512 + 4 * 65: 4 * 512 + 4 * 65 + 8]
    PES_MU = pE[:, 2048:2560]

    stack = contextlib.ExitStack()
    S_ld = stack.enter_context(nc.semaphore("s_ld"))
    S_gp = stack.enter_context(nc.semaphore("s_gp"))
    S_pe = stack.enter_context(nc.semaphore("s_pe"))
    S_ve = stack.enter_context(nc.semaphore("s_ve"))
    S_ae = stack.enter_context(nc.semaphore("s_ae"))
    S_sp = stack.enter_context(nc.semaphore("s_sp"))
    blk = stack.enter_context(nc.Block())

    env = dict(locals())

    def run_replay(role, real):
        P = Prog(role, real, None)
        emit_program(P, env)

    @blk.sync
    def _(e):
        run_replay("sync", e)

    @blk.gpsimd
    def _(e):
        run_replay("gpsimd", e)

    @blk.tensor
    def _(e):
        run_replay("tensor", e)

    @blk.vector
    def _(e):
        run_replay("vector", e)

    @blk.scalar
    def _(e):
        run_replay("scalar", e)

    stack.close()
    return nc, dbg


class _Scope:

    def __init__(self, E, name):
        self.E, self.name = E, name
        self.sid, _ = E["nc"].enter_named_scope(name, False)

    def end(self):
        self.E["nc"].leave_named_scope(self.name, self.sid, False)


def emit_program(P, E):
    ld, gp, pe, ve, ae, sp = (E["S_ld"], E["S_gp"], E["S_pe"], E["S_ve"],
                              E["S_ae"], E["S_sp"])
    sy, gs, tn, vc, sc = P.sync, P.gpsimd, P.tensor, P.vector, P.scalar
    debug = E["debug"]
    dbg = E["dbg"]
    pb, pE, pEb, pES, PES_VSUM = E["pb"], E["pE"], E["pEb"], E["pES"], E["PES_VSUM"]
    N = lambda k: E[k]
    _s0 = _Scope(E, "Pinit")

    for src, dst in [("ident_in", "id_f"), ("identb_in", "id_b"),
                     ("iotam_in", "iotam"), ("iotam4_in", "iotam4"),
                     ("onescb_in", "onescol_b"),
                     ("onescf_in", "onescol_f"), ("onesrb_in", "onesrow_b"),
                     ("onesrf_in", "onesrow_f"), ("indh_in", "indh"),
                     ("tri_i_in", "tri_i"), ("tri_s_in", "tri_s"),
                     ("parity_in", "par_sb"), ("salts_in", "salt_sb"),
                     ("rw_in", "rw_sb")]:
        P.inc(sy.dma_start(out=N(dst), in_=N(src)), ld, 16)
    for nm in E["bias_c"]:
        P.inc(sy.dma_start(out=E["bias_c"][nm], in_=E["BC"][nm]), ld, 16)
    gsc, gsb = N("gsc"), N("gsb")
    P.inc(sy.dma_start(out=gsc[:, 0:64], in_=N("proj_in")[0:128, :]), ld, 16)
    P.inc(sy.dma_start(out=gsc[:, 64:128], in_=N("proj_in")[128:256, :]), ld, 16)
    P.inc(sy.dma_start(out=gsc[0:1, 128:128 + D], in_=N("bv_row")), ld, 16)
    LD0 = P.val(ld)

    for nm in E["W_in"]:
        P.inc(gs.dma_start(out=E["wb_d"][nm], in_=E["W_in"][nm]), gp, 16)
    P.inc(gs.dma_start(out=N("xhi_d"), in_=N("x_in")), gp, 16)
    GP0 = P.val(gp)

    P.wait(sy, gp, GP0)
    wkT, wvT = N("wkT"), N("wvT")
    for kc in range(8):
        P.inc(sy.dma_start(out=wkT[:, kc * 1024:(kc + 1) * 1024],
                           in_=E["wb_d"]["wk"][:, kc * 128:(kc + 1) * 128],
                           transpose=True), ld, 16)
        P.inc(sy.dma_start(out=wvT[:, kc * 1024:(kc + 1) * 1024],
                           in_=E["wb_d"]["wv"][:, kc * 128:(kc + 1) * 128],
                           transpose=True), ld, 16)
    LD_WKV = P.val(ld)

    for _i, _v in enumerate([LN_EPS, -0.5, NORM, LNR, RATIO * EPS_K, -1.0]):
        vc.memset(E["cvals"][:, _i:_i + 1], float(_v))
    vc.memset(N("epsrow"), RATIO * EPS_K)
    vc.drain()
    P.wait(vc, ld, LD0)
    rw_sb, rwh_sb, rwl_sb = N("rw_sb"), N("rwh_sb"), N("rwl_sb")
    vc.tensor_copy(rwh_sb, rw_sb)
    vc.drain()
    vc.tensor_tensor(rwl_sb, rw_sb, rwh_sb, ALU.subtract)
    vc.drain()
    vc.reduce_sum(N("s1col"), rw_sb, axis=AX.X)
    vv = N("v_b").rearrange("p (t h c) -> p t h c", h=HEADS, c=65)
    vc.memset(vv[:, :, :, 64:65], 1.0)
    vc.memset(N("ES"), 0.0)
    vc.memset(N("stab_run"), -30000.0)
    P.inc(vc.memset(N("maxc"), -30000.0), ve, 1)
    VE_RW = P.val(ve)

    P.wait(tn, ld, LD0)
    P.wait(tn, ve, VE_RW)
    id_f, id_b = N("id_f"), N("id_b")
    tn.transpose(pb[0][:, 0:128], gsc[:, 0:128], id_f)
    tn.transpose(pb[1][:, 0:128], gsc[:, 0:128], id_f)
    for kc in range(8):
        tn.transpose(pEb[:, kc * 16:kc * 16 + 8], rwh_sb[:, kc * 128:(kc + 1) * 128], id_b[0:8, 0:8])
    for kc in range(8):
        tn.transpose(pEb[:, 128 + kc * 16:128 + kc * 16 + 8], rwl_sb[:, kc * 128:(kc + 1) * 128], id_b[0:8, 0:8])
    tn.matmul(pb[2], N("onesrow_f"), gsc[0:1, 128:128 + 512], start=True, stop=True)
    P.inc(tn.matmul(pE[:, 2048:2560], N("onesrow_f"), gsc[0:1, 128 + 512:128 + D], start=True, stop=True), pe, 1)
    PE_INIT = P.val(pe)

    P.wait(sc, pe, PE_INIT)
    projT, bv_bc = N("projT"), N("bv_bc")
    sc.activation(out=projT[0:64, 0:128], in_=pb[0][0:64, 0:128], func=AF.Identity, scale=NORM)
    sc.activation(out=projT[0:64, 128:256], in_=pb[0][64:128, 0:128], func=AF.Identity, scale=NORM)
    sc.activation(out=projT[64:128, 0:128], in_=pb[1][0:64, 0:128], func=AF.Identity, scale=NORM)
    sc.activation(out=projT[64:128, 128:256], in_=pb[1][64:128, 0:128], func=AF.Identity, scale=NORM)
    sc.activation(out=bv_bc[:, 0:512], in_=pb[2], func=AF.Copy)
    P.inc(sc.activation(out=bv_bc[:, 512:1024], in_=pE[:, 2048:2560], func=AF.Copy), ae, 1)
    P.wait(vc, ae)
    vc.tensor_copy(N("rwhT"), pEb[:, 0:128].rearrange("p (k c) -> p k c", c=16)[:, :, 0:8])
    P.inc(vc.tensor_copy(N("rwlT"), pEb[:, 128:256].rearrange("p (k c) -> p k c", c=16)[:, :, 0:8]), ve, 1)
    VE_CONST = P.val(ve)
    P.wait(tn, ve, VE_CONST)

    _s0.end()
    emit_g(P, E)


def emit_g(P, E):
    ld, gp, pe, ve, ae, sp = (E["S_ld"], E["S_gp"], E["S_pe"], E["S_ve"],
                              E["S_ae"], E["S_sp"])
    sy, gs, tn, vc, sc = P.sync, P.gpsimd, P.tensor, P.vector, P.scalar
    pb, pE, pEb, pES = E["pb"], E["pE"], E["pEb"], E["pES"]
    _sg = _Scope(E, "Pg")
    E["G_PE_LG"] = 0
    E["G_PE_KV"] = 0
    N = lambda k: E[k]
    x_in, xhi_d, xlo_d = N("x_in"), N("xhi_d"), N("xlo_d")
    xq_g, gsc, gsb, stat_c = N("xq_g"), N("gsc"), N("gsb"), N("stat_c")
    stat_r, mu_rf = N("stat_r"), N("mu_rf")
    hiT_s, loT_s, xlo_st = N("hiT_s"), N("loT_s"), N("xlo_st")
    xnT_s, xnT_own = N("xnT_s"), N("xnT_own")
    kT_b, v_b, wkT, wvT = N("kT_b"), N("v_b"), N("wkT"), N("wvT")
    kp_t, k2_t = N("kp_t"), N("k2_t")
    diag_r, diag_c, dstab = N("diag_r"), N("diag_c"), N("dstab")
    maxc, stabc, srow = N("maxc"), N("stabc"), N("srow")
    stab_run, ES, vsum_sb = N("stab_run"), N("ES"), N("vsum_sb")
    prob_g, idx_g, iotam = N("prob_g"), N("idx_g"), N("iotam")
    projT, indh = N("projT"), N("indh")
    bias_c, bv_bc = N("bias_c"), N("bv_bc")
    onesrow_f, onesrow_b, onescol_b = N("onesrow_f"), N("onesrow_b"), N("onescol_b")
    id_f, s1col, p_sc = N("id_f"), N("s1col"), N("p_sc")

    for slot in range(NSLOT):
        for half in range(2):
            r0 = slot * BT + half * HB
            ebank_started = set()

            def estart(i):
                b = 3 + (i * 65) // 512 if False else 3 + ((i // 7))
                if b in ebank_started:
                    return False
                ebank_started.add(b)
                return True

            for t5 in range(2):
                tb = r0 + t5 * 512
                lo_half = half * HB + t5 * 512
                for t1 in range(4):
                    buf = xq_g[t1 % 2]
                    P.wait(sy, ve)
                    P.inc(sy.dma_start(out=buf, in_=x_in[tb + t1 * 128: tb + (t1 + 1) * 128, :]), ld, 16)
                    P.wait(sc, ld)
                    P.inc(sc.activation(out=gsc[:, 0:1024], in_=buf, func=AF.Square), ae, 1)
                    P.wait(vc, ld)
                    vc.reduce_sum(stat_c[:, t1:t1 + 1], buf, axis=AX.X)
                    P.wait(vc, ae)
                    vc.reduce_sum(stat_c[:, 4 + t1:5 + t1], gsc[:, 0:1024], axis=AX.X)
                    vc.tensor_copy(gsb[:, 0:1024], buf)
                    P.inc(vc.tensor_tensor(xlo_st[:, t1 * D:(t1 + 1) * D], buf, gsb[:, 0:1024], ALU.subtract), ve, 1)
                vc.drain()
                vc.tensor_scalar(stat_c[:, 0:4], stat_c[:, 0:4], 1.0 / D, None, ALU.mult)
                vc.tensor_scalar(stat_c[:, 4:8], stat_c[:, 4:8], 1.0 / D, None, ALU.mult)
                vc.drain()
                vc.tensor_tensor(stat_c[:, 8:12], stat_c[:, 0:4], stat_c[:, 0:4], ALU.mult)
                vc.drain()
                P.inc(vc.tensor_tensor(stat_c[:, 4:8], stat_c[:, 4:8], stat_c[:, 8:12], ALU.subtract), ve, 1)
                P.wait(sc, ve)
                sc.activation(out=stat_c[:, 8:12], in_=stat_c[:, 4:8], func=AF.Ln, bias=LN_EPS)
                P.inc(sc.activation(out=stat_c[:, 4:8], in_=stat_c[:, 8:12], func=AF.Exp, scale=-0.5), ae, 1)
                P.wait(tn, ae)
                lastt = None
                for t1 in range(4):
                    tn.transpose(pb[2][0:1, t1 * 128:(t1 + 1) * 128], stat_c[:, t1:t1 + 1], id_f)
                    lastt = tn.transpose(pb[0][0:1, t1 * 128:(t1 + 1) * 128], stat_c[:, 4 + t1:5 + t1], id_f)
                P.inc(lastt, pe, 1)
                P.wait(sc, pe)
                sc.activation(out=mu_rf[0:1, 0:512], in_=pb[2][0:1, 0:512], func=AF.Copy)
                P.inc(sc.activation(out=stat_r[0:1, 0:512], in_=pb[0][0:1, 0:512], func=AF.Copy), ae, 1)
                P.wait(sy, ve)
                P.inc(sy.dma_start(out=xlo_d[tb:tb + 512, :].rearrange("(a p) d -> p a d", p=128),
                                   in_=xlo_st.rearrange("p (a d) -> p a d", d=D)), ld, 16)
                XLO_ST = P.val(ld)
                P.wait(sy, ld, XLO_ST)
                P.wait(sy, pe, E.get("G_PE_LG", 0))
                for kc in range(8):
                    P.inc(sy.dma_start(out=hiT_s[:, kc * 512:(kc + 1) * 512],
                                       in_=xhi_d[tb:tb + 512, kc * 128:(kc + 1) * 128], transpose=True), ld, 16)
                    P.inc(sy.dma_start(out=loT_s[:, kc * 512:(kc + 1) * 512],
                                       in_=xlo_d[tb:tb + 512, kc * 128:(kc + 1) * 128], transpose=True), ld, 16)
                LD_T5 = P.val(ld)
                P.wait(tn, ae)
                P.inc(tn.matmul(pb[2], onesrow_b, stat_r[0:1, 0:512], start=True, stop=True), pe, 1)
                P.wait(sc, pe)
                P.inc(sc.activation(out=gsc[:, 1024:1536], in_=pb[2], func=AF.Copy), ae, 1)
                P.wait(tn, ae)
                P.inc(tn.matmul(pb[2], onesrow_f, mu_rf[0:1, 0:512], start=True, stop=True), pe, 1)
                PE_MU = P.val(pe)
                P.wait(vc, pe, PE_MU)
                P.wait(vc, pe, E.get("G_PE_KV", 0))
                P.wait(vc, ld, LD_T5)
                for kc in range(8):
                    vc.tensor_tensor(gsb[:, 0:512], hiT_s[:, kc * 512:(kc + 1) * 512], pb[2], ALU.subtract)
                    vc.tensor_tensor(xnT_s[:, kc * 512:(kc + 1) * 512], gsb[:, 0:512], gsc[:, 1024:1536], ALU.mult)
                    if slot == 0:
                        vc.tensor_copy(xnT_own[:, kc * BT + lo_half: kc * BT + lo_half + 512],
                                       xnT_s[:, kc * 512:(kc + 1) * 512])
                P.inc(vc.engine_nop(), ve, 1)
                VE_XN = P.val(ve)
                P.wait(tn, ve, VE_XN)
                P.wait(tn, ld, LD_T5)
                first = True
                last_i = None
                for kc in range(8):
                    for lh, rh in [(N("rwhT"), hiT_s), (N("rwhT"), loT_s),
                                   (N("rwlT"), hiT_s), (N("rwlT"), loT_s)]:
                        last_i = tn.matmul(pE[0:8, 0:512], lh[:, kc * 8:(kc + 1) * 8],
                                           rh[:, kc * 512:(kc + 1) * 512],
                                           start=first, stop=(kc == 7 and lh is N("rwlT") and rh is loT_s))
                        first = False
                P.inc(last_i, pe, 1)
                P.inc(tn.matmul(pb[0][0:8, :], onesrow_f[0:1, 0:8],
                               mu_rf[0:1, 0:512], start=True, stop=True), pe, 1)
                PE_LG = P.val(pe)
                E["G_PE_LG"] = PE_LG
                P.wait(sc, pe, PE_LG)
                P.inc(sc.activation(out=gsc[0:8, 0:512], in_=pE[0:8, 0:512], func=AF.Copy), ae, 1)
                P.wait(vc, ae)
                vc.tensor_scalar(gsc[0:8, 512:1024], pb[0][0:8, :], s1col, None, ALU.mult)
                vc.drain()
                vc.tensor_tensor(gsc[0:8, 0:512], gsc[0:8, 0:512], gsc[0:8, 512:1024], ALU.subtract)
                vc.drain()
                P.inc(vc.tensor_tensor(gsc[0:8, 0:512], gsc[0:8, 0:512], gsc[0:8, 1024:1536], ALU.mult), ve, 1)
                VE_LG = P.val(ve)
                cols0 = slot * 16 + half * 8 + t5 * 4
                P.wait(tn, ve, VE_LG)
                P.wait(tn, ae)
                lastt = None
                for t1 in range(4):
                    lastt = tn.transpose(pb[1][:, t1 * 8:(t1 + 1) * 8],
                                         gsc[0:8, t1 * 128:(t1 + 1) * 128], id_f[0:8, 0:8])
                P.inc(lastt, pe, 1)
                P.wait(vc, pe)
                lt = p_sc[:, 0:32]
                P.inc(vc.tensor_copy(lt, pb[1][:, 0:32]), ve, 1)
                VE_SMC = P.val(ve)
                vc.drain()
                vc.tensor_reduce(p_sc[:, 32:36], lt.rearrange("p (a b) -> p a b", b=8),
                                 axis=AX.X, op=ALU.max)
                vc.drain()
                for t1 in range(4):
                    vc.tensor_scalar(p_sc[:, 36 + t1 * 8:44 + t1 * 8], lt[:, t1 * 8:(t1 + 1) * 8],
                                     p_sc[:, 32 + t1:33 + t1], None, ALU.subtract)
                P.inc(vc.engine_nop(), ve, 1)
                P.wait(sc, ve)
                P.inc(sc.activation(out=p_sc[:, 68:100], in_=p_sc[:, 36:68], func=AF.Exp), ae, 1)
                P.wait(vc, ae)
                vc.tensor_reduce(p_sc[:, 100:104], p_sc[:, 68:100].rearrange("p (a b) -> p a b", b=8),
                                 axis=AX.X, op=ALU.add)
                vc.drain()
                vc.reciprocal(prob_g[:, cols0:cols0 + 4], p_sc[:, 100:104])
                for t1 in range(4):
                    vc.tensor_scalar(p_sc[:, 104 + t1 * 8:112 + t1 * 8], lt[:, t1 * 8:(t1 + 1) * 8],
                                     p_sc[:, 32 + t1:33 + t1], None, ALU.is_equal)
                vc.drain()
                vc.tensor_tensor(p_sc[:, 136:168], p_sc[:, 104:136], N("iotam4"), ALU.mult)
                vc.drain()
                vc.tensor_scalar(p_sc[:, 136:168], p_sc[:, 136:168], BIG, None, ALU.add)
                vc.drain()
                P.inc(vc.tensor_reduce(idx_g[:, cols0:cols0 + 4],
                                       p_sc[:, 136:168].rearrange("p (a b) -> p a b", b=8),
                                       axis=AX.X, op=ALU.min), ve, 1)
                P.wait(tn, ve, VE_SMC)
                evac_done = {}
                for dc in range(8):
                    pbuf = pb[dc % 2]
                    if dc >= 2:
                        P.wait(tn, ae, evac_done[dc % 2])
                    lastm = None
                    for kc in range(8):
                        lastm = tn.matmul(pbuf, wkT[:, kc * 1024 + dc * 128: kc * 1024 + (dc + 1) * 128],
                                          xnT_s[:, kc * 512:(kc + 1) * 512], start=(kc == 0), stop=(kc == 7))
                    P.inc(lastm, pe, 1)
                    P.wait(sc, pe)
                    sc.activation(out=kT_b[:, dc * HB + t5 * 512: dc * HB + t5 * 512 + 512],
                                  in_=pbuf, func=AF.Identity, bias=bias_c["bk"][:, dc:dc + 1])
                    P.inc(sc.activation(out=k2_t, in_=pbuf, func=AF.Square, bias=bias_c["bk"][:, dc:dc + 1]), ae, 1)
                    evac_done[dc % 2] = P.val(ae)
                    P.wait(tn, ae)
                    P.inc(tn.matmul(pb[2][0:16, :], indh[:, dc * 16:(dc + 1) * 16], k2_t,
                                    start=(dc == 0), stop=(dc == 7)), pe, 1)
                P.wait(sc, pe)
                P.inc(sc.activation(out=diag_r[:, 0:512], in_=pb[2][0:16, :], func=AF.Copy), ae, 1)
                AE_DIAG = P.val(ae)
                P.wait(tn, ae)
                vdone = {}
                for t1 in range(4):
                    for dh2 in range(2):
                        i = (t1 * 2 + dh2) % 2
                        pbuf = pb[i]
                        if t1 * 2 + dh2 >= 2:
                            P.wait(tn, ve, vdone[i])
                        lastm = None
                        for kc in range(8):
                            lastm = tn.matmul(pbuf,
                                              xnT_s[:, kc * 512 + t1 * 128: kc * 512 + (t1 + 1) * 128],
                                              wvT[:, kc * 1024 + dh2 * 512: kc * 1024 + (dh2 + 1) * 512],
                                              start=(kc == 0), stop=(kc == 7))
                        P.inc(lastm, pe, 1)
                        P.wait(vc, pe)
                        tcg = t5 * 4 + t1
                        vc.tensor_tensor(N("v_c")[:, dh2 * 512:(dh2 + 1) * 512], pbuf,
                                         bv_bc[:, dh2 * 512:(dh2 + 1) * 512], ALU.add)
                        vc.drain()
                        dst = v_b.rearrange("p (t h c) -> p t h c", h=HEADS, c=65)[
                            :, tcg, dh2 * 8:(dh2 + 1) * 8, 0:64]
                        P.inc(vc.tensor_copy(dst, N("v_c")[:, dh2 * 512:(dh2 + 1) * 512].rearrange(
                            "p (h c) -> p h c", c=64)), ve, 1)
                        vdone[i] = P.val(ve)
                    P.wait(tn, ve)
                    P.wait(tn, ae)
                    lastm = None
                    for dc in range(8):
                        lastm = tn.matmul(pb[2][:, dc:dc + 1], N("v_c")[:, dc * 128:(dc + 1) * 128],
                                          N("onescol_b"), start=(dc == 0), stop=(dc == 7))
                    P.inc(lastm, pe, 1)
                    P.wait(vc, pe)
                    vc.tensor_tensor(vsum_sb, vsum_sb, pb[2][:, 0:8], ALU.add)
                    vc.drain()
                    P.inc(vc.engine_nop(), ve, 1)
                E["G_PE_KV"] = P.val(pe)
                P.wait(tn, ae, AE_DIAG)
                P.wait(tn, ve)
                for t1 in range(4):
                    tcg = t5 * 4 + t1
                    P.wait(tn, ae)
                    P.inc(tn.transpose(pb[2][:, 0:16], diag_r[:, t1 * 128: t1 * 128 + 128],
                                       id_f[0:16, 0:16]), pe, 1)
                    P.wait(sc, pe)
                    P.inc(sc.activation(out=diag_c[:, tcg * 16:(tcg + 1) * 16], in_=pb[2][:, 0:16], func=AF.Copy), ae, 1)

            P.wait(tn, ve)
            P.wait(tn, ae)
            xp1_tiles = [pb[0], pb[1], pb[2], pE[:, 0:512], pE[:, 512:1024],
                         pE[:, 1024:1536], pE[:, 1536:2048], pE[:, 2048:2560]]
            rd_done = {0: 0, 1: 0}
            for h in range(HEADS):
                hb64 = (h % 2) * 64
                g4 = xp1_tiles[(h % 2) * 4:(h % 2) * 4 + 4]
                P.wait(tn, ve, rd_done[h % 2])
                lastm = None
                for mc in range(2):
                    for t5 in range(2):
                        lastm = tn.matmul(g4[mc * 2 + t5], projT[hb64:hb64 + 64, mc * 128:(mc + 1) * 128],
                                          kT_b[hb64:hb64 + 64, (h // 2) * HB + t5 * 512:(h // 2) * HB + t5 * 512 + 512],
                                          start=True, stop=True)
                P.inc(lastm, pe, 1)
                P.wait(vc, pe)
                for j in range(4):
                    vc.reduce_max(p_sc[:, 46 + j:47 + j], g4[j], axis=AX.X)
                vc.drain()
                vc.tensor_reduce(p_sc[:, 50:51], p_sc[:, 46:50].rearrange("p (a b) -> p a b", b=4),
                                 axis=AX.X, op=ALU.max)
                vc.drain()
                vc.tensor_tensor(maxc[:, h:h + 1], maxc[:, h:h + 1], p_sc[:, 50:51], ALU.max)
                P.inc(vc.engine_nop(), ve, 1)
                rd_done[h % 2] = P.val(ve)
            P.wait(tn, ve)
            P.inc(tn.transpose(pb[2][0:16, 0:128], maxc, id_f), pe, 1)
            P.wait(vc, pe)
            P.inc(vc.reduce_max(p_sc[0:16, 51:52], pb[2][0:16, 0:128], axis=AX.X), ve, 1)
            P.wait(tn, ve)
            P.inc(tn.transpose(pb[2][0:1, 128:144], p_sc[0:16, 51:52], id_f[0:16, 0:16]), pe, 1)
            P.wait(sc, pe)
            P.inc(sc.activation(out=srow[0:1, 0:16], in_=pb[2][0:1, 128:144], func=AF.Copy), ae, 1)
            P.wait(vc, ae)
            vc.memset(maxc, -30000.0)
            vc.drain()
            vc.tensor_tensor(srow[0:1, 16:32], stab_run, srow[0:1, 0:16], ALU.max)
            vc.drain()
            P.inc(vc.tensor_tensor(srow[0:1, 32:48], stab_run, srow[0:1, 16:32], ALU.subtract), ve, 1)
            P.wait(sc, ve)
            P.inc(sc.activation(out=srow[0:1, 48:64], in_=srow[0:1, 32:48], func=AF.Exp), ae, 1)
            P.wait(vc, ae)
            P.inc(vc.tensor_copy(stab_run, srow[0:1, 16:32]), ve, 1)
            P.wait(tn, ve)
            P.inc(tn.matmul(pb[2][:, 0:16], onesrow_f, srow[0:1, 48:64], start=True, stop=True), pe, 1)
            P.wait(sc, pe)
            P.inc(sc.activation(out=stabc, in_=pb[2][:, 0:16], func=AF.Copy), ae, 1)
            P.wait(vc, ae)
            for i in range(32):
                h = i // 2
                vc.tensor_scalar(ES[:, i * 65:(i + 1) * 65], ES[:, i * 65:(i + 1) * 65],
                                 stabc[:, h:h + 1], None, ALU.mult)
            P.inc(vc.engine_nop(), ve, 1)
            P.wait(tn, ve)
            P.inc(tn.matmul(pb[2][:, 16:32], onesrow_f, srow[0:1, 16:32], start=True, stop=True), pe, 1)
            P.wait(vc, pe)
            vc.tensor_scalar(gsc[:, 0:16], pb[2][:, 16:32], -LNR, None, ALU.add)
            vc.tensor_scalar(gsc[:, 512:640], diag_c, -1.0, None, ALU.mult)
            vc.drain()
            for tcg in range(8):
                vc.tensor_tensor(dstab[:, tcg * 16:(tcg + 1) * 16],
                                 gsc[:, 512 + tcg * 16:512 + (tcg + 1) * 16], gsc[:, 0:16], ALU.subtract)
            P.inc(vc.engine_nop(), ve, 1)
            VE_DSTAB = P.val(ve)
            ebank_started.clear()
            pend = None
            kp_done = {}
            exp_done = {}
            e_done = {}
            for h in range(HEADS):
                hb64 = (h % 2) * 64
                for tc in range(8):
                    i = tc % 2
                    pbuf = pb[i]
                    P.wait(tn, ve, VE_DSTAB)
                    if (h * 8 + tc) >= 2:
                        P.wait(tn, ae, exp_done[i])
                    P.inc(tn.matmul(pbuf[:, 0:256], kT_b[hb64:hb64 + 64, (h // 2) * HB + tc * 128:(h // 2) * HB + (tc + 1) * 128],
                                    projT[hb64:hb64 + 64, :], start=True, stop=True), pe, 1)
                    XP_PE = P.val(pe)
                    if pend is not None:
                        ph, ptc, pkp = pend
                        P.wait(tn, ae, kp_done[(ph, ptc)])
                        for mc in range(2):
                            si = ph * 2 + mc
                            st = estart(si)
                            lastm = tn.matmul(pES(si), pkp[:, mc * 128:(mc + 1) * 128],
                                              v_b.rearrange("p (t h c) -> p t h c", h=HEADS, c=65)[:, ptc, ph, :],
                                              start=st, stop=(ptc == 7))
                        P.inc(lastm, pe, 1)
                        e_done[ph * 8 + ptc] = P.val(pe)
                    P.wait(sc, pe, XP_PE)
                    P.inc(sc.activation(out=kp_t[i], in_=pbuf[:, 0:256], func=AF.Exp,
                                        bias=dstab[:, tc * 16 + h:tc * 16 + h + 1]), ae, 1)
                    exp_done[i] = P.val(ae)
                    kp_done[(h, tc)] = P.val(ae)
                    pend = (h, tc, kp_t[i])
            if pend is not None:
                ph, ptc, pkp = pend
                P.wait(tn, ae, kp_done[(ph, ptc)])
                for mc in range(2):
                    si = ph * 2 + mc
                    st = estart(si)
                    lastm = tn.matmul(pES(si), pkp[:, mc * 128:(mc + 1) * 128],
                                      v_b.rearrange("p (t h c) -> p t h c", h=HEADS, c=65)[:, ptc, ph, :],
                                      start=st, stop=True)
                P.inc(lastm, pe, 1)
                e_done[ph * 8 + ptc] = P.val(pe)
            PE_EDONE = P.val(pe)
            P.wait(vc, pe, PE_EDONE)
            for i in range(32):
                vc.tensor_tensor(ES[:, i * 65:(i + 1) * 65], ES[:, i * 65:(i + 1) * 65],
                                 pES(i), ALU.add)
            P.inc(vc.engine_nop(), ve, 1)
            P.wait(tn, ve)
            if E["debug"] and half == 1:
                nm = "xnT0e" if slot == 0 else f"xnT0s{slot}"
                P.wait(sy, ve)
                P.inc(sy.dma_start(out=E["dbg"][nm], in_=xnT_own), ld, 16)
                P.wait(sy, ld)
    _sg.end()
    _s = _Scope(E, "Pexch")
    emit_exchange_combine(P, E)
    _s.end()
    if E["debug"]:
        emit_gdebug(P, E)
    _s = _Scope(E, "Pa1")
    emit_a1(P, E)
    _s.end()
    _s = _Scope(E, "Proute")
    emit_routing(P, E)
    _s.end()
    _s = _Scope(E, "Pa2")
    emit_a2(P, E)
    _s.end()
    _s = _Scope(E, "Pmoe")
    emit_moe(P, E)
    _s.end()
    _s = _Scope(E, "Pfin")
    emit_final(P, E)
    _s.end()
    if E["debug"]:
        emit_fdebug(P, E)


def emit_exchange_combine(P, E):
    ld, gp, pe, ve, ae, sp = (E["S_ld"], E["S_gp"], E["S_pe"], E["S_ve"],
                              E["S_ae"], E["S_sp"])
    sy, gs, tn, vc, sc = P.sync, P.gpsimd, P.tensor, P.vector, P.scalar
    pb, pE, pEb = E["pb"], E["pE"], E["pEb"]
    N = lambda k: E[k]
    ES, vsum_sb, stab_run = N("ES"), N("vsum_sb"), N("stab_run")
    prob_g, idx_g, srow = N("prob_g"), N("idx_g"), N("srow")
    exch_w, stabc, maxc, gsc = N("exch_w"), N("stabc"), N("maxc"), N("gsc")
    ctx_sb, ksum_col = N("ctx_sb"), N("ksum_col")
    onesrow_f = N("onesrow_f")

    P.wait(vc, ae)
    vc.tensor_copy(exch_w[:, 0:2080], ES)
    vc.tensor_copy(exch_w[:, 2080:2088], vsum_sb)
    vc.tensor_copy(exch_w[0:1, 2088:2104], stab_run)
    vc.tensor_copy(exch_w[:, 2104:2168], prob_g[:, 0:64])
    vc.tensor_copy(exch_w[:, 2168:2232], idx_g[:, 0:64])
    vc.drain()
    P.inc(vc.engine_nop(), ve, 1)

    P.wait(gs, ve)
    if gs.live:
        gr = P.real
        with gr.register("p") as rp, gr.register("cnt") as rc, \
             gr.register("fv") as rf, gr.register("sv") as rs, \
             gr.register("tmp") as rt:
            gr.load(rp, N("par_sb"))
            gr.load(rs, N("salt_sb")[0:1, 0:1])
            with gr.If_eq(rp, 0):
                gr.dma_start(out=N("exch_a"), in_=exch_w).then_inc(gp, 16)
            with gr.Else():
                gr.dma_start(out=N("exch_b"), in_=exch_w).then_inc(gp, 16)
            gr.wait_ge(gp, P.val(gp) + 16)
            with gr.If_eq(rp, 0):
                gr.dma_start(out=N("flag_a"), in_=N("salt_sb")[0:1, 0:1]).then_inc(gp, 16)
            with gr.Else():
                gr.dma_start(out=N("flag_b"), in_=N("salt_sb")[0:1, 0:1]).then_inc(gp, 16)
            gr.wait_ge(gp, P.val(gp) + 32)

            def spin(flag):
                def cond():
                    gr.reg_add(rc, rc, 16)
                    gr.dma_start(out=N("flg_sb"), in_=flag).then_inc(sp, 16)
                    gr.wait_ge(sp, rc)
                    gr.load(rf, N("flg_sb"))
                    gr.reg_sub(rt, rf, rs)
                    return rt
                with gr.While(cond):
                    gr.nop()

            gr.reg_mov(rc, 0)
            with gr.If_eq(rp, 0):
                spin(N("flag_b"))
            with gr.Else():
                spin(N("flag_a"))
            with gr.If_eq(rp, 0):
                gr.dma_start(out=exch_w, in_=N("exch_b")).then_inc(gp, 16)
            with gr.Else():
                gr.dma_start(out=exch_w, in_=N("exch_a")).then_inc(gp, 16)
            gr.wait_ge(gp, P.val(gp) + 48)
    P.bump(gp, 48)
    if gs.live:
        P.real.nop().then_inc(gp, 1)
    P.bump(gp, 1)
    GPX = P.val(gp)

    if E["debug"]:
        P.wait(sy, gp, GPX)
        P.inc(sy.dma_start(out=E["dbg"]["ESpre"], in_=ES), ld, 16)
        P.inc(sy.dma_start(out=E["dbg"]["stabpre"], in_=stab_run), ld, 16)
        P.inc(sy.dma_start(out=E["dbg"]["xnT0m"], in_=E["xnT_own"]), ld, 16)
        P.inc(sy.dma_start(out=E["dbg"]["diagc"], in_=E["diag_c"]), ld, 16)
        P.inc(sy.dma_start(out=E["dbg"]["dstab"], in_=E["dstab"]), ld, 16)
        P.wait(sy, ld)
    P.wait(vc, gp, GPX)
    pstab = exch_w[0:1, 2088:2104]
    vc.tensor_tensor(srow[0:1, 0:16], stab_run, pstab, ALU.max)
    vc.drain()
    vc.tensor_tensor(srow[0:1, 16:32], stab_run, srow[0:1, 0:16], ALU.subtract)
    vc.drain()
    P.inc(vc.tensor_tensor(srow[0:1, 48:64], pstab, srow[0:1, 0:16], ALU.subtract), ve, 1)
    P.wait(sc, ve)
    sc.activation(out=srow[0:1, 32:48], in_=srow[0:1, 16:32], func=AF.Exp)
    P.inc(sc.activation(out=srow[0:1, 64:80], in_=srow[0:1, 48:64], func=AF.Exp), ae, 1)
    P.wait(vc, ae)
    P.inc(vc.tensor_copy(stab_run, srow[0:1, 0:16]), ve, 1)
    P.wait(tn, ae)
    tn.matmul(pb[2][:, 0:16], onesrow_f, srow[0:1, 32:48], start=True, stop=True)
    P.inc(tn.matmul(pb[2][:, 16:32], onesrow_f, srow[0:1, 64:80], start=True, stop=True), pe, 1)
    P.wait(sc, pe)
    sc.activation(out=stabc, in_=pb[2][:, 0:16], func=AF.Copy)
    P.inc(sc.activation(out=maxc, in_=pb[2][:, 16:32], func=AF.Copy), ae, 1)
    P.wait(vc, ae)
    for i in range(32):
        h = i // 2
        vc.tensor_scalar(gsc[:, 0:65], exch_w[:, i * 65:(i + 1) * 65], maxc[:, h:h + 1], None, ALU.mult)
        vc.drain()
        vc.scalar_tensor_tensor(ES[:, i * 65:(i + 1) * 65], ES[:, i * 65:(i + 1) * 65],
                                stabc[:, h:h + 1], gsc[:, 0:65], ALU.mult, ALU.add)
        vc.drain()
    vc.tensor_tensor(vsum_sb, vsum_sb, exch_w[:, 2080:2088], ALU.add)
    vc.drain()
    vc.tensor_copy(prob_g[:, 64:128], exch_w[:, 2104:2168])
    vc.tensor_copy(idx_g[:, 64:128], exch_w[:, 2168:2232])
    vc.drain()
    P.inc(vc.engine_nop(), ve, 1)
    P.wait(sy, ve)
    P.inc(sy.dma_start(out=N("vsum_d").rearrange("o (a b) -> o b a", b=128), in_=vsum_sb,
                       allow_slow_non_contiguous=True), ld, 16)
    P.wait(sy, ld)
    P.inc(sy.dma_start(out=N("vsr_sb"), in_=N("vsum_d")), ld, 16)
    VSROW = P.val(ld)
    P.wait(tn, ld, VSROW)
    for h in range(HEADS):
        P.wait(tn, ve)
        P.inc(tn.matmul(pb[2][:, 64:128], onesrow_f, N("vsr_sb")[0:1, h * 64:(h + 1) * 64],
                        start=True, stop=True), pe, 1)
        P.wait(vc, pe)
        for mc in range(2):
            i = h * 2 + mc
            vc.scalar_tensor_tensor(ctx_sb[:, i * 64:(i + 1) * 64], pb[2][:, 64:128],
                                    RATIO * EPS_K, ES[:, i * 65:i * 65 + 64], ALU.mult, ALU.add)
            vc.drain()
        P.inc(vc.engine_nop(), ve, 1)
    esv = ES.rearrange("p (i c) -> p i c", c=65)[:, :, 64:65]
    P.inc(vc.tensor_scalar(ksum_col.rearrange("p (i c) -> p i c", c=1), esv,
                           RATIO * EPS_K * T, None, ALU.add), ve, 1)
    VE_CTX = P.val(ve)
    P.wait(tn, ve, VE_CTX)


def emit_gdebug(P, E):
    ld, ve, ae, pe = E["S_ld"], E["S_ve"], E["S_ae"], E["S_pe"]
    sy = P.sync
    dbg = E["dbg"]
    P.wait(sy, ve)
    P.wait(sy, ae)
    for nm, src in [("xnT0", E["xnT_own"]), ("prob", E["prob_g"]), ("idx", E["idx_g"]),
                    ("stab", E["stab_run"]), ("ES", E["ES"]), ("vsum", E["vsum_sb"]),
                    ("ctx", E["ctx_sb"]), ("ksum", E["ksum_col"])]:
        if nm in dbg:
            P.inc(sy.dma_start(out=dbg[nm], in_=src), ld, 16)
    P.wait(sy, ld)


def emit_a1(P, E):
    ld, gp, pe, ve, ae = (E["S_ld"], E["S_gp"], E["S_pe"], E["S_ve"], E["S_ae"])
    sy, gs, tn, vc, sc = P.sync, P.gpsimd, P.tensor, P.vector, P.scalar
    pb, pE, pEb = E["pb"], E["pE"], E["pEb"]
    N = lambda k: E[k]
    xnT_own, qT, wqT, woT = N("xnT_own"), N("qT"), N("wqT"), N("woT")
    ctx_sb, ksum_col = N("ctx_sb"), N("ksum_col")
    diagq_r, diagq_c = N("diagq_r"), N("diagq_c")
    q2_t, qp_t, a1sc, a1sb = N("q2_t"), N("qp_t"), N("a1sc"), N("a1sb")
    attnT, s2c, statA = N("attnT"), N("s2c"), N("statA")
    x2T, x2_d = N("x2T"), N("x2_d")
    stab_run = N("stab_run")
    bias_c, indh, projT, id_f = N("bias_c"), N("indh"), N("projT"), N("id_f")
    onesrow_f, onesrow_b, onescol_b = N("onesrow_f"), N("onesrow_b"), N("onescol_b")
    p_sc = N("p_sc")

    P.wait(sy, ve)
    for kc in range(8):
        P.inc(sy.dma_start(out=wqT[:, kc * 1024:(kc + 1) * 1024],
                           in_=E["wb_d"]["wq"][:, kc * 128:(kc + 1) * 128], transpose=True), ld, 16)
        P.inc(sy.dma_start(out=woT[:, kc * 1024:(kc + 1) * 1024],
                           in_=E["wb_d"]["wo"][:, kc * 128:(kc + 1) * 128], transpose=True), ld, 16)
    LD_QO = P.val(ld)
    P.wait(tn, ld, LD_QO)

    evac_done = {}
    for t5 in range(4):
        for dc in range(8):
            pbuf = pb[dc % 2]
            key = (t5 * 8 + dc) % 2
            if t5 * 8 + dc >= 2:
                P.wait(tn, ae, evac_done[key])
            lastm = None
            for kc in range(8):
                lastm = tn.matmul(pbuf, wqT[:, kc * 1024 + dc * 128: kc * 1024 + (dc + 1) * 128],
                                  xnT_own[:, kc * BT + t5 * 512: kc * BT + (t5 + 1) * 512],
                                  start=(kc == 0), stop=(kc == 7))
            P.inc(lastm, pe, 1)
            P.wait(sc, pe)
            sc.activation(out=qT[:, dc * BT + t5 * 512: dc * BT + t5 * 512 + 512],
                          in_=pbuf, func=AF.Identity, bias=bias_c["bq"][:, dc:dc + 1])
            P.inc(sc.activation(out=q2_t, in_=pbuf, func=AF.Square, bias=bias_c["bq"][:, dc:dc + 1]), ae, 1)
            evac_done[key] = P.val(ae)
            P.wait(tn, ae)
            P.inc(tn.matmul(pb[2][0:16, :], indh[:, dc * 16:(dc + 1) * 16], q2_t,
                            start=(dc == 0), stop=(dc == 7)), pe, 1)
        P.wait(sc, pe)
        P.inc(sc.activation(out=diagq_r[:, 0:512], in_=pb[2][0:16, :], func=AF.Copy), ae, 1)
        for t1 in range(4):
            tcg = t5 * 4 + t1
            P.wait(tn, ae)
            P.inc(tn.transpose(pb[2][:, 0:16], diagq_r[:, t1 * 128:(t1 + 1) * 128], id_f[0:16, 0:16]), pe, 1)
            P.wait(sc, pe)
            P.inc(sc.activation(out=diagq_c[:, tcg * 16:(tcg + 1) * 16], in_=pb[2][:, 0:16], func=AF.Copy), ae, 1)
    AE_QD = P.val(ae)

    A1_EXP = 0
    A1_WO = {0: 0, 1: 0}
    X2ST = {0: 0, 1: 0}
    qp4, epsrow = N("qp4"), N("epsrow")
    ecol, kse = N("ecol_b"), N("kse_b")
    P.wait(tn, ae, AE_QD)
    first = True
    for h in range(HEADS):
        for mc in range(2):
            i2 = h * 2 + mc
            tn.matmul(pE[0:1, h * 64:(h + 1) * 64], N("onescol_b"),
                      ctx_sb[:, i2 * 64:(i2 + 1) * 64], start=first,
                      stop=(h == HEADS - 1 and mc == 1))
            first = False
    P.inc(tn.matmul(pb[2][0:1, 0:32], N("onescol_b"), ksum_col, start=True, stop=True), pe, 1)
    P.wait(sc, pe)
    sc.activation(out=ecol, in_=pE[0:1, 0:1024], func=AF.Identity, scale=RATIO * EPS_K)
    P.inc(sc.activation(out=N("kse2_b"), in_=pb[2][0:1, 0:32], func=AF.Identity,
                        scale=RATIO * EPS_K), ae, 1)
    P.wait(vc, ae)
    k2v = N("kse2_b").rearrange("o (h m) -> o h m", m=2)
    P.inc(vc.tensor_tensor(kse.rearrange("o (h m) -> o h m", m=1), k2v[:, :, 0:1],
                           k2v[:, :, 1:2], ALU.add), ve, 1)
    VE_KSE = P.val(ve)
    P.wait(tn, ve, VE_KSE)

    for t5 in range(4):
        for h in range(HEADS):
            hb64 = (h % 2) * 64
            P.wait(tn, ae, max(A1_EXP, AE_QD))
            for t1 in range(4):
                tcg = t5 * 4 + t1
                pbuf = pb[t1 // 2][:, (t1 % 2) * 256:(t1 % 2) * 256 + 256]
                lastm = tn.matmul(pbuf, qT[hb64:hb64 + 64, (h // 2) * BT + tcg * 128:(h // 2) * BT + (tcg + 1) * 128],
                                  projT[hb64:hb64 + 64, :], start=True, stop=True)
            P.inc(lastm, pe, 1)
            P.wait(vc, pe)
            for t1 in range(4):
                vc.reduce_max(p_sc[:, 40 + t1:41 + t1], pb[t1 // 2][:, (t1 % 2) * 256:(t1 % 2) * 256 + 256], axis=AX.X)
            vc.drain()
            vc.tensor_tensor(p_sc[:, 44:48].rearrange("p (a b) -> p a b", b=1),
                             p_sc[:, 40:44].rearrange("p (a b) -> p a b", b=1),
                             diagq_c.rearrange("p (t c) -> p t c", c=16)[:, t5 * 4:(t5 + 1) * 4, h:h + 1],
                             ALU.add)
            vc.drain()
            P.inc(vc.tensor_scalar(p_sc[:, 48:52], p_sc[:, 44:48], -1.0, LNR, ALU.mult, ALU.add), ve, 1)
            P.wait(sc, ve)
            lasta = None
            for t1 in range(4):
                lasta = sc.activation(out=qp4[:, t1 * 256:(t1 + 1) * 256],
                                      in_=pb[t1 // 2][:, (t1 % 2) * 256:(t1 % 2) * 256 + 256],
                                      func=AF.Exp, bias=p_sc[:, 48 + t1:49 + t1])
            P.inc(lasta, ae, 1)
            A1_EXP = P.val(ae)
            P.wait(tn, ae, A1_EXP)
            lastt = None
            for k in range(8):
                lastt = tn.transpose(pEb[:, k * 128:(k + 1) * 128], qp4[:, k * 128:(k + 1) * 128], N("id_b"))
            P.inc(lastt, pe, 1)
            P.wait(vc, pe)
            for t1 in range(4):
                for mc in range(2):
                    vc.tensor_copy(a1sb[:, mc * 512 + t1 * 128: mc * 512 + (t1 + 1) * 128],
                                   pEb[:, (t1 * 2 + mc) * 128:(t1 * 2 + mc + 1) * 128])
            P.inc(vc.engine_nop(), ve, 1)
            VE_QPT = P.val(ve)
            P.wait(tn, ve, VE_QPT)
            i2 = h * 2
            tn.matmul(pE[0:64, 1024:1536], ctx_sb[:, i2 * 64:(i2 + 1) * 64], a1sb[:, 0:512], start=True, stop=False)
            tn.matmul(pE[0:64, 1024:1536], ecol[0:1, h * 64:(h + 1) * 64], epsrow, start=False, stop=False)
            P.inc(tn.matmul(pE[0:64, 1024:1536], ctx_sb[:, (i2 + 1) * 64:(i2 + 2) * 64], a1sb[:, 512:1024], start=False, stop=True), pe, 1)
            tn.matmul(pE[0:1, 1536:2048], ksum_col[:, i2:i2 + 1], a1sb[:, 0:512], start=True, stop=False)
            tn.matmul(pE[0:1, 1536:2048], kse[0:1, h:h + 1], epsrow, start=False, stop=False)
            P.inc(tn.matmul(pE[0:1, 1536:2048], ksum_col[:, i2 + 1:i2 + 2], a1sb[:, 512:1024], start=False, stop=True), pe, 1)
            P.wait(sc, pe)
            sc.activation(out=N("a1sc2")[0:1, :], in_=pE[0:1, 1536:2048], func=AF.Ln)
            P.inc(sc.activation(out=p_sc[0:1, 8:520], in_=N("a1sc2")[0:1, :], func=AF.Exp,
                                scale=-1.0), ae, 1)
            P.wait(tn, ae)
            P.inc(tn.matmul(pE[0:64, 2048:2560], onesrow_f[0:1, 0:64], p_sc[0:1, 8:520], start=True, stop=True), pe, 1)
            P.wait(sc, pe)
            P.inc(sc.activation(out=a1sc[0:64, 0:512], in_=pE[0:64, 1024:1536], func=AF.Copy), ae, 1)
            P.wait(vc, ae)
            hb64 = (h % 2) * 64
            P.inc(vc.tensor_tensor(attnT[hb64:hb64 + 64, (h // 2) * 512:(h // 2 + 1) * 512],
                                   a1sc[0:64, 0:512], pE[0:64, 2048:2560], ALU.mult), ve, 1)
        P.wait(tn, ve)
        P.wait(tn, ae)
        VE_WORD = 0
        for dc in range(8):
            pbuf = pb[dc % 2]
            P.wait(tn, ve, A1_WO[dc % 2])
            lastm = None
            for kc in range(8):
                lastm = tn.matmul(pbuf, woT[:, kc * 1024 + dc * 128: kc * 1024 + (dc + 1) * 128],
                                  attnT[:, kc * 512:(kc + 1) * 512], start=(kc == 0), stop=(kc == 7))
            P.inc(lastm, pe, 1)
            P.wait(sc, pe)
            P.wait(sc, ve, VE_WORD)
            P.inc(sc.activation(out=a1sc[:, 0:512], in_=pbuf, func=AF.Identity, bias=bias_c["bo"][:, dc:dc + 1]), ae, 1)
            P.wait(vc, ae)
            P.inc(vc.tensor_tensor(s2c[:, dc * 512:(dc + 1) * 512], a1sc[:, 0:512],
                                   xnT_own[:, dc * BT + t5 * 512: dc * BT + (t5 + 1) * 512], ALU.add), ve, 1)
            A1_WO[dc % 2] = P.val(ve)
            VE_WORD = P.val(ve)
        VE_S2 = P.val(ve)
        P.wait(vc, ve, VE_S2)
        P.wait(sc, ve, VE_S2)
        STAT_PE = {0: 0, 1: 0}
        for dc in range(8):
            b = dc % 2
            sqbuf = N("a1sb2") if b == 0 else N("qp4")[:, 0:512]
            P.wait(vc, pe, STAT_PE[b])
            vc.tensor_copy(a1sb[:, b * 512:(b + 1) * 512], s2c[:, dc * 512:(dc + 1) * 512])
            P.inc(vc.engine_nop(), ve, 1)
            P.wait(tn, ve)
            P.inc(tn.matmul(pE[0:1, 0:512], onescol_b, a1sb[:, b * 512:(b + 1) * 512],
                            start=(dc == 0), stop=(dc == 7)), pe, 1)
            P.wait(sc, pe, STAT_PE[b])
            P.inc(sc.activation(out=sqbuf, in_=s2c[:, dc * 512:(dc + 1) * 512], func=AF.Square), ae, 1)
            P.wait(tn, ae)
            P.inc(tn.matmul(pE[0:1, 512:1024], onescol_b, sqbuf, start=(dc == 0), stop=(dc == 7)), pe, 1)
            STAT_PE[b] = P.val(pe)
        P.wait(vc, pe)
        st = statA
        vc.tensor_scalar(st[0:1, 0:512], pE[0:1, 0:512], 1.0 / D, None, ALU.mult)
        vc.drain()
        vc.tensor_scalar(st[0:1, 512:1024], pE[0:1, 512:1024], 1.0 / D, None, ALU.mult)
        vc.drain()
        vc.tensor_tensor(st[0:1, 1024:1536], st[0:1, 0:512], st[0:1, 0:512], ALU.mult)
        vc.drain()
        P.inc(vc.tensor_tensor(st[0:1, 512:1024], st[0:1, 512:1024], st[0:1, 1024:1536], ALU.subtract), ve, 1)
        P.wait(sc, ve)
        sc.activation(out=st[0:1, 1024:1536], in_=st[0:1, 512:1024], func=AF.Ln, bias=LN_EPS)
        P.inc(sc.activation(out=st[0:1, 1536:2048], in_=st[0:1, 1024:1536], func=AF.Exp, scale=-0.5), ae, 1)
        P.wait(tn, ae)
        tn.matmul(pE[:, 0:512], onesrow_f, st[0:1, 0:512], start=True, stop=True)
        P.inc(tn.matmul(pE[:, 512:1024], onesrow_f, st[0:1, 1536:2048], start=True, stop=True), pe, 1)
        P.wait(vc, pe)
        AFF = 0
        for dc in range(8):
            P.wait(vc, ae, AFF)
            vc.tensor_tensor(a1sc[:, 0:512], s2c[:, dc * 512:(dc + 1) * 512], pE[:, 0:512], ALU.subtract)
            vc.tensor_tensor(N("a1sc2"), a1sc[:, 0:512], pE[:, 512:1024], ALU.mult)
            P.inc(vc.engine_nop(), ve, 1)
            x2buf = N("a1sc3") if dc % 2 else a1sc[:, 512:1024]
            P.wait(sc, ve)
            P.wait(sc, ld, X2ST[dc % 2])
            sc.activation(out=x2buf, in_=N("a1sc2"), func=AF.Identity,
                          scale=bias_c["g2"][:, dc:dc + 1], bias=bias_c["be2"][:, dc:dc + 1])
            P.inc(sc.activation(out=x2T[:, dc * BT + t5 * 512: dc * BT + (t5 + 1) * 512],
                                in_=N("a1sc2"), func=AF.Identity,
                                scale=bias_c["g2"][:, dc:dc + 1], bias=bias_c["be2"][:, dc:dc + 1]), ae, 1)
            AFF = P.val(ae)
            P.wait(sy, ae)
            P.inc(sy.dma_start(out=x2_d[:, dc * BT + t5 * 512: dc * BT + (t5 + 1) * 512], in_=x2buf), ld, 16)
            X2ST[dc % 2] = P.val(ld)
    P.wait(sy, ve)
    P.inc(sy.dma_start(out=N("xn_own_d"), in_=xnT_own), ld, 16)
    P.wait(sy, ld)
    if E["debug"]:
        P.wait(sy, ae)
        P.inc(sy.dma_start(out=E["dbg"]["qTd"], in_=qT), ld, 16)
        P.inc(sy.dma_start(out=E["dbg"]["dqc"], in_=diagq_c), ld, 16)
        P.inc(sy.dma_start(out=E["dbg"]["attnT"], in_=attnT), ld, 16)
        P.inc(sy.dma_start(out=E["dbg"]["s2c"], in_=s2c), ld, 16)
        P.wait(sy, ld)


def emit_routing(P, E):
    ld, gp, pe, ve, ae = (E["S_ld"], E["S_gp"], E["S_pe"], E["S_ve"], E["S_ae"])
    sy, gs, tn, vc, sc = P.sync, P.gpsimd, P.tensor, P.vector, P.scalar
    pb, pE, pEb = E["pb"], E["pE"], E["pEb"]
    N = lambda k: E[k]
    prob_g, idx_g, fe_g = N("prob_g"), N("idx_g"), N("fe_g")
    rtw, rtw2, rts = N("rtw"), N("rtw2"), N("rts")
    mask_b, tri_i, tri_s = N("mask_b"), N("tri_i"), N("tri_s")
    onesrow_f, onescol_b, id_f, id_b = N("onesrow_f"), N("onescol_b"), N("id_f"), N("id_b")
    a1sb, a1sb2 = N("a1sb"), N("a1sb2")
    sc8 = N("a1sc2")
    sc8b = N("a1sc3")
    fsel = N("a1sc")

    def R(c0, c1):
        return rts[0:1, c0:c1]

    def bcast_row(row, dstcols, tag):
        P.wait(tn, ve)
        P.wait(tn, ae)
        P.inc(tn.matmul(pb[2][:, 32:40], onesrow_f, row, start=True, stop=True), pe, 1)
        P.wait(sc, pe)
        P.inc(sc.activation(out=dstcols, in_=pb[2][:, 32:40], func=AF.Copy), ae, 1)

    def counts_of_blocks(dst):
        vc.tensor_reduce(rtw[:, 1024:1032], rtw2[:, 0:1024].rearrange("p (e c) -> p e c", c=128),
                         axis=AX.X, op=ALU.add)
        vc.drain()
        P.inc(vc.tensor_copy(a1sb[:, 0:8], rtw[:, 1024:1032]), ve, 1)
        P.wait(tn, ve)
        P.wait(tn, ae)
        P.inc(tn.matmul(pb[2][0:1, 40:48], onescol_b, a1sb[:, 0:8], start=True, stop=True), pe, 1)
        P.wait(sc, pe)
        P.inc(sc.activation(out=dst, in_=pb[2][0:1, 40:48], func=AF.Copy), ae, 1)

    P.wait(vc, ae)
    P.wait(vc, ld)
    vc.tensor_scalar(rtw[:, 1024:1088], prob_g[:, 0:64], 1.0, None, ALU.add)
    vc.drain()
    vc.tensor_scalar(rtw2[:, 1024:1088], prob_g[:, 64:128], 1.0, None, ALU.add)
    vc.drain()
    P.inc(vc.engine_nop(), ve, 1)
    for e in range(NE):
        vc.tensor_single_scalar(rtw2[:, e * 128:(e + 1) * 128], idx_g, float(e), ALU.is_equal)
        vc.drain()
    P.inc(vc.engine_nop(), ve, 1)
    for e in range(NE):
        vc.tensor_tensor(rtw[:, e * 128:e * 128 + 64], rtw2[:, e * 128:e * 128 + 64],
                         rtw[:, 1024:1088], ALU.mult)
        vc.drain()
        vc.tensor_tensor(rtw[:, e * 128 + 64:(e + 1) * 128], rtw2[:, e * 128 + 64:(e + 1) * 128],
                         rtw2[:, 1024:1088], ALU.mult)
        vc.drain()
    vc.tensor_scalar(rtw[:, 0:1024], rtw[:, 0:1024], -1.0, None, ALU.add)
    vc.drain()
    P.inc(vc.engine_nop(), ve, 1)
    counts_of_blocks(R(0, 8))

    P.wait(vc, ae)
    vc.memset(R(8, 16), 0.0)
    vc.memset(R(16, 24), 1.01)
    vc.drain()
    P.inc(vc.engine_nop(), ve, 1)
    for it in range(27):
        vc.tensor_tensor(R(24, 32), R(8, 16), R(16, 24), ALU.add)
        vc.drain()
        P.inc(vc.tensor_scalar(R(24, 32), R(24, 32), 0.5, None, ALU.mult), ve, 1)
        bcast_row(R(24, 32), sc8[:, 0:8], "mid")
        P.wait(vc, ae)
        for e in range(NE):
            vc.tensor_scalar(rtw2[:, e * 128:(e + 1) * 128], rtw[:, e * 128:(e + 1) * 128],
                             sc8[:, e:e + 1], None, ALU.is_gt)
            vc.drain()
        P.inc(vc.engine_nop(), ve, 1)
        counts_of_blocks(R(32, 40))
        P.wait(vc, ae)
        vc.tensor_single_scalar(R(40, 48), R(32, 40), float(CAP), ALU.is_gt)
        vc.drain()
        vc.tensor_tensor(R(48, 56), R(24, 32), R(8, 16), ALU.subtract)
        vc.drain()
        vc.tensor_tensor(R(48, 56), R(48, 56), R(40, 48), ALU.mult)
        vc.drain()
        vc.tensor_tensor(R(8, 16), R(8, 16), R(48, 56), ALU.add)
        vc.drain()
        vc.tensor_tensor(R(48, 56), R(16, 24), R(24, 32), ALU.subtract)
        vc.drain()
        vc.tensor_tensor(R(48, 56), R(48, 56), R(40, 48), ALU.mult)
        vc.drain()
        vc.tensor_tensor(R(16, 24), R(24, 32), R(48, 56), ALU.add)
        vc.drain()
        P.inc(vc.engine_nop(), ve, 1)

    bcast_row(R(16, 24), sc8[:, 0:8], "hi")
    P.wait(vc, ae)
    for e in range(NE):
        vc.tensor_scalar(rtw2[:, e * 128:(e + 1) * 128], rtw[:, e * 128:(e + 1) * 128],
                         sc8[:, e:e + 1], None, ALU.is_le)
        vc.drain()
        vc.tensor_scalar(sc8b[:, 0:128], rtw[:, e * 128:(e + 1) * 128], 1.0, None, ALU.add)
        vc.drain()
        vc.tensor_tensor(sc8b[:, 128:256], sc8b[:, 0:128], rtw2[:, e * 128:(e + 1) * 128], ALU.mult)
        vc.drain()
        vc.tensor_scalar(rtw2[:, e * 128:(e + 1) * 128], sc8b[:, 128:256], -1.0, None, ALU.add)
        vc.drain()
    vc.tensor_reduce(rtw[:, 1024:1032], rtw2[:, 0:1024].rearrange("p (e c) -> p e c", c=128),
                     axis=AX.X, op=ALU.max)
    vc.drain()
    P.inc(vc.engine_nop(), ve, 1)
    P.wait(tn, ve)
    P.wait(tn, ae)
    P.inc(tn.transpose(pb[2][0:8, 48:176], rtw[:, 1024:1032], id_f), pe, 1)
    P.wait(vc, pe)
    P.inc(vc.reduce_max(rtw[0:8, 1032:1033], pb[2][0:8, 48:176], axis=AX.X), ve, 1)
    P.wait(tn, ve)
    P.inc(tn.transpose(pb[2][0:1, 0:8], rtw[0:8, 1032:1033], id_f[0:8, 0:8]), pe, 1)
    P.wait(sc, pe)
    P.inc(sc.activation(out=R(56, 64), in_=pb[2][0:1, 0:8], func=AF.Copy), ae, 1)

    bcast_row(R(56, 64), sc8[:, 0:8], "theta")
    P.wait(vc, ae)
    for e in range(NE):
        vc.tensor_scalar(rtw2[:, e * 128:(e + 1) * 128], rtw[:, e * 128:(e + 1) * 128],
                         sc8[:, e:e + 1], None, ALU.is_gt)
        vc.drain()
    P.inc(vc.engine_nop(), ve, 1)
    counts_of_blocks(R(64, 72))
    P.wait(vc, ae)
    vc.tensor_scalar(R(72, 80), R(64, 72), -1.0, float(CAP), ALU.mult, ALU.add)
    vc.tensor_single_scalar(R(80, 88), R(0, 8), float(CAP), ALU.is_le)
    vc.drain()
    P.inc(vc.engine_nop(), ve, 1)
    bcast_row(R(72, 80), sc8[:, 8:16], "quota")
    bcast_row(R(80, 88), sc8[:, 16:24], "cond")

    P.wait(vc, ae)
    vc.memset(fsel[:, 0:128], 0.0)
    vc.drain()
    P.inc(vc.engine_nop(), ve, 1)
    for e in range(NE):
        P.wait(vc, ae)
        vc.tensor_scalar(a1sb2[:, 0:128], rtw[:, e * 128:(e + 1) * 128],
                         sc8[:, e:e + 1], None, ALU.is_equal)
        vc.drain()
        P.inc(vc.engine_nop(), ve, 1)
        P.wait(tn, ve)
        P.wait(tn, ae)
        P.inc(tn.matmul(pb[0][:, 0:128], tri_i, a1sb2[:, 0:128], start=True, stop=True), pe, 1)
        P.inc(tn.matmul(pb[1][0:1, 0:128], onescol_b, a1sb2[:, 0:128], start=True, stop=True), pe, 1)
        P.wait(sc, pe)
        P.inc(sc.activation(out=a1sb2[0:1, 128:256], in_=pb[1][0:1, 0:128], func=AF.Copy), ae, 1)
        P.wait(tn, ae)
        P.inc(tn.transpose(pEb[:, 0:1], a1sb2[0:1, 128:256], id_b[0:1, 0:1]), pe, 1)
        P.wait(vc, pe)
        P.inc(vc.tensor_copy(a1sb2[:, 256:257], pEb[:, 0:1]), ve, 1)
        P.wait(tn, ve)
        P.inc(tn.matmul(pb[1][:, 128:129], tri_s, a1sb2[:, 256:257], start=True, stop=True), pe, 1)
        P.wait(vc, pe)
        P.inc(vc.tensor_copy(a1sb2[:, 257:258], pb[1][:, 128:129]), ve, 1)
        P.wait(tn, ve)
        P.inc(tn.transpose(pEb[0:1, 4:132], a1sb2[:, 257:258], id_b), pe, 1)
        P.wait(sc, pe)
        P.inc(sc.activation(out=a1sb2[0:1, 258:386], in_=pEb[0:1, 4:132], func=AF.Copy), ae, 1)
        P.wait(tn, ae)
        P.inc(tn.matmul(pb[1][:, 256:384], N("onesrow_b"), a1sb2[0:1, 258:386], start=True, stop=True), pe, 1)
        P.wait(vc, pe)
        vc.tensor_copy(sc8b[:, 0:128], pb[0][:, 0:128])
        vc.drain()
        vc.tensor_tensor(sc8b[:, 128:256], sc8b[:, 0:128], pb[1][:, 256:384], ALU.add)
        vc.drain()
        vc.tensor_scalar(sc8b[:, 256:384], sc8b[:, 128:256], sc8[:, 8 + e:8 + e + 1], None, ALU.is_le)
        vc.drain()
        vc.tensor_scalar(sc8b[:, 384:512], rtw[:, e * 128:(e + 1) * 128],
                         sc8[:, e:e + 1], None, ALU.is_equal)
        vc.drain()
        vc.tensor_tensor(sc8b[:, 256:384], sc8b[:, 256:384], sc8b[:, 384:512], ALU.mult)
        vc.drain()
        vc.tensor_tensor(sc8b[:, 256:384], sc8b[:, 256:384], rtw2[:, e * 128:(e + 1) * 128], ALU.add)
        vc.drain()
        vc.tensor_scalar(sc8b[:, 384:512], sc8b[:, 256:384], -1.0, 1.0, ALU.mult, ALU.add)
        vc.drain()
        vc.tensor_scalar(sc8b[:, 384:512], sc8b[:, 384:512], sc8[:, 16 + e:16 + e + 1], None, ALU.mult)
        vc.drain()
        vc.tensor_tensor(sc8b[:, 256:384], sc8b[:, 256:384], sc8b[:, 384:512], ALU.add)
        vc.drain()
        vc.tensor_single_scalar(sc8b[:, 384:512], idx_g, float(e), ALU.is_equal)
        vc.drain()
        vc.tensor_tensor(sc8b[:, 384:512], sc8b[:, 384:512], sc8b[:, 256:384], ALU.mult)
        vc.drain()
        vc.tensor_tensor(fsel[:, 0:128], fsel[:, 0:128], sc8b[:, 384:512], ALU.add)
        vc.drain()
        P.inc(vc.engine_nop(), ve, 1)
    vc.tensor_tensor(fe_g, idx_g, fsel[:, 0:128], ALU.mult)
    vc.drain()
    for e in range(NE):
        vc.tensor_single_scalar(mask_b[:, e * 16:(e + 1) * 16], fe_g[:, 0:16], float(e), ALU.is_equal)
        vc.drain()
    P.inc(vc.engine_nop(), ve, 1)


def emit_a2(P, E):
    ld, gp, pe, ve, ae = (E["S_ld"], E["S_gp"], E["S_pe"], E["S_ve"], E["S_ae"])
    sy, gs, tn, vc, sc = P.sync, P.gpsimd, P.tensor, P.vector, P.scalar
    pb, pE = E["pb"], E["pE"]
    N = lambda k: E[k]
    w1aT, w2aTq, Ha = N("w1aT"), N("w2aTq"), N("Ha")
    x2T, x2f, x3f = N("x2T"), N("x2f"), N("x3f")
    a2sc, a2sb, statB = N("a2sc"), N("a2sb"), N("statB")
    bias_c = N("bias_c")
    onesrow_f, onescol_b = N("onesrow_f"), N("onescol_b")
    x2_d, x3_d, ya_d = N("x2_d"), N("x3_d"), N("ya_d")

    P.wait(sy, ve)
    P.wait(sy, ae)
    for kc in range(8):
        P.inc(sy.dma_start(out=w1aT[:, kc * DFF:(kc + 1) * DFF],
                           in_=E["wb_d"]["w1a"][:, kc * 128:(kc + 1) * 128], transpose=True), ld, 16)
    LD_W1A = P.val(ld)
    P.wait(tn, ld, LD_W1A)

    WQ_PE = {}
    for t5 in range(4):
        HEV = {0: 0, 1: 0}
        for ffc in range(32):
            pbuf = pb[ffc % 2]
            P.wait(tn, ae, HEV[ffc % 2])
            lastm = None
            for kc in range(8):
                lastm = tn.matmul(pbuf, w1aT[:, kc * DFF + ffc * 128: kc * DFF + (ffc + 1) * 128],
                                  x2T[:, kc * BT + t5 * 512: kc * BT + (t5 + 1) * 512],
                                  start=(kc == 0), stop=(kc == 7))
            P.inc(lastm, pe, 1)
            P.wait(sc, pe)
            P.inc(sc.activation(out=Ha[:, ffc * 512:(ffc + 1) * 512], in_=pbuf, func=AF.Relu,
                                bias=bias_c["b1a"][:, ffc:ffc + 1]), ae, 1)
            HEV[ffc % 2] = P.val(ae)
        AE_H = P.val(ae)
        ytile = {0: pb[0], 1: pb[1], 2: pb[2], 3: pE[:, 0:512], 4: pE[:, 512:1024],
                 5: pE[:, 1024:1536], 6: pE[:, 1536:2048], 7: pE[:, 2048:2560]}
        P.wait(tn, ae, AE_H)
        P.wait(tn, ve)
        X2L = {}
        VE_X3 = [0]
        for q in range(4):
            P.wait(sy, pe, WQ_PE.get(q % 2, 0))
            for f2 in range(8):
                ffc = q * 8 + f2
                P.inc(sy.dma_start(out=w2aTq[q % 2][:, f2 * 1024:(f2 + 1) * 1024],
                                   in_=E["wb_d"]["w2a"][:, ffc * 128:(ffc + 1) * 128],
                                   transpose=True), ld, 16)
            WQL = P.val(ld)
            P.wait(tn, ld, WQL)
            lastm = None
            for dc in range(8):
                for f2 in range(8):
                    ffc = q * 8 + f2
                    lastm = tn.matmul(ytile[dc], w2aTq[q % 2][:, f2 * 1024 + dc * 128: f2 * 1024 + (dc + 1) * 128],
                                      Ha[:, ffc * 512:(ffc + 1) * 512],
                                      start=(q == 0 and f2 == 0), stop=(q == 3 and f2 == 7))
            P.inc(lastm, pe, 1)
            WQ_PE[q % 2] = P.val(pe)
        for dc in range(8):
            P.wait(sy, ve)
            P.inc(sy.dma_start(out=x2f[dc % 2], in_=x2_d[:, dc * BT + t5 * 512: dc * BT + (t5 + 1) * 512]), ld, 16)
            X2L[dc % 2] = P.val(ld)
            P.wait(sc, pe)
            P.wait(sc, ve, VE_X3[0])
            P.inc(sc.activation(out=a2sc[:, 0:512], in_=ytile[dc], func=AF.Identity,
                                bias=bias_c["b2a"][:, dc:dc + 1]), ae, 1)
            P.wait(vc, ae)
            P.wait(vc, ld, X2L[dc % 2])
            vc.tensor_tensor(x3f[dc % 2], a2sc[:, 0:512], x2f[dc % 2], ALU.add)
            vc.drain()
            vc.tensor_copy(Ha[:, dc * 512:(dc + 1) * 512], x3f[dc % 2])
            P.inc(vc.engine_nop(), ve, 1)
            VE_X3[0] = P.val(ve)
            P.wait(sy, ve)
            P.inc(sy.dma_start(out=x3_d[:, dc * BT + t5 * 512: dc * BT + (t5 + 1) * 512], in_=x3f[dc % 2]), ld, 16)
            P.wait(sc, ve)
            P.inc(sc.activation(out=Ha[:, 4096 + dc * 512:4096 + (dc + 1) * 512],
                                in_=x3f[dc % 2], func=AF.Square), ae, 1)
        P.wait(tn, ve)
        P.wait(tn, ae)
        for dc in range(8):
            tn.matmul(pb[0][0:1, :], onescol_b, Ha[:, dc * 512:(dc + 1) * 512],
                      start=(dc == 0), stop=(dc == 7))
            lastm = tn.matmul(pb[1][0:1, :], onescol_b, Ha[:, 4096 + dc * 512:4096 + (dc + 1) * 512],
                              start=(dc == 0), stop=(dc == 7))
        P.inc(lastm, pe, 1)
        P.wait(vc, pe)
        st = statB
        vc.tensor_scalar(st[0:1, 0:512], pb[0][0:1, :], 1.0 / D, None, ALU.mult)
        vc.tensor_scalar(st[0:1, 512:1024], pb[1][0:1, :], 1.0 / D, None, ALU.mult)
        vc.drain()
        vc.tensor_tensor(st[0:1, 1024:1536], st[0:1, 0:512], st[0:1, 0:512], ALU.mult)
        vc.drain()
        P.inc(vc.tensor_tensor(st[0:1, 512:1024], st[0:1, 512:1024], st[0:1, 1024:1536], ALU.subtract), ve, 1)
        P.wait(sc, ve)
        sc.activation(out=st[0:1, 1024:1536], in_=st[0:1, 512:1024], func=AF.Ln, bias=LN_EPS)
        P.inc(sc.activation(out=st[0:1, 1536:2048], in_=st[0:1, 1024:1536], func=AF.Exp, scale=-0.5), ae, 1)
        P.wait(tn, ae)
        tn.matmul(pE[:, 0:512], onesrow_f, st[0:1, 0:512], start=True, stop=True)
        P.inc(tn.matmul(pE[:, 512:1024], onesrow_f, st[0:1, 1536:2048], start=True, stop=True), pe, 1)
        AFF2 = 0
        for dc in range(8):
            P.wait(sy, ve)
            P.inc(sy.dma_start(out=x3f[dc % 2], in_=x3_d[:, dc * BT + t5 * 512: dc * BT + (t5 + 1) * 512]), ld, 16)
            P.wait(vc, ld)
            P.wait(vc, pe)
            P.wait(vc, ae, AFF2)
            vc.tensor_tensor(a2sc[:, 0:512], x3f[dc % 2], pE[:, 0:512], ALU.subtract)
            vc.tensor_tensor(a2sc[:, 512:1024], a2sc[:, 0:512], pE[:, 512:1024], ALU.mult)
            P.inc(vc.engine_nop(), ve, 1)
            P.wait(sc, ve)
            P.inc(sc.activation(out=a2sc[:, 0:512], in_=a2sc[:, 512:1024], func=AF.Identity,
                                scale=bias_c["g3"][:, dc:dc + 1], bias=bias_c["be3"][:, dc:dc + 1]), ae, 1)
            AFF2 = P.val(ae)
            P.wait(sy, ae)
            P.inc(sy.dma_start(out=ya_d[:, dc * BT + t5 * 512: dc * BT + (t5 + 1) * 512], in_=a2sc[:, 0:512]), ld, 16)
    P.wait(sy, ld)


def emit_moe(P, E):
    ld, gp, pe, ve, ae = (E["S_ld"], E["S_gp"], E["S_pe"], E["S_ve"], E["S_ae"])
    sy, gs, tn, vc, sc = P.sync, P.gpsimd, P.tensor, P.vector, P.scalar
    pb, pE = E["pb"], E["pE"]
    N = lambda k: E[k]
    w1sb, w2sb, Hm = N("w1sb"), N("w2sb"), N("Hm")
    xnm, yst, zt = N("xnm"), N("yst"), N("zt")
    b1m_sb, b2m_sb = N("b1m_sb"), N("b2m_sb")
    mask_b = N("mask_b")
    xn_own_d, ym_d = N("xn_own_d"), N("ym_d")
    onesrow_b = N("onesrow_b")

    P.wait(gs, pe)
    P.wait(gs, ve)
    P.wait(gs, ae)
    P.wait(gs, ld)
    P.wait(vc, ae)
    P.inc(vc.memset(zt, 0.0), ve, 1)
    P.wait(sy, ve)
    for r in range(16):
        P.inc(sy.dma_start(out=ym_d[r * 128:(r + 1) * 128, 0:512], in_=zt), ld, 16)
        P.inc(sy.dma_start(out=ym_d[r * 128:(r + 1) * 128, 512:1024], in_=zt), ld, 16)
    ZRO = P.val(ld)
    P.inc(gs.dma_start(out=w1sb.rearrange("p (a f) -> p a f", f=DFF), in_=E["w1m_in"][0:D, :].rearrange("(a p) f -> p a f", p=128)), gp, 16)
    P.inc(gs.dma_start(out=b1m_sb[0], in_=E["b1m_in"][0:128, :]), gp, 16)
    W1_RDY = {0: P.val(gp)}
    P.inc(gs.dma_start(out=w2sb.rearrange("p (a f) -> p a f", f=D), in_=E["w2m_in"][0:DFF, :].rearrange("(a p) f -> p a f", p=128)), gp, 16)
    P.inc(gs.dma_start(out=b2m_sb[0], in_=E["b2m_in"][0:1, :]), gp, 16)
    W2_RDY = {0: P.val(gp)}
    P.wait(gs, ld, ZRO)
    XN_LD = {}
    MOE_PE0 = P.val(pe)
    XN_FREE = {0: MOE_PE0, 1: MOE_PE0}
    P.wait(sc, ve)
    P.wait(sc, ld)
    Y_ST = {0: 0, 1: 0}
    Y_DMA = {0: 0, 1: 0}
    for e in range(NE):
        eb = e % 2
        for t5 in range(4):
            i = t5 % 2
            P.wait(sy, ve)
            P.wait(sy, pe, XN_FREE[i])
            P.inc(sy.dma_start(out=xnm[i],
                               in_=xn_own_d.rearrange("p (k t) -> p k t", t=BT)[:, :, t5 * 512:(t5 + 1) * 512]),
                  ld, 16)
            XN_LD[i] = P.val(ld)
            HEV = {0: 0, 1: 0}
            P.wait(tn, ld, XN_LD[i])
            P.wait(tn, gp, W1_RDY[e])
            P.wait(tn, ve, max(Y_ST[0], Y_ST[1]))
            for ffc in range(32):
                pbuf = pb[ffc % 2]
                P.wait(tn, ae, HEV[ffc % 2])
                lastm = None
                for kc in range(8):
                    lastm = tn.matmul(pbuf, w1sb[:, kc * DFF + ffc * 128: kc * DFF + (ffc + 1) * 128],
                                      xnm[i][:, kc * 512:(kc + 1) * 512],
                                      start=(kc == 0), stop=(kc == 7))
                P.inc(lastm, pe, 1)
                P.wait(sc, pe)
                P.inc(sc.activation(out=Hm[:, ffc * 512:(ffc + 1) * 512], in_=pbuf, func=AF.Relu,
                                    bias=b1m_sb[eb][:, ffc:ffc + 1]), ae, 1)
                HEV[ffc % 2] = P.val(ae)
            XN_FREE[i] = P.val(pe)
            if e + 1 < NE and t5 == 3:
                P.wait(gs, ae)
                P.inc(gs.dma_start(out=w1sb.rearrange("p (a f) -> p a f", f=DFF),
                                   in_=E["w1m_in"][(e + 1) * D:(e + 2) * D, :].rearrange(
                    "(a p) f -> p a f", p=128)), gp, 16)
                P.inc(gs.dma_start(out=b1m_sb[1 - eb], in_=E["b1m_in"][(e + 1) * 128:(e + 2) * 128, :]), gp, 16)
                W1_RDY[e + 1] = P.val(gp)
            P.wait(tn, ae)
            P.wait(tn, gp, W2_RDY[e])
            for t1 in range(4):
                for dh in range(2):
                    j = (t1 * 2 + dh) % 2
                    pbuf = pb[j]
                    P.wait(tn, ve, Y_ST[j])
                    lastm = None
                    for ffc in range(32):
                        lastm = tn.matmul(pbuf, Hm[:, ffc * 512 + t1 * 128: ffc * 512 + (t1 + 1) * 128],
                                          w2sb[:, ffc * 1024 + dh * 512: ffc * 1024 + (dh + 1) * 512],
                                          start=(ffc == 0), stop=False)
                    lastm = tn.matmul(pbuf, onesrow_b, b2m_sb[eb][0:1, dh * 512:(dh + 1) * 512],
                                      start=False, stop=True)
                    P.inc(lastm, pe, 1)
                    tcg = t5 * 4 + t1
                    P.wait(vc, pe)
                    P.wait(vc, gp, Y_DMA[j])
                    P.inc(vc.tensor_scalar(yst[j], pbuf, mask_b[:, e * 16 + tcg:e * 16 + tcg + 1],
                                           None, ALU.mult), ve, 1)
                    Y_ST[j] = P.val(ve)
                    P.wait(gs, ve, Y_ST[j])
                    P.inc(gs.dma_start(out=ym_d[tcg * 128:(tcg + 1) * 128, dh * 512:(dh + 1) * 512],
                                       in_=yst[j], accum_op=ALU.add), gp, 16)
                    Y_DMA[j] = P.val(gp)
                    P.wait(gs, gp, Y_DMA[j])
        if e + 1 < NE:
            P.wait(gs, pe)
            P.inc(gs.dma_start(out=w2sb.rearrange("p (a f) -> p a f", f=D),
                               in_=E["w2m_in"][(e + 1) * DFF:(e + 2) * DFF, :].rearrange(
                "(a p) f -> p a f", p=128)), gp, 16)
            P.inc(gs.dma_start(out=b2m_sb[1 - eb], in_=E["b2m_in"][e + 1:e + 2, :]), gp, 16)
            W2_RDY[e + 1] = P.val(gp)
    P.wait(gs, gp)


def emit_fdebug(P, E):
    ld, ve = E["S_ld"], E["S_ve"]
    sy = P.sync
    dbg = E["dbg"]
    P.wait(sy, ve)
    P.inc(sy.dma_start(out=dbg["x2"], in_=E["x2_d"]), ld, 16)
    P.inc(sy.dma_start(out=dbg["x3"], in_=E["x3_d"]), ld, 16)
    P.inc(sy.dma_start(out=dbg["ya"], in_=E["ya_d"]), ld, 16)
    P.inc(sy.dma_start(out=dbg["ym"], in_=E["ym_d"]), ld, 16)
    P.inc(sy.dma_start(out=dbg["fe"], in_=E["fe_g"]), ld, 16)
    P.inc(sy.dma_start(out=dbg["xnd"], in_=E["xn_own_d"]), ld, 16)
    P.inc(sy.dma_start(out=dbg["maskb"], in_=E["mask_b"]), ld, 16)
    P.wait(sy, ld)


def emit_final(P, E):
    ld, gp, pe, ve, ae = (E["S_ld"], E["S_gp"], E["S_pe"], E["S_ve"], E["S_ae"])
    sy, gs, tn, vc, sc = P.sync, P.gpsimd, P.tensor, P.vector, P.scalar
    pb = E["pb"]
    N = lambda k: E[k]
    fx, fya, fym, fout = N("fx"), N("fya"), N("fym"), N("fout")
    x_in, ya_d, ym_d, out = N("x_in"), N("ya_d"), N("ym_d"), N("out")
    id_f = N("id_f")

    P.wait(sy, gp)
    P.wait(sy, ve)
    FOUT = {0: 0, 1: 0}
    for t1 in range(16):
        i = t1 % 2
        P.wait(sy, ld, FOUT[i])
        P.inc(sy.dma_start(out=fx[i], in_=x_in[t1 * 128:(t1 + 1) * 128, :]), ld, 16)
        P.inc(sy.dma_start(out=fym[i], in_=ym_d[t1 * 128:(t1 + 1) * 128, :]), ld, 16)
        P.inc(sy.dma_start(out=fya[i],
                           in_=ya_d.rearrange("p (k t) -> p k t", t=BT)[:, :, t1 * 128:(t1 + 1) * 128]), ld, 16)
        LD_F = P.val(ld)
        P.wait(vc, ld, LD_F)
        P.inc(vc.tensor_tensor(fout[i], fx[i], fym[i], ALU.add), ve, 1)
        P.wait(tn, ld, LD_F)
        P.wait(tn, ve)
        for dc in range(8):
            P.wait(tn, ve)
            P.inc(tn.transpose(pb[0][:, 0:128] if dc % 2 == 0 else pb[1][:, 0:128],
                               fya[i][:, dc * 128:(dc + 1) * 128], id_f), pe, 1)
            P.wait(vc, pe)
            P.inc(vc.tensor_tensor(fout[i][:, dc * 128:(dc + 1) * 128], fout[i][:, dc * 128:(dc + 1) * 128],
                                   pb[0][:, 0:128] if dc % 2 == 0 else pb[1][:, 0:128], ALU.add), ve, 1)
        P.wait(sy, ve)
        P.inc(sy.dma_start(out=out[t1 * 128:(t1 + 1) * 128, :], in_=fout[i]), ld, 16)
        FOUT[i] = P.val(ld)
    P.wait(sy, ld)
    P.wait(gs, ld)


_BUILD_CACHE = {}


def _get_program(debug=False):
    key = bool(debug)
    if key not in _BUILD_CACHE:
        _BUILD_CACHE[key] = build_program(debug=debug)
    return _BUILD_CACHE[key]


def _col_layout(b, nch):
    return np.ascontiguousarray(np.asarray(b, np.float32).reshape(nch, 128).T)


def make_inputs(inputs, debug=False):
    x = np.ascontiguousarray(np.asarray(inputs["L0_attn_video"], np.float32).reshape(T, D))
    ident = np.eye(128, dtype=np.float32)
    identb = np.eye(128, dtype=BFNP)
    iotam = np.tile(np.arange(NE, dtype=np.float32) - BIG, (128, 1))
    onescb = np.ones((128, 1), BFNP)
    onescf = np.ones((128, 1), np.float32)
    onesrb = np.ones((1, 128), BFNP)
    onesrf = np.ones((1, 128), np.float32)
    halves = np.zeros((2, 128), np.float32)
    halves[0, 0:64] = 1.0
    halves[1, 64:128] = 1.0
    indh = np.zeros((128, 128), np.float32)
    for dc in range(8):
        for h16 in range(16):
            if h16 == 2 * dc:
                indh[0:64, dc * 16 + h16] = 0.5 * NORM * NORM
            elif h16 == 2 * dc + 1:
                indh[64:128, dc * 16 + h16] = 0.5 * NORM * NORM
    indh = indh.astype(BFNP)
    k = np.arange(128)
    tri_i = (k[:, None] <= k[None, :]).astype(BFNP)
    tri_s = (k[:, None] < k[None, :]).astype(BFNP)
    salt = int(np.random.randint(1 << 20, 1 << 30))
    salts = (salt + np.arange(64, dtype=np.int64)).astype(np.int32).reshape(1, 64)

    base = {
        "wq": np.asarray(inputs["L0_attn_Wq"], np.float32),
        "wk": np.asarray(inputs["L0_attn_Wk"], np.float32),
        "wv": np.asarray(inputs["L0_attn_Wv"], np.float32),
        "wo": np.asarray(inputs["L0_attn_Wo"], np.float32),
        "w1a": np.asarray(inputs["L0_attn_W1"], np.float32),
        "w2a": np.asarray(inputs["L0_attn_W2"], np.float32),
        "bq": _col_layout(inputs["L0_attn_bq"], 8),
        "bk": _col_layout(inputs["L0_attn_bk"], 8),
        "bo": _col_layout(inputs["L0_attn_bo"], 8),
        "b2a": _col_layout(inputs["L0_attn_b2"], 8),
        "g2": _col_layout(inputs["L0_attn_g2"], 8),
        "be2": _col_layout(inputs["L0_attn_be2"], 8),
        "g3": _col_layout(inputs["L0_attn_g3"], 8),
        "be3": _col_layout(inputs["L0_attn_be3"], 8),
        "b1a": _col_layout(inputs["L0_attn_b1"], 32),
        "bv_row": np.asarray(inputs["L0_attn_bv"], np.float32).reshape(1, D),
        "proj_in": np.asarray(inputs["L0_attn_proj"], np.float32),
        "rw_in": np.asarray(inputs["L0_ffn_router_w"], np.float32),
        "w1m": np.ascontiguousarray(np.asarray(inputs["L0_ffn_W1"], np.float32).reshape(NE * D, DFF)),
        "b1m": np.concatenate([_col_layout(np.asarray(inputs["L0_ffn_b1"], np.float32)[e], 32)
                               for e in range(NE)], axis=0),
        "w2m": np.ascontiguousarray(np.asarray(inputs["L0_ffn_W2"], np.float32).reshape(NE * DFF, D)),
        "b2m": np.asarray(inputs["L0_ffn_b2"], np.float32),
        "ident_in": ident, "identb_in": identb, "iotam_in": iotam,
        "iotam4_in": np.tile(iotam, (1, 4)),
        "onescb_in": onescb, "onescf_in": onescf, "onesrb_in": onesrb,
        "onesrf_in": onesrf, "halves_in": halves, "indh_in": indh,
        "tri_i_in": tri_i, "tri_s_in": tri_s, "salts_in": salts,
    }
    in_maps = []
    for c in range(NCORES):
        slots = [(c + 2 * j) % 8 for j in range(NSLOT)]
        xi = np.concatenate([x[s * BT:(s + 1) * BT] for s in slots], axis=0)
        m = dict(base)
        m["x_in"] = np.ascontiguousarray(xi)
        m["parity_in"] = np.full((1, 1), c % 2, np.int32)
        in_maps.append(m)
    return in_maps


def kernel(**inputs):
    nc, dbg = _get_program(debug=False)
    in_maps = make_inputs(inputs)
    res = run_bass_kernel_spmd(nc, in_maps, core_ids=list(range(NCORES)))
    outp = np.empty((T, D), np.float32)
    for c in range(NCORES):
        outp[c * BT:(c + 1) * BT] = np.asarray(res.results[c]["out"], np.float32)
    return outp.reshape(1, T, D)


if __name__ == "__main__":
    nc, dbg = build_program(debug=False)
    n = sum(1 for _ in nc.all_instructions())
    print("built ok, instructions:", n)
